# version 05
# speedup vs baseline: 1.7546x; 1.3478x over previous
import numpy as np
import ml_dtypes
from contextlib import ExitStack

import concourse.bass as bass
import concourse.mybir as mybir
from concourse.bass_utils import run_bass_kernel_spmd

F32 = mybir.dt.float32
BF16 = mybir.dt.bfloat16
U8 = mybir.dt.uint8
AF = mybir.ActivationFunctionType
ALU = mybir.AluOpType
AX = mybir.AxisListType

NCORES = 8
D = 1024
NG = 8
CH = 512
BF = ml_dtypes.bfloat16

_SER_OPS = {
    "tensor_copy", "tensor_tensor", "tensor_add", "tensor_sub", "tensor_mul",
    "tensor_max", "tensor_scalar", "tensor_single_scalar", "tensor_scalar_add",
    "tensor_scalar_sub", "tensor_scalar_mul", "tensor_scalar_max", "tensor_scalar_min",
    "tensor_relu", "tensor_reduce", "reduce_max", "reduce_sum", "scalar_tensor_tensor",
    "tensor_tensor_scan", "memset", "memzero", "reciprocal", "select", "copy_predicated",
    "activation", "sign", "square", "sqrt", "mul", "copy", "add", "iota",
}
_PE_OPS = {"matmul", "transpose", "ldweights"}


class Ct:
    def __init__(self, ctx, nc, name):
        self.sem = ctx.enter_context(nc.semaphore(name))
        self.n = 0

    def inc(self, instr, k=1):
        instr.then_inc(self.sem, k)
        self.n += k
        return instr

    def wait(self, eng_or_ew, val=None):
        eng = eng_or_ew.eng if isinstance(eng_or_ew, EW) else eng_or_ew
        eng.wait_ge(self.sem, self.n if val is None else val)


class EW:
    def __init__(self, ct, serialize=True):
        self.ct = ct
        self.serialize = serialize
        self.eng = None

    def bind(self, eng):
        self.eng = eng
        return self

    def __getattr__(self, name):
        eng = object.__getattribute__(self, "eng")
        fn = getattr(eng, name)
        if name in _SER_OPS or name in _PE_OPS:
            ct = self.ct
            ser = self.serialize and name in _SER_OPS

            def wrap(*a, **k):
                if ser:
                    eng.wait_ge(ct.sem, ct.n)
                ins = fn(*a, **k)
                ct.inc(ins)
                return ins
            return wrap
        return fn


class Phase:
    def __init__(self, nc, ctx, prefix="", blk_ctx=None):
        self.nc = nc
        self.blk = (blk_ctx if blk_ctx is not None else ctx).enter_context(nc.Block())
        self.prefix = prefix
        self.sL = Ct(ctx, nc, prefix + "sL")
        self.sG = Ct(ctx, nc, prefix + "sG")
        self.sS = Ct(ctx, nc, prefix + "sS")
        self.cT = Ct(ctx, nc, prefix + "cT")
        self.cV = Ct(ctx, nc, prefix + "cV")
        self.cA = Ct(ctx, nc, prefix + "cA")
        self.cG = Ct(ctx, nc, prefix + "cG")
        self._ctx = ctx
        self._nc = nc
        self.T_ = EW(self.cT, serialize=False)
        self.V_ = EW(self.cV)
        self.A_ = EW(self.cA)
        self.G_ = EW(self.cG)
        self.Vf_ = EW(self.cV, serialize=False)
        self.Af_ = EW(self.cA, serialize=False)

    def ct(self, name):
        return Ct(self._ctx, self._nc, self.prefix + name)

    def tensor(self, f):
        self.blk.tensor(lambda eng: f(self.T_.bind(eng)))

    def vector(self, f):
        self.blk.vector(lambda eng: f(self.V_.bind(eng)))

    def vector_f(self, f):
        self.blk.vector(lambda eng: f(self.Vf_.bind(eng)))

    def scalar_f(self, f):
        self.blk.scalar(lambda eng: f(self.Af_.bind(eng)))

    def scalar(self, f):
        self.blk.scalar(lambda eng: f(self.A_.bind(eng)))

    def gpsimd(self, f):
        self.blk.gpsimd(lambda eng: f(self.G_.bind(eng)))

    def sync(self, f):
        self.blk.sync(f)


def _img(w):
    K_, M_ = w.shape
    return np.ascontiguousarray(
        w.reshape(K_ // 128, 128, M_).transpose(1, 0, 2).reshape(128, -1), dtype=np.float32)


def _img_mmajor(w):
    K_, M_ = w.shape
    t = w.reshape(K_ // 128, 128, M_ // 128, 128)
    return np.ascontiguousarray(t.transpose(1, 2, 0, 3).reshape(128, -1), dtype=np.float32)


def _apify(t):
    return t[:, :]


def _vec_img(v):
    return np.ascontiguousarray(np.asarray(v, np.float32).reshape(8, 128).T)



def emit_A(nc, semctx, d, T, debug=False):
    NT = T // 128
    NCH = T // CH
    QT = CH // 128
    x0_d = d["x0"]
    wk_d = d["w_k"]
    wq_d = d["w_q"]
    wv_d = d["w_v"]
    wp_d = d["wpdup"]
    eyeF_d = d["eyeF"]
    eyeB_d = d["eyeB"]
    exp16_d = d["expand16"]
    tril_d = d["trils"]
    x1T_d = d["x1T"]
    zT_d = d["zT_scratch"]
    xnT_dbg = d.get("xnT_dbg")
    mats_dbg = d.get("mats_dbg")
    kqv_dbg = d.get("kqv_dbg")
    zT_dbg = d.get("zT_dbg")
    with ExitStack() as ctx:
        sb = lambda name, shape, dt: ctx.enter_context(nc.sbuf_tensor(name + "_sb", shape, dt))

        xnT = sb("xnT", [128, NG * T], BF16)
        zstg = [sb(f"zstg{j}", [64, CH], BF16) for j in range(2)]
        zrow = sb("zrow", [128, D], BF16)
        eyeF = sb("eyeF", [128, 128], F32)
        eyeB = sb("eyeB", [128, 128], BF16)
        wp = sb("wp", [128, 1024], F32)
        exp16 = sb("exp16", [16, 1024], F32)
        trils = sb("trils", [128, 4 * CH], BF16)
        onescol_b = sb("onescol_b", [128, 1], BF16)
        ones1x64 = sb("ones1x64", [1, 64], F32)
        kT_i = sb("kT_i", [64, T], BF16)
        qT_i = sb("qT_i", [64, T], BF16)
        vT_i = sb("vT_i", [64, T], BF16)
        vnat = sb("vnat", [128, NT * 65], BF16)
        expS = [sb(f"expS{j}", [128, CH], BF16) for j in range(3)]
        rec_row = sb("rec_row", [1, CH], F32)
        ztmp = sb("ztmp", [64, CH], F32)
        mats = {nm: sb(f"mat_{nm}", [128, NG * 1024], BF16) for nm in ("K", "Q", "V")}
        pcols = {nm: sb(f"pcol_{nm}", [128, NG * 16], F32) for nm in ("K", "Q", "V")}
        wraw = {nm: sb(f"wraw_{nm}", [16, 16], F32) for nm in ("K", "Q", "V")}
        probsT = {nm: sb(f"probsT_{nm}", [16, 16], F32) for nm in ("K", "Q", "V")}
        scr16 = sb("scr16", [16, 16], F32)
        col16 = sb("col16", [16, 1], F32)
        x0nat = [sb(f"x0nat{j}", [128, D], F32) for j in range(2)]
        sqnat = sb("sqnat", [128, D], F32)
        statc = sb("statc", [128, 6], F32)
        xn_nat = sb("xn_nat", [128, D], BF16)
        x1row = [sb(f"x1row{j}", [128, D], F32) for j in range(2)]

        P = Phase(nc, semctx, "A_", blk_ctx=ctx)
        sL, sS, cT, cV, cA = P.sL, P.sS, P.cT, P.cV, P.cA
        sLx = [P.ct("sLx0"), P.ct("sLx1")]
        sSx = [P.ct("sSx0"), P.ct("sSx1")]
        sSz = [P.ct("sSz0"), P.ct("sSz1")]

        pctx = ExitStack()
        psf = lambda name, shape, dt=F32: pctx.enter_context(nc.psum_tensor(name + "_ps", shape, dt))
        psT0 = psf("psT0", [128, 128], BF16)
        psT1 = psf("psT1", [128, 128], BF16)
        psPC = psf("psPC", [128, 16], F32)
        ps16 = psf("ps16", [16, 16], F32)

        def f(sync):
            for dst, src in ((eyeF, eyeF_d), (eyeB, eyeB_d), (wp, wp_d),
                             (exp16, exp16_d), (trils, tril_d)):
                sL.inc(sync.dma_start(dst[:, :], src.ap()), 16)
            for nm, src in (("K", wk_d), ("Q", wq_d), ("V", wv_d)):
                sL.inc(sync.dma_start(wraw[nm][:, :], src.ap()), 16)
        P.sync(f)
        consts_loaded = sL.n

        def f(V):
            V.memset(onescol_b[:, :], 1.0)
            V.memset(ones1x64[:, :], 1.0)
            V.memset(vnat[:, :].rearrange("p (n c) -> p n c", c=65)[:, :, 64], 1.0)
        P.vector(f)

        tp1_trail = []
        tile_done = {}
        for i in range(NT):
            xb = x0nat[i % 2]

            lsem = sLx[i % 2]

            def f(sync, i=i, xb=xb, lsem=lsem):
                if i >= 2:
                    sync.wait_ge(cV.sem, tile_done[i - 2])
                lsem.inc(sync.dma_start(xb[:, :], x0_d.ap()[i * 128:(i + 1) * 128, :]), 16)
            P.sync(f)
            ld_now = lsem.n

            def f(A, xb=xb, ld_now=ld_now, lsem=lsem):
                A.eng.wait_ge(lsem.sem, ld_now)
                A.activation(out=sqnat[:, :], in_=xb[:, :], func=AF.Square)
            P.scalar(f)
            sq_done = cA.n

            def f(V, xb=xb, ld_now=ld_now, sq_done=sq_done, lsem=lsem):
                V.eng.wait_ge(lsem.sem, ld_now)
                V.tensor_reduce(statc[:, 0:1], xb[:, :], AX.X, ALU.add)
                cA.wait(V, sq_done)
                V.tensor_reduce(statc[:, 1:2], sqnat[:, :], AX.X, ALU.add)
                V.tensor_scalar_mul(statc[:, 2:3], statc[:, 0:1], 1.0 / D)
                V.tensor_mul(statc[:, 3:4], statc[:, 2:3], statc[:, 2:3])
                V.scalar_tensor_tensor(statc[:, 4:5], statc[:, 1:2], 1.0 / D,
                                       statc[:, 3:4], ALU.mult, ALU.subtract)
                V.tensor_scalar_add(statc[:, 4:5], statc[:, 4:5], 1e-5)
            P.vector(f)
            var_done = cV.n

            def f(A, var_done=var_done):
                cV.wait(A, var_done)
                A.activation(out=statc[:, 5:6], in_=statc[:, 4:5], func=AF.Sqrt)
            P.scalar(f)
            sqrt_done = cA.n

            def f(V, xb=xb, sqrt_done=sqrt_done):
                cA.wait(V, sqrt_done)
                V.reciprocal(statc[:, 4:5], statc[:, 5:6])
                V.tensor_scalar(xn_nat[:, :], xb[:, :], statc[:, 2:3],
                                statc[:, 4:5], ALU.subtract, ALU.mult)
            P.vector(f)
            xn_done = cV.n

            for g in range(NG):
                pst = psT0 if g % 2 == 0 else psT1

                def f(Te, g=g, pst=pst, xn_done=xn_done, i=i):
                    if i == 0 and g == 0:
                        Te.eng.wait_ge(sL.sem, consts_loaded)
                    cV.wait(Te, xn_done)
                    if len(tp1_trail) >= 2:
                        Te.eng.wait_ge(cV.sem, tp1_trail[-2])
                    Te.transpose(pst[:, :], xn_nat[:, g * 128:(g + 1) * 128], eyeB[:, :])
                P.tensor(f)
                tpn = cT.n

                def f(V, g=g, i=i, pst=pst, tpn=tpn):
                    cT.wait(V, tpn)
                    V.tensor_copy(xnT[:, g * T + i * 128: g * T + (i + 1) * 128], pst[:, :])
                    tp1_trail.append(cV.n)
                P.vector(f)
            tile_done[i] = cV.n

        for nm in ("K", "Q", "V"):
            def f(V, nm=nm):
                V.eng.wait_ge(sL.sem, consts_loaded)
                V.tensor_reduce(col16[:, :], wraw[nm][:, :], AX.X, ALU.max)
                V.tensor_scalar_mul(col16[:, :], col16[:, :], -1.0)
            P.vector(f)
            neg_done = cV.n

            def f(A, nm=nm, neg_done=neg_done):
                A.eng.wait_ge(sL.sem, consts_loaded)
                cV.wait(A, neg_done)
                A.activation(out=scr16[:, :], in_=wraw[nm][:, :], func=AF.Exp,
                             bias=col16[:, :], accum_out=col16[:, :])
            P.scalar(f)
            exp_done = cA.n

            def f(V, nm=nm, exp_done=exp_done):
                cA.wait(V, exp_done)
                V.reciprocal(col16[:, :], col16[:, :])
                V.tensor_single_scalar(scr16[:, :], scr16[:, :], col16[:, :], ALU.mult)
            P.vector(f)
            probs_done = cV.n

            def f(Te, probs_done=probs_done):
                cV.wait(Te, probs_done)
                Te.transpose(ps16[:, :], scr16[:, :], eyeF[0:16, 0:16])
            P.tensor(f)
            t16 = cT.n

            def f(V, nm=nm, t16=t16):
                cT.wait(V, t16)
                V.tensor_copy(probsT[nm][:, :], ps16[:, :])
            P.vector(f)
            pc_cp = cV.n

            for r in range(NG):
                def f(Te, nm=nm, r=r, pc_cp=pc_cp):
                    cV.wait(Te, pc_cp)
                    Te.matmul(psPC[:, :], exp16[:, r * 128:(r + 1) * 128],
                              probsT[nm][:, :], start=True, stop=True)
                P.tensor(f)
                pc_mm = cT.n

                def f(V, nm=nm, r=r, pc_mm=pc_mm):
                    cT.wait(V, pc_mm)
                    V.tensor_copy(pcols[nm][:, r * 16:(r + 1) * 16], psPC[:, :])
                P.vector(f)
                pc_cp = cV.n

        def f(V):
            for nm in ("K", "Q", "V"):
                for v in range(16):
                    pb = (v % 2) * 64
                    rr = v // 2
                    for i in range(16):
                        if nm == "Q":
                            src = wp[pb:pb + 64, i * 64:(i + 1) * 64]
                        else:
                            src = eyeF[pb:pb + 64, pb:pb + 64]
                        V.tensor_single_scalar(
                            mats[nm][pb:pb + 64, rr * 1024 + i * 64: rr * 1024 + (i + 1) * 64],
                            src, pcols[nm][pb:pb + 64, rr * 16 + i: rr * 16 + i + 1], ALU.mult)
        P.vector(f)
        mats_done = cV.n

        if debug:
            def f(sync):
                sync.wait_ge(cV.sem, mats_done)
                sS.inc(sync.dma_start(xnT_dbg.ap(), xnT[:, :]), 16)
                for mi, nm in enumerate(("K", "Q", "V")):
                    sS.inc(sync.dma_start(
                        mats_dbg.ap()[:, mi * NG * 1024:(mi + 1) * NG * 1024],
                        mats[nm][:, :]), 16)
            P.sync(f)

        pctx.close()
        pctx = ExitStack()
        psf = lambda name, shape, dt=F32: pctx.enter_context(nc.psum_tensor(name + "_ps", shape, dt))
        psKQV0 = psf("psKQV0", [64, CH])
        psKQV1 = psf("psKQV1", [64, CH])
        psTB0 = psf("psTB0", [128, 64], BF16)
        psTB1 = psf("psTB1", [128, 64], BF16)
        psS0 = psf("psS0", [128, CH])
        psS1 = psf("psS1", [128, CH])
        psS2 = psf("psS2", [128, CH])
        psZ = psf("psZ", [65, CH])

        evac_trail = []
        vt_trail = []
        zfin_trail = []
        kqv_dump_sS = None
        for hi in range(16):
            for pi, (nm, dst) in enumerate((("K", kT_i), ("Q", qT_i), ("V", vT_i))):
                for tc in range(NCH):
                    j = (pi * NCH + tc) % 2
                    pskv = psKQV0 if j == 0 else psKQV1

                    def f(Te, nm=nm, tc=tc, pskv=pskv, pi=pi, hi=hi):
                        if hi == 0 and pi == 0 and tc == 0:
                            cV.wait(Te, mats_done)
                        if len(evac_trail) >= 2:
                            Te.eng.wait_ge(cV.sem, evac_trail[-2])
                        for r in range(NG):
                            Te.matmul(
                                pskv[:, :],
                                mats[nm][:, r * 1024 + hi * 64: r * 1024 + (hi + 1) * 64],
                                xnT[:, r * T + tc * CH: r * T + (tc + 1) * CH],
                                start=(r == 0), stop=(r == NG - 1))
                    P.tensor(f)
                    mm_done = cT.n

                    def f(V, dst=dst, tc=tc, pskv=pskv, mm_done=mm_done,
                          hi=hi, pi=pi, kqv_dump_sS=kqv_dump_sS):
                        if hi == 1 and pi == 0 and tc == 0 and kqv_dump_sS is not None:
                            V.eng.wait_ge(sS.sem, kqv_dump_sS)
                        cT.wait(V, mm_done)
                        V.tensor_copy(dst[:, tc * CH:(tc + 1) * CH], pskv[:, :])
                        evac_trail.append(cV.n)
                    P.vector(f)

            if debug and hi == 0:
                kqv_done0 = cV.n

                def f(sync, kqv_done0=kqv_done0):
                    sync.wait_ge(cV.sem, kqv_done0)
                    for src_t, off in ((kT_i, 0), (qT_i, T), (vT_i, 2 * T)):
                        sS.inc(sync.dma_start(kqv_dbg.ap()[:, off:off + T], src_t[:, :]), 16)
                P.sync(f)
                kqv_dump_sS = sS.n

            kqv_done = cV.n
            for pt in range(NT):
                pst = psTB0 if pt % 2 == 0 else psTB1

                def f(Te, pt=pt, pst=pst, kqv_done=kqv_done):
                    if pt == 0:
                        cV.wait(Te, kqv_done)
                    if len(vt_trail) >= 2:
                        Te.eng.wait_ge(cV.sem, vt_trail[-2])
                    Te.transpose(pst[:, :], vT_i[:, pt * 128:(pt + 1) * 128], eyeB[0:64, 0:64])
                P.tensor(f)
                vt_mm = cT.n

                def f(V, pt=pt, pst=pst, vt_mm=vt_mm):
                    cT.wait(V, vt_mm)
                    V.tensor_copy(vnat[:, pt * 65:pt * 65 + 64], pst[:, :])
                    vt_trail.append(cV.n)
                P.vector_f(f)

            vnat_done = cV.n
            ZLAG = 2
            for qc in range(NCH):
                np_tiles = (qc + 1) * QT
                cA_base = cA.n
                exp_specs = []
                smm_list = []
                zmm_list = []
                zspecs = []

                def emit_z(spec, np_tiles=np_tiles):
                    pt, eb, diag, e_done, m_done = spec

                    def f(Te, pt=pt, eb=eb, diag=diag, np_tiles=np_tiles,
                          e_done=e_done, m_done=m_done):
                        if diag:
                            cV.wait(Te, m_done)
                        else:
                            cA.wait(Te, e_done)
                        if pt == 0 and zfin_trail:
                            Te.eng.wait_ge(cV.sem, zfin_trail[-1])
                        Te.matmul(psZ[:, :], vnat[:, pt * 65:pt * 65 + 65], eb[:, :],
                                  start=(pt == 0), stop=(pt == np_tiles - 1))
                    P.tensor(f)
                    zmm_list.append(cT.n)

                for pt in range(np_tiles):
                    pss = (psS0, psS1, psS2)[pt % 3]
                    eb = expS[pt % 3]
                    diag = pt >= qc * QT
                    e_done = cA_base + pt + 1

                    def f(Te, pt=pt, pss=pss, qc=qc, hi=hi, vnat_done=vnat_done,
                          e_done=e_done, cA_base=cA_base):
                        if qc == 0 and pt == 0:
                            cV.wait(Te, vnat_done)
                        if pt >= 3:
                            Te.eng.wait_ge(cA.sem, cA_base + pt - 2)
                        Te.matmul(pss[:, :], kT_i[:, pt * 128:(pt + 1) * 128],
                                  qT_i[:, qc * CH:(qc + 1) * CH], start=True, stop=True)
                    P.tensor(f)
                    smm_list.append(cT.n)
                    exp_specs.append((smm_list[-1], pss, eb, pt))

                    m_done = None
                    if diag:
                        def f(V, eb=eb, pt=pt, qc=qc, e_done=e_done):
                            cA.wait(V, e_done)
                            dpos = pt - qc * QT
                            V.tensor_mul(eb[:, :], eb[:, :], trils[:, dpos * CH:(dpos + 1) * CH])
                        P.vector(f)
                        m_done = cV.n

                    zspecs.append((pt, eb, diag, e_done, m_done))
                    if pt >= ZLAG:
                        emit_z(zspecs[pt - ZLAG])
                for pt in range(max(0, np_tiles - ZLAG), np_tiles):
                    emit_z(zspecs[pt])

                def f(A, exp_specs=exp_specs, zmm_list=zmm_list):
                    for smm, pss, eb, zi in exp_specs:
                        A.eng.wait_ge(cT.sem, smm)
                        if zi >= 3:
                            A.eng.wait_ge(cT.sem, zmm_list[zi - 3])
                        A.activation(out=eb[:, :], in_=pss[:, :], func=AF.Exp, scale=0.125)
                P.scalar_f(f)
                assert cA.n == cA_base + np_tiles

                zacc_done = cT.n

                def f(V, zacc_done=zacc_done):
                    cT.wait(V, zacc_done)
                    V.reciprocal(rec_row[:, :], psZ[64:65, :])
                    V.tensor_copy(ztmp[:, :], psZ[0:64, :])
                P.vector(f)
                ev_done = cV.n

                def f(Te, ev_done=ev_done):
                    cV.wait(Te, ev_done)
                    Te.matmul(psZ[0:64, :], ones1x64[:, :], rec_row[:, :], start=True, stop=True)
                P.tensor(f)
                bc_done = cT.n

                zslot = (hi * NCH + qc) % 2
                zs = zstg[zslot]
                zsem = sSz[zslot]

                def f(V, bc_done=bc_done, zs=zs, zsem=zsem):
                    cT.wait(V, bc_done)
                    V.eng.wait_ge(zsem.sem, zsem.n)
                    V.tensor_mul(zs[:, :], ztmp[:, :], psZ[0:64, :])
                    zfin_trail.append(cV.n)
                P.vector(f)
                zmul_done = cV.n

                def f(sync, qc=qc, hi=hi, zs=zs, zsem=zsem, zmul_done=zmul_done):
                    sync.wait_ge(cV.sem, zmul_done)
                    zsem.inc(sync.dma_start(
                        zT_d.ap()[hi * 64:(hi + 1) * 64, qc * CH:(qc + 1) * CH], zs[:, :]), 16)
                P.sync(f)

        attn_done = cV.n
        if debug:
            def f(sync):
                sync.wait_ge(sSz[0].sem, sSz[0].n)
                sync.wait_ge(sSz[1].sem, sSz[1].n)
                sS.inc(sync.dma_start(zT_dbg.ap(), zT_d.ap()), 16)
            P.sync(f)

        pctx.close()
        pctx = ExitStack()
        psf = lambda name, shape, dt=F32: pctx.enter_context(nc.psum_tensor(name + "_ps", shape, dt))
        psX0 = psf("psX0", [128, 128])
        psX1 = psf("psX1", [128, 128])

        add_trail = []
        row_done = {}
        st_issued = {}
        for i in range(NT):
            xb = x0nat[i % 2]
            xr = x1row[i % 2]

            lsem = sLx[i % 2]
            ssem = sSx[i % 2]

            def f(sync, i=i, xb=xb, lsem=lsem, ssem=ssem):
                if i == 0:
                    sync.wait_ge(cV.sem, attn_done)
                    sync.wait_ge(sSz[0].sem, sSz[0].n)
                    sync.wait_ge(sSz[1].sem, sSz[1].n)
                if i >= 2:
                    sync.wait_ge(ssem.sem, ssem.n)
                    sync.wait_ge(cV.sem, row_done[i - 2])
                lsem.inc(sync.dma_start(xb[:, :], x0_d.ap()[i * 128:(i + 1) * 128, :]), 16)
                if i >= 1:
                    sync.wait_ge(cV.sem, row_done[i - 1])
                for g in range(NG):
                    lsem.inc(sync.dma_start(
                        zrow[:, g * 128:(g + 1) * 128],
                        zT_d.ap()[g * 128:(g + 1) * 128, i * 128:(i + 1) * 128]), 16)
            P.sync(f)
            ld_now = lsem.n

            for g in range(NG):
                pst = psX0 if g % 2 == 0 else psX1

                def f(Te, g=g, pst=pst, xb=xb, ld_now=ld_now, lsem=lsem):
                    Te.eng.wait_ge(lsem.sem, ld_now)
                    if len(add_trail) >= 2:
                        Te.eng.wait_ge(cV.sem, add_trail[-2])
                    Te.transpose(pst[:, :], xb[:, g * 128:(g + 1) * 128], eyeF[:, :])
                P.tensor(f)
                x_mm = cT.n

                def f(V, g=g, i=i, pst=pst, xr=xr, x_mm=x_mm, ld_now=ld_now):
                    V.eng.wait_ge(lsem.sem, ld_now)
                    cT.wait(V, x_mm)
                    V.tensor_add(xr[:, g * 128:(g + 1) * 128], pst[:, :],
                                 zrow[:, g * 128:(g + 1) * 128])
                    add_trail.append(cV.n)
                P.vector(f)
            row_done[i] = cV.n

            def f(sync, i=i, xr=xr, ssem=ssem):
                sync.wait_ge(cV.sem, row_done[i])
                for g in range(NG):
                    ssem.inc(sync.dma_start(
                        x1T_d.ap()[g * 128:(g + 1) * 128, i * 128:(i + 1) * 128],
                        xr[:, g * 128:(g + 1) * 128]), 16)
            P.sync(f)

        def f(sync):
            sync.wait_ge(sSx[0].sem, sSx[0].n)
            sync.wait_ge(sSx[1].sem, sSx[1].n)
            sync.wait_ge(sS.sem, sS.n)
        P.sync(f)
        pctx.close()





def build_A(T, debug=False):
    nc = bass.Bass("TRN2", target_bir_lowering=False)
    NT = T // 128
    NCH = T // CH
    QT = CH // 128

    x0_d = nc.dram_tensor("x0", [T, D], F32, kind="ExternalInput")
    wk_d = nc.dram_tensor("w_k", [16, 16], F32, kind="ExternalInput")
    wq_d = nc.dram_tensor("w_q", [16, 16], F32, kind="ExternalInput")
    wv_d = nc.dram_tensor("w_v", [16, 16], F32, kind="ExternalInput")
    wp_d = nc.dram_tensor("wpdup", [128, 1024], F32, kind="ExternalInput")
    eyeF_d = nc.dram_tensor("eyeF", [128, 128], F32, kind="ExternalInput")
    eyeB_d = nc.dram_tensor("eyeB", [128, 128], BF16, kind="ExternalInput")
    exp16_d = nc.dram_tensor("expand16", [16, 1024], F32, kind="ExternalInput")
    tril_d = nc.dram_tensor("trils", [128, 4 * CH], BF16, kind="ExternalInput")
    x1T_d = nc.dram_tensor("x1T", [D, T], F32, kind="ExternalOutput")
    zT_d = nc.dram_tensor("zT_scratch", [D, T], BF16, kind="Internal")
    if debug:
        xnT_dbg = nc.dram_tensor("xnT_dbg", [128, NG * T], BF16, kind="ExternalOutput")
        mats_dbg = nc.dram_tensor("mats_dbg", [128, 3 * NG * 1024], BF16, kind="ExternalOutput")
        kqv_dbg = nc.dram_tensor("kqv_dbg", [64, 3 * T], BF16, kind="ExternalOutput")
        zT_dbg = nc.dram_tensor("zT_dbg", [D, T], BF16, kind="ExternalOutput")

    d = {"x0": x0_d, "w_k": wk_d, "w_q": wq_d, "w_v": wv_d, "wpdup": wp_d, "eyeF": eyeF_d, "eyeB": eyeB_d, "expand16": exp16_d, "trils": tril_d, "x1T": x1T_d, "zT_scratch": zT_d}
    if debug:
        d.update({"xnT_dbg": xnT_dbg, "mats_dbg": mats_dbg, "kqv_dbg": kqv_dbg, "zT_dbg": zT_dbg})
    with ExitStack() as semctx:
        emit_A(nc, semctx, d, T, debug=debug)
    return nc


def emit_B(nc, semctx, d, T, debug=False):
    NHALF = 2
    TH = T // NHALF
    NTCH = TH // CH
    E = 64
    x1T_d = d["x1T"]
    moe_d = d["moe_img"]
    sg_d = d["sg_img"]
    su_d = d["su_img"]
    sd_d = d["sd_img"]
    gate_d = d["gate_img"]
    gbias_d = d["gate_bias"]
    sel_d = d["sel_img"]
    eyeF_d = d["eyeF"]
    x2T_d = d["x2T"]
    xn_dbg = d.get("xn_dbg")
    comb_dbg = d.get("comb_dbg")
    sh_dbg = d.get("sh_dbg")
    with ExitStack() as ctx:
        sb = lambda name, shape, dt: ctx.enter_context(nc.sbuf_tensor(name + "_sb", shape, dt))

        acc = sb("acc", [128, NG * TH], F32)
        xnT = sb("xnTb", [128, NG * TH], BF16)
        combT = sb("combT", [64, TH], BF16)
        hs = sb("hs", [128, NG * CH], BF16)
        hc = sb("hc", [128, 4 * CH], BF16)
        h1 = sb("h1", [128, CH], BF16)
        silu_sb = [sb(f"silu{j}", [128, CH], BF16) for j in range(2)]
        wslot = [sb(f"wslot{j}", [128, 12288], BF16) for j in range(2)]
        selslot = [sb(f"selslot{j}", [64, 128], BF16) for j in range(2)]
        gatew = sb("gatew", [128, 512], F32)
        gsum = sb("gsum", [64, 1], F32)
        mrows = sb("mrows", [1, 2048], F32)
        rsrows = sb("rsrows", [1, 2048], F32)

        gbias = sb("gbias", [64, 1], F32)
        eyeF = sb("eyeFb", [128, 128], F32)
        onescol_f = sb("onescol_f", [128, 1], F32)
        ones1x128 = sb("ones1x128", [1, 128], F32)
        scrB = sb("scrB", [128, CH], F32)
        sq = scrB
        gmb = scrB[0:64, :]
        lgb = scrB[64:128, :]
        rowst = sb("rowst", [1, 3 * CH], F32)

        sc_tm = sb("sc_tm", [128, 4 * 64], F32)
        sgate = scrB[0:64, :]
        gr = _apify(sb("gr", [128, 8], F32))
        gso = _apify(sb("gso", [128, 8], F32))
        mcol = _apify(sb("mcol", [128, 1], F32))
        gmask = _apify(sb("gmask", [128, 8], F32))
        msk64 = _apify(sb("msk64", [128, 64], F32))
        msv = _apify(sb("msv", [128, 64], F32))
        mso = _apify(sb("mso", [128, 64], F32))
        eqv = _apify(sb("eqv", [128, 64], F32))
        cu = _apify(sb("cu", [128, 64], F32))
        dcol = _apify(sb("dcol", [128, 1], F32))
        comb_tm = _apify(sb("comb_tm", [128, 64], F32))

        P = Phase(nc, semctx, "B_", blk_ctx=ctx)
        sL, sS, cT, cV, cA = P.sL, P.sS, P.cT, P.cV, P.cA
        sG = P.sG
        sWs = [P.ct("sW0"), P.ct("sW1")]

        pctx = ExitStack()
        pool = [pctx.enter_context(nc.psum_tensor(f"pp{j}_ps", [128, CH], F32))
                for j in range(8)]
        psSUM = pool[0][0:1, :]
        psSQ = pool[1][0:1, :]
        psM = pool[2][:, :]
        psRS = pool[3][:, :]

        def f(sync):
            for dst, src in ((gbias, gbias_d), (eyeF, eyeF_d)):
                sL.inc(sync.dma_start(dst[:, :], src.ap()), 16)
            sL.inc(sync.dma_start(gatew[:, :], gate_d.ap()), 16)
        P.sync(f)

        consts_sL = sL.n

        def f(V):
            V.memset(onescol_f[:, :], 1.0)
            V.memset(ones1x128[:, :], 1.0)
        P.vector(f)
        memset_done = cV.n

        def f(Te):
            cV.wait(Te, memset_done)
            Te.eng.wait_ge(sL.sem, consts_sL)
            for r in range(NG):
                Te.matmul(pool[6][0:64, 0:1], gatew[:, r * 64:(r + 1) * 64],
                          onescol_f[:, :], start=(r == 0), stop=(r == NG - 1))
        P.tensor(f)
        gsum_mm = cT.n

        def f(V, gsum_mm=gsum_mm):
            cT.wait(V, gsum_mm)
            V.tensor_copy(gsum[:, :], pool[6][0:64, 0:1])
        P.vector(f)
        gsum_done = cV.n

        wfree = {}
        last_store = 0

        for h in range(NHALF):
            sLh = P.ct(f"sLh{h}")

            def f(sync, h=h, sLh=sLh):
                if h > 0:
                    sync.wait_ge(sS.sem, sS.n)
                for g in range(NG):
                    sLh.inc(sync.dma_start(
                        acc[:, g * TH:(g + 1) * TH],
                        x1T_d.ap()[g * 128:(g + 1) * 128, h * TH:(h + 1) * TH]), 16)
            P.sync(f)
            x1_ld = sLh.n

            prev_var_done = 0
            prev_xn_done = 0
            for tc in range(NTCH):
                sq_tgt = []
                mm_tgt = []
                an = cA.n
                for g in range(NG):
                    an += 1
                    sq_tgt.append(an)

                    def f(Te, tc=tc, g=g, x1_ld=x1_ld, sLh=sLh, h=h,
                          pvd=prev_var_done):
                        Te.eng.wait_ge(sLh.sem, x1_ld)
                        if h == 0 and tc == 0 and g == 0:
                            cV.wait(Te, memset_done)
                            Te.eng.wait_ge(sL.sem, consts_sL)
                        if g == 0 and pvd:
                            Te.eng.wait_ge(cV.sem, pvd)
                        cA.wait(Te, sq_tgt[g])
                        Te.matmul(psSUM, onescol_f[:, :],
                                  acc[:, g * TH + tc * CH: g * TH + (tc + 1) * CH],
                                  start=(g == 0), stop=(g == NG - 1))
                        Te.matmul(psSQ, onescol_f[:, :], sq[:, :],
                                  start=(g == 0), stop=(g == NG - 1))
                    P.tensor(f)
                    mm_tgt.append(cT.n)

                def f(A, tc=tc, x1_ld=x1_ld, sLh=sLh, pxd=prev_xn_done,
                      mm_tgt=mm_tgt):
                    for g in range(NG):
                        if g == 0:
                            A.eng.wait_ge(sLh.sem, x1_ld)
                            if pxd:
                                A.eng.wait_ge(cV.sem, pxd)
                        else:
                            A.eng.wait_ge(cT.sem, mm_tgt[g - 1])
                        A.activation(out=sq[:, :],
                                     in_=acc[:, g * TH + tc * CH: g * TH + (tc + 1) * CH],
                                     func=AF.Square)
                P.scalar_f(f)
                assert cA.n == sq_tgt[-1], (cA.n, sq_tgt[-1])
                sums_done = cT.n

                def f(V, sums_done=sums_done):
                    cT.wait(V, sums_done)
                    V.tensor_scalar_mul(rowst[:, 0:CH], psSUM, 1.0 / D)
                    V.tensor_mul(rowst[:, 2 * CH:3 * CH], rowst[:, 0:CH], rowst[:, 0:CH])
                    V.scalar_tensor_tensor(rowst[:, CH:2 * CH], psSQ, 1.0 / D,
                                           rowst[:, 2 * CH:3 * CH], ALU.mult, ALU.subtract)
                    V.tensor_scalar_add(rowst[:, CH:2 * CH], rowst[:, CH:2 * CH], 1e-5)
                P.vector(f)
                prev_var_done = var_done = cV.n

                def f(A, var_done=var_done):
                    cV.wait(A, var_done)
                    A.activation(out=rowst[:, 2 * CH:3 * CH], in_=rowst[:, CH:2 * CH],
                                 func=AF.Sqrt)
                P.scalar(f)
                sqrt_done = cA.n

                def f(V, sqrt_done=sqrt_done):
                    cA.wait(V, sqrt_done)
                    V.reciprocal(rowst[:, CH:2 * CH], rowst[:, 2 * CH:3 * CH])
                P.vector(f)
                rs_done = cV.n

                def f(Te, rs_done=rs_done):
                    cV.wait(Te, rs_done)
                    Te.matmul(psM, ones1x128[:, :], rowst[:, 0:CH], start=True, stop=True)
                    Te.matmul(psRS, ones1x128[:, :], rowst[:, CH:2 * CH],
                              start=True, stop=True)
                P.tensor(f)
                bc_done = cT.n

                def f(V, tc=tc, bc_done=bc_done):
                    cT.wait(V, bc_done)
                    V.tensor_copy(mrows[:, tc * CH:(tc + 1) * CH], rowst[:, 0:CH])
                    V.tensor_copy(rsrows[:, tc * CH:(tc + 1) * CH], rowst[:, CH:2 * CH])
                    for g in range(NG):
                        V.tensor_tensor(sq[:, :],
                                        acc[:, g * TH + tc * CH: g * TH + (tc + 1) * CH],
                                        psM, ALU.subtract)
                        V.tensor_tensor(xnT[:, g * TH + tc * CH: g * TH + (tc + 1) * CH],
                                        sq[:, :], psRS, ALU.mult)
                P.vector(f)
                prev_xn_done = cV.n
            xn_done_h = cV.n

            if debug and h == 0:
                def f(sync, xn_done_h=xn_done_h):
                    sync.wait_ge(cV.sem, xn_done_h)
                    sS.inc(sync.dma_start(xn_dbg.ap(), xnT[:, :]), 16)
                P.sync(f)

            last_sig = 0
            last_tr_cp = 0
            last_gm = 0
            for tc in range(NTCH):
                def f(Te, tc=tc, xn_done_h=xn_done_h, last_gm=last_gm, h=h):
                    cV.wait(Te, xn_done_h)
                    if h == 0 and tc == 0:
                        cV.wait(Te, gsum_done)
                    if last_gm:
                        Te.eng.wait_ge(cV.sem, last_gm)
                    for r in range(NG):
                        Te.matmul(pool[4][0:64, :], gatew[:, r * 64:(r + 1) * 64],
                                  acc[:, r * TH + tc * CH: r * TH + (tc + 1) * CH],
                                  start=(r == 0), stop=(r == NG - 1))
                    Te.matmul(pool[2][0:64, :], ones1x128[:, 0:64],
                              mrows[:, tc * CH:(tc + 1) * CH], start=True, stop=True)
                    Te.matmul(pool[3][0:64, :], ones1x128[:, 0:64],
                              rsrows[:, tc * CH:(tc + 1) * CH], start=True, stop=True)
                P.tensor(f)
                g_mm = cT.n

                def f(V, g_mm=g_mm):
                    cT.wait(V, g_mm)
                    V.tensor_single_scalar(gmb, pool[2][0:64, :], gsum[:, :], ALU.mult)
                    V.tensor_tensor(lgb, pool[4][0:64, :], gmb, ALU.subtract)
                    V.tensor_tensor(lgb, lgb, pool[3][0:64, :], ALU.mult)
                P.vector(f)
                last_gm = lg_done = cV.n

                def f(A, lg_done=lg_done):
                    cV.wait(A, lg_done)
                    A.activation(out=sgate, in_=lgb, func=AF.Sigmoid)
                P.scalar(f)
                last_sig = sig_done = cA.n

                def f(V, sig_done=sig_done):
                    cA.wait(V, sig_done)
                    V.tensor_scalar_add(sgate, sgate, gbias[:, :])
                P.vector(f)
                sb_done = cV.n

                for tt in range(4):
                    def f(Te, tt=tt, sb_done=sb_done, last_tr_cp=last_tr_cp):
                        cV.wait(Te, sb_done)
                        if last_tr_cp:
                            Te.eng.wait_ge(cV.sem, last_tr_cp)
                        Te.transpose(pool[5][:, 0:64], sgate[:, tt * 128:(tt + 1) * 128][0:64, :] if False else scrB[0:64, tt * 128:(tt + 1) * 128],
                                     eyeF[0:64, 0:64])
                    P.tensor(f)
                    tr_done = cT.n

                    def f(V, tt=tt, tr_done=tr_done):
                        cT.wait(V, tr_done)
                        V.tensor_copy(sc_tm[:, tt * 64:(tt + 1) * 64], pool[5][:, 0:64])
                    P.vector(f)
                    last_tr_cp = cV.n

                for tt in range(4):
                    def f(V, tt=tt):
                        sc = sc_tm[:, tt * 64:(tt + 1) * 64]
                        scv = sc.rearrange("p (g j) -> p g j", j=8)
                        V.tensor_reduce(gr.rearrange("p (g o) -> p g o", o=1),
                                        scv, AX.X, ALU.max)
                        V.tensor_copy(gso, gr)
                        for j in range(4):
                            V.reduce_max(mcol, gr, AX.X)
                            if j < 3:
                                V.tensor_single_scalar(gmask, gr, mcol,
                                                       ALU.is_ge)
                                V.scalar_tensor_tensor(gr, gmask, -1e30,
                                                       gr, ALU.mult, ALU.add)
                        V.tensor_single_scalar(gmask, gso, mcol, ALU.is_ge)
                        m64 = msk64.rearrange("p (g j) -> p g j", j=8)
                        for jj in range(8):
                            V.tensor_copy(m64[:, :, jj], gmask)
                        V.tensor_mul(msv, sc, msk64)
                        V.tensor_copy(mso, msv)
                        for j in range(6):
                            V.reduce_max(mcol, msv, AX.X)
                            if j < 5:
                                V.tensor_single_scalar(eqv, msv, mcol,
                                                       ALU.is_ge)
                                V.scalar_tensor_tensor(msv, eqv, -1e30,
                                                       msv, ALU.mult, ALU.add)
                        V.tensor_scalar_max(mcol, mcol, 0.0)
                        V.tensor_single_scalar(eqv, mso, mcol, ALU.is_ge)
                        V.tensor_mul(cu, eqv, mso)
                        V.reduce_sum(dcol, cu, AX.X)
                        V.tensor_scalar_add(dcol, dcol, 1e-8)
                        V.reciprocal(dcol, dcol)
                        V.tensor_single_scalar(comb_tm, cu, dcol, ALU.mult)
                    P.vector(f)
                    ct_done = cV.n

                    def f(Te, ct_done=ct_done, last_tr_cp=last_tr_cp):
                        cV.wait(Te, ct_done)
                        Te.eng.wait_ge(cV.sem, last_tr_cp)
                        Te.transpose(pool[5][0:64, 0:128], comb_tm, eyeF[:, :])
                    P.tensor(f)
                    cb_mm = cT.n

                    def f(V, tc=tc, tt=tt, cb_mm=cb_mm):
                        cT.wait(V, cb_mm)
                        V.tensor_copy(combT[:, tc * CH + tt * 128: tc * CH + (tt + 1) * 128],
                                      pool[5][0:64, 0:128])
                    P.vector(f)
                    last_tr_cp = cV.n
            comb_done_h = cV.n

            if debug and h == 0:
                def f(sync, comb_done_h=comb_done_h):
                    sync.wait_ge(cV.sem, comb_done_h)
                    sS.inc(sync.dma_start(comb_dbg.ap()[:, 0:TH], combT[:, :]), 16)
                P.sync(f)

            tfull = cT.n

            def f(G, tfull=tfull):
                G.eng.wait_ge(cT.sem, tfull)
                sG.inc(G.eng.dma_start(wslot[0][:, 0:8192], sg_d.ap()), 16)
                sG.inc(G.eng.dma_start(wslot[0][:, 8192:12288], su_d.ap()[:, 0:4096]), 16)
                sG.inc(G.eng.dma_start(wslot[1][:, 0:4096], su_d.ap()[:, 4096:8192]), 16)
                sG.inc(G.eng.dma_start(wslot[1][:, 4096:12288], sd_d.ap()), 16)
            P.gpsimd(f)
            shw_ld = sG.n

            def _su(r, mt):
                off = 8192 + r * 1024 + mt * 128
                if off < 12288:
                    return wslot[0][:, off:off + 128]
                return wslot[1][:, off - 12288:off - 12288 + 128]

            act_next = cA.n
            silu_tgt = []
            hmul_tgt = []
            for tc in range(NTCH):
                specs = []
                for mt in range(NG):
                    k = len(silu_tgt)
                    pg = pool[4] if k % 2 == 0 else pool[0]
                    pu = pool[5] if k % 2 == 0 else pool[1]

                    def f(Te, tc=tc, mt=mt, k=k, pg=pg, comb_done_h=comb_done_h,
                          shw_ld=shw_ld):
                        if tc == 0 and mt == 0:
                            cV.wait(Te, comb_done_h)
                            Te.eng.wait_ge(sG.sem, shw_ld)
                        if k >= 2:
                            Te.eng.wait_ge(cA.sem, silu_tgt[k - 2])
                        for r in range(NG):
                            Te.matmul(pg[:, :],
                                      wslot[0][:, r * 1024 + mt * 128: r * 1024 + (mt + 1) * 128],
                                      xnT[:, r * TH + tc * CH: r * TH + (tc + 1) * CH],
                                      start=(r == 0), stop=(r == NG - 1))
                    P.tensor(f)
                    ga_done = cT.n
                    act_next += 1
                    silu_tgt.append(act_next)
                    specs.append((ga_done, pg, k))

                    def f(Te, tc=tc, mt=mt, k=k, pu=pu):
                        if k >= 2:
                            Te.eng.wait_ge(cV.sem, hmul_tgt[k - 2])
                        for r in range(NG):
                            Te.matmul(pu[:, :], _su(r, mt),
                                      xnT[:, r * TH + tc * CH: r * TH + (tc + 1) * CH],
                                      start=(r == 0), stop=(r == NG - 1))
                    P.tensor(f)
                    ua_done = cT.n

                    def f(V, mt=mt, k=k, ua_done=ua_done, pu=pu):
                        cT.wait(V, ua_done)
                        cA.wait(V, silu_tgt[k])
                        V.tensor_tensor(hs[:, mt * CH:(mt + 1) * CH],
                                        silu_sb[k % 2][:, :], pu[:, :], ALU.mult)
                    P.vector(f)
                    hmul_tgt.append(cV.n)
                hs_done = cV.n

                def f(A, specs=specs):
                    for ga, pg, k in specs:
                        A.eng.wait_ge(cT.sem, ga)
                        if k >= 2:
                            A.eng.wait_ge(cV.sem, hmul_tgt[k - 2])
                        A.activation(out=silu_sb[k % 2][:, :], in_=pg[:, :],
                                     func=AF.Silu)
                P.scalar_f(f)
                assert cA.n == silu_tgt[-1], (cA.n, silu_tgt[-1])

                ev_done = []
                for mt in range(NG):
                    def f(Te, tc=tc, mt=mt, hs_done=hs_done):
                        if mt == 0:
                            cV.wait(Te, hs_done)
                        if mt >= 2:
                            cV.wait(Te, ev_done[mt - 2])
                        pd = pool[6] if mt % 2 == 0 else pool[7]
                        for r in range(NG):
                            Te.matmul(pd[:, :],
                                      wslot[1][:, 4096 + r * 1024 + mt * 128: 4096 + r * 1024 + (mt + 1) * 128],
                                      hs[:, r * CH:(r + 1) * CH],
                                      start=(r == 0), stop=(r == NG - 1))
                    P.tensor(f)
                    da_done = cT.n

                    def f(V, tc=tc, mt=mt, da_done=da_done):
                        cT.wait(V, da_done)
                        pd = pool[6] if mt % 2 == 0 else pool[7]
                        a = acc[:, mt * TH + tc * CH: mt * TH + (tc + 1) * CH]
                        V.tensor_tensor(a, a, pd[:, :], ALU.add)
                    P.vector(f)
                    ev_done.append(cV.n)
            shared_done = cV.n

            sh_dump_sS = 0
            if debug and h == 0:
                def f(sync, shared_done=shared_done):
                    sync.wait_ge(cV.sem, shared_done)
                    for g in range(NG):
                        sS.inc(sync.dma_start(
                            sh_dbg.ap()[g * 128:(g + 1) * 128, :],
                            acc[:, g * TH:(g + 1) * TH]), 16)
                P.sync(f)
                sh_dump_sS = sS.n

            wfree[h * E + 0] = cT.n
            wfree[h * E + 1] = cT.n
            last_hc = 0
            for e in range(E):
                gi = h * E + e
                slot = gi % 2
                ws = wslot[slot]
                wsem = sWs[slot]

                sel_s = selslot[slot]

                def f(G, gi=gi, e=e, ws=ws, wsem=wsem, sel_s=sel_s):
                    if gi in wfree:
                        G.eng.wait_ge(cT.sem, wfree[gi])
                    wsem.inc(G.eng.dma_start(ws[:, :], moe_d.ap()[e]), 16)
                    wsem.inc(G.eng.dma_start(sel_s[:, :], sel_d.ap()[:, e * 128:(e + 1) * 128]), 16)
                P.gpsimd(f)
                w_ld = wsem.n

                especs = []
                for tc in range(NTCH):
                    def f(Te, e=e, tc=tc, w_ld=w_ld, wsem=wsem, shared_done=shared_done,
                          last_hc=last_hc, sel_s=sel_s):
                        if tc == 0:
                            Te.eng.wait_ge(wsem.sem, w_ld)
                            if e == 0:
                                cV.wait(Te, shared_done)
                        if last_hc:
                            Te.eng.wait_ge(cV.sem, last_hc)
                        Te.matmul(pool[2][:, :], sel_s[:, :],
                                  combT[:, tc * CH:(tc + 1) * CH], start=True, stop=True)
                    P.tensor(f)

                    for it in range(4):
                        k = len(silu_tgt)
                        pg = pool[4] if k % 2 == 0 else pool[0]
                        pu = pool[5] if k % 2 == 0 else pool[1]

                        def f(Te, tc=tc, it=it, ws=ws, k=k, pg=pg):
                            Te.eng.wait_ge(cA.sem, silu_tgt[k - 2])
                            for r in range(NG):
                                Te.matmul(pg[:, :],
                                          ws[:, r * 512 + it * 128: r * 512 + (it + 1) * 128],
                                          xnT[:, r * TH + tc * CH: r * TH + (tc + 1) * CH],
                                          start=(r == 0), stop=(r == NG - 1))
                        P.tensor(f)
                        gmm = cT.n
                        act_next += 1
                        silu_tgt.append(act_next)
                        especs.append((gmm, pg, k))

                        def f(Te, tc=tc, it=it, ws=ws, k=k, pu=pu):
                            Te.eng.wait_ge(cV.sem, hmul_tgt[k - 2])
                            for r in range(NG):
                                Te.matmul(
                                    pu[:, :],
                                    ws[:, 4096 + r * 512 + it * 128: 4096 + r * 512 + (it + 1) * 128],
                                    xnT[:, r * TH + tc * CH: r * TH + (tc + 1) * CH],
                                    start=(r == 0), stop=(r == NG - 1))
                        P.tensor(f)
                        umm = cT.n

                        def f(V, it=it, k=k, umm=umm, pu=pu):
                            cT.wait(V, umm)
                            cA.wait(V, silu_tgt[k])
                            V.tensor_tensor(h1[:, :], silu_sb[k % 2][:, :], pu[:, :], ALU.mult)
                            V.tensor_tensor(hc[:, it * CH:(it + 1) * CH], h1[:, :],
                                            pool[2][:, :], ALU.mult)
                        P.vector(f)
                        hmul_tgt.append(cV.n)
                    last_hc = hc_done = cV.n

                    ev_done = []
                    for mt in range(NG):
                        def f(Te, tc=tc, mt=mt, ws=ws, hc_done=hc_done):
                            if mt == 0:
                                cV.wait(Te, hc_done)
                            if mt >= 2:
                                cV.wait(Te, ev_done[mt - 2])
                            pd = pool[6] if mt % 2 == 0 else pool[7]
                            for r in range(4):
                                Te.matmul(
                                    pd[:, :],
                                    ws[:, 8192 + r * 1024 + mt * 128: 8192 + r * 1024 + (mt + 1) * 128],
                                    hc[:, r * CH:(r + 1) * CH],
                                    start=(r == 0), stop=(r == 3))
                        P.tensor(f)
                        dmm = cT.n

                        def f(V, tc=tc, mt=mt, dmm=dmm, e=e, sh_dump_sS=sh_dump_sS):
                            if e == 0 and tc == 0 and mt == 0 and sh_dump_sS:
                                V.eng.wait_ge(sS.sem, sh_dump_sS)
                            cT.wait(V, dmm)
                            pd = pool[6] if mt % 2 == 0 else pool[7]
                            a = acc[:, mt * TH + tc * CH: mt * TH + (tc + 1) * CH]
                            V.tensor_tensor(a, a, pd[:, :], ALU.add)
                        P.vector(f)
                        ev_done.append(cV.n)

                def f(A, especs=especs):
                    for gmm, pg, k in especs:
                        A.eng.wait_ge(cT.sem, gmm)
                        A.eng.wait_ge(cV.sem, hmul_tgt[k - 2])
                        A.activation(out=silu_sb[k % 2][:, :], in_=pg[:, :],
                                     func=AF.Silu)
                P.scalar_f(f)
                assert cA.n == silu_tgt[-1], (cA.n, silu_tgt[-1])
                wfree[gi + 2] = cT.n

            experts_done = cV.n

            def f(sync, h=h, experts_done=experts_done):
                sync.wait_ge(cV.sem, experts_done)
                for g in range(NG):
                    sS.inc(sync.dma_start(
                        x2T_d.ap()[g * 128:(g + 1) * 128, h * TH:(h + 1) * TH],
                        acc[:, g * TH:(g + 1) * TH]), 16)
            P.sync(f)

        def f(sync):
            sync.wait_ge(sS.sem, sS.n)
        P.sync(f)
        pctx.close()




def build_B(T, debug=False):
    nc = bass.Bass("TRN2", target_bir_lowering=False)
    NHALF = 2
    TH = T // NHALF
    NTCH = TH // CH
    E = 64

    x1T_d = nc.dram_tensor("x1T", [D, T], F32, kind="ExternalInput")
    moe_d = nc.dram_tensor("moe_img", [E, 128, 12288], F32, kind="ExternalInput")
    sg_d = nc.dram_tensor("sg_img", [128, 8192], F32, kind="ExternalInput")
    su_d = nc.dram_tensor("su_img", [128, 8192], F32, kind="ExternalInput")
    sd_d = nc.dram_tensor("sd_img", [128, 8192], F32, kind="ExternalInput")
    gate_d = nc.dram_tensor("gate_img", [128, 512], F32, kind="ExternalInput")
    gbias_d = nc.dram_tensor("gate_bias", [64, 1], F32, kind="ExternalInput")
    sel_d = nc.dram_tensor("sel_img", [64, 8192], BF16, kind="ExternalInput")
    eyeF_d = nc.dram_tensor("eyeF", [128, 128], F32, kind="ExternalInput")
    x2T_d = nc.dram_tensor("x2T", [D, T], F32, kind="ExternalOutput")
    if debug:
        xn_dbg = nc.dram_tensor("xn_dbg", [128, NG * TH], BF16, kind="ExternalOutput")
        comb_dbg = nc.dram_tensor("comb_dbg", [64, T], BF16, kind="ExternalOutput")
        sh_dbg = nc.dram_tensor("sh_dbg", [D, TH], F32, kind="ExternalOutput")

    d = {"x1T": x1T_d, "moe_img": moe_d, "sg_img": sg_d, "su_img": su_d, "sd_img": sd_d, "gate_img": gate_d, "gate_bias": gbias_d, "sel_img": sel_d, "eyeF": eyeF_d, "x2T": x2T_d}
    if debug:
        d.update({"xn_dbg": xn_dbg, "comb_dbg": comb_dbg, "sh_dbg": sh_dbg})
    with ExitStack() as semctx:
        emit_B(nc, semctx, d, T, debug=debug)
    return nc


def emit_C(nc, semctx, d, T, debug=False):
    NCH = T // CH
    NW = 2
    x2T_d = d["x2T"]
    wk_d = d["wk_img"]
    wv_d = d["wv_img"]
    wr_d = d["wr_img"]
    wo_d = d["wo_img"]
    fwr_d = d["fwr_img"]
    fwk_d = d["fwk_img"]
    fwv_d = d["fwv_img"]
    vecs_d = d["vimgs"]
    out_d = d["outn"]
    k_dbg = d.get("k_dbg")
    y_dbg = d.get("y_dbg")
    xh_dbg = d.get("xh_dbg")
    x3_dbg = d.get("x3_dbg")
    with ExitStack() as ctx:
        sb = lambda name, shape, dt: ctx.enter_context(nc.sbuf_tensor(name + "_sb", shape, dt))

        GW = NG * CH
        Rb = sb("Rb", [128, GW], F32)
        K16 = sb("K16", [128, GW], F32)
        x2b = K16
        XH = sb("XH", [128, GW], F32)
        MIXB = [sb(f"mixb{j}", [128, GW], BF16) for j in range(3)]
        SR8 = sb("SR8", [128, GW], BF16)
        G8 = sb("G8", [128, GW], BF16)
        KK = sb("KK", [128, 32 * CH], BF16)
        V16 = KK[:, 0:8192].bitcast(F32)
        Lb = KK[:, 8192:16384].bitcast(F32)
        SU = sb("SU", [128, GW], U8)
        s1f = sb("s1f", [128, CH], F32)
        decays = sb("decays", [128, CH], F32)
        ones512 = sb("ones512", [128, CH], F32)
        zeros512 = sb("zeros512", [128, CH], F32)
        zeros8 = sb("zeros8", [128, 8], F32)
        sqscr = sb("sqscr", [128, CH], F32)
        scr = {nm: sb(nm, [128, CH], F32)
               for nm in ("ek", "ekB", "ekv", "Pp", "Qq", "cum", "Psh", "Qsh", "numv")}
        rowst = sb("rowstC", [1, 2 * CH], F32)
        vecs = sb("vecs", [128, 104], F32)
        edec = sb("edec", [128, 8], F32)
        Pc = sb("Pc", [128, 8], F32)
        Qc = sb("Qc", [128, 8], F32)
        v1s = sb("v1s", [128, 8], F32)
        v2s = sb("v2s", [128, 8], F32)
        tail1 = sb("tail1", [128, 8], F32)
        tail2 = sb("tail2", [128, 8], F32)
        onescol_f = sb("onescol_fc", [128, 1], F32)
        ones1x128 = sb("ones1x128c", [1, 128], F32)
        eyeF = sb("eyeFc", [128, 128], F32)
        wring = [sb(f"wring{j}", [128, 8192], BF16) for j in range(NW)]
        outnat = [sb(f"outnat{j}", [128, D], F32) for j in range(2)]

        P = Phase(nc, semctx, "C_", blk_ctx=ctx)
        sL, sS, cT, cV, cA = P.sL, P.sS, P.cT, P.cV, P.cA
        sG = P.sG
        sWr = [P.ct(f"sWr{j}") for j in range(NW)]
        sLc = [P.ct("sLc0"), P.ct("sLc1")]
        sSo = [P.ct("sSo0"), P.ct("sSo1")]

        pctx = ExitStack()
        pool = [pctx.enter_context(nc.psum_tensor(f"pc{j}_ps", [128, CH], F32))
                for j in range(8)]
        psSUM = pool[0][0:1, :]
        psSQ = pool[1][0:1, :]
        psM = pool[2][:, :]
        psRS = pool[3][:, :]

        ring_state = {"i": 0}
        ring_free_at = {}

        def load_piece(src_ap):
            i = ring_state["i"]
            slot = i % NW
            ring_state["i"] = i + 1

            def f(G, i=i, slot=slot, src_ap=src_ap):
                if i >= NW:
                    G.eng.wait_ge(cT.sem, ring_free_at[i - NW])
                sWr[slot].inc(G.eng.dma_start(wring[slot][:, :], src_ap), 16)
            P.gpsimd(f)
            return slot, sWr[slot].n, i

        def piece_done(i):
            ring_free_at[i] = cT.n

        def f(sync):
            for j in range(13):
                sL.inc(sync.dma_start(vecs[:, j * 8:(j + 1) * 8], vecs_d.ap()[j]), 16)
            sL.inc(sync.dma_start(eyeF[:, :], eyeF_dC.ap()), 16)
        eyeF_dC = nc.dram_tensor("eyeFc_in", [128, 128], F32, kind="ExternalInput")
        P.sync(f)
        init_ld = sL.n

        def f(V):
            V.memset(ones512[:, :], 1.0)
            V.memset(zeros512[:, :], 0.0)
            V.memset(zeros8[:, :], 0.0)
            V.memset(Pc[:, :], 0.0)
            V.memset(Qc[:, :], 0.0)
            V.memset(v1s[:, :], 0.0)
            V.memset(v2s[:, :], 0.0)
            V.memset(tail1[:, :], 0.0)
            V.memset(tail2[:, :], 0.0)
            V.memset(onescol_f[:, :], 1.0)
            V.memset(ones1x128[:, :], 1.0)
            V.eng.wait_ge(sL.sem, init_ld)
            V.tensor_scalar_add(vecs[:, 32:72], vecs[:, 32:72], -1.0)
        P.vector(f)
        memset_done = cV.n

        def f(A):
            A.eng.wait_ge(sL.sem, init_ld)
            cV.wait(A, memset_done)
            A.activation(out=edec[:, :], in_=vecs[:, 72:80], func=AF.Exp)
            A.activation(out=edec[:, :], in_=edec[:, :], func=AF.Exp, scale=-1.0)
        P.scalar(f)
        dc_done = cA.n

        decays_done = dc_done

        def emit_lnstats(srcb, sq_wait_extra=None):
            cv_src = cV.n
            sq_tgt = []
            mm_tgt = []
            an = cA.n
            for g in range(NG):
                an += 1
                sq_tgt.append(an)

                def f(Te, g=g):
                    cA.wait(Te, sq_tgt[g])
                    if g == 0:
                        cV.wait(Te)
                    Te.matmul(psSUM, onescol_f[:, :], srcb[:, g * CH:(g + 1) * CH],
                              start=(g == 0), stop=(g == NG - 1))
                    Te.matmul(psSQ, onescol_f[:, :], sqscr[:, :],
                              start=(g == 0), stop=(g == NG - 1))
                P.tensor(f)
                mm_tgt.append(cT.n)

            def f(A, cv_src=cv_src, mm_tgt=mm_tgt):
                for g in range(NG):
                    if g == 0:
                        cV.wait(A, cv_src)
                    else:
                        A.eng.wait_ge(cT.sem, mm_tgt[g - 1])
                    A.activation(out=sqscr[:, :], in_=srcb[:, g * CH:(g + 1) * CH],
                                 func=AF.Square)
            P.scalar_f(f)
            assert cA.n == sq_tgt[-1], (cA.n, sq_tgt[-1])
            sums_done = cT.n

            def f(V, sums_done=sums_done):
                cT.wait(V, sums_done)
                V.tensor_scalar_mul(rowst[:, 0:CH], psSUM, 1.0 / D)
                V.tensor_mul(rowst[:, CH:2 * CH], rowst[:, 0:CH], rowst[:, 0:CH])
                V.scalar_tensor_tensor(rowst[:, CH:2 * CH], psSQ, 1.0 / D,
                                       rowst[:, CH:2 * CH], ALU.mult, ALU.subtract)
                V.tensor_scalar_add(rowst[:, CH:2 * CH], rowst[:, CH:2 * CH], 1e-5)
            P.vector(f)
            vd = cV.n

            def f(A, vd=vd):
                cV.wait(A, vd)
                A.activation(out=rowst[:, CH:2 * CH], in_=rowst[:, CH:2 * CH], func=AF.Sqrt)
            P.scalar(f)
            sqd2 = cA.n

            def f(V, sqd2=sqd2):
                cA.wait(V, sqd2)
                V.reciprocal(rowst[:, CH:2 * CH], rowst[:, CH:2 * CH])
            P.vector(f)
            rd = cV.n

            def f(Te, rd=rd):
                cV.wait(Te, rd)
                Te.matmul(psM, ones1x128[:, :], rowst[:, 0:CH], start=True, stop=True)
                Te.matmul(psRS, ones1x128[:, :], rowst[:, CH:2 * CH], start=True, stop=True)
            P.tensor(f)
            return cT.n

        def mm8(dst_psum, lhs_slices, rhs_buf, extra_wait=None):
            def f(Te):
                if extra_wait is not None:
                    extra_wait(Te)
                for r in range(NG):
                    Te.matmul(dst_psum, lhs_slices[r], rhs_buf[:, r * CH:(r + 1) * CH],
                              start=(r == 0), stop=(r == NG - 1))
            P.tensor(f)
            return cT.n

        dumpstate = {"x3": 0}
        for tc in range(NCH):
            lsem = sLc[tc % 2]

            def f(sync, tc=tc, lsem=lsem):
                if tc >= 1:
                    sync.wait_ge(cV.sem, x2_used[tc - 1])
                    sync.wait_ge(cT.sem, trans_done[tc - 1])
                for g in range(NG):
                    lsem.inc(sync.dma_start(
                        x2b[:, g * CH:(g + 1) * CH],
                        x2T_d.ap()[g * 128:(g + 1) * 128, tc * CH:(tc + 1) * CH]), 16)
            if tc == 0:
                x2_used = {}
                trans_done = {}
            P.sync(f)
            x2_ld = lsem.n

            def f(Te, x2_ld=x2_ld, lsem=lsem, tc=tc):
                Te.eng.wait_ge(lsem.sem, x2_ld)
                if tc == 0:
                    cA.wait(Te, decays_done)
                    Te.eng.wait_ge(sL.sem, init_ld)
            P.tensor(f)

            def f(A, x2_ld=x2_ld, lsem=lsem):
                A.eng.wait_ge(lsem.sem, x2_ld)
            P.scalar(f)

            def f(V, x2_ld=x2_ld, lsem=lsem):
                V.eng.wait_ge(lsem.sem, x2_ld)
            P.vector(f)

            bc = emit_lnstats(x2b)

            def f(V, bc=bc):
                if dumpstate["x3"]:
                    V.eng.wait_ge(sS.sem, dumpstate["x3"])
                cT.wait(V, bc)
                for g in range(NG):
                    V.tensor_tensor(sqscr[:, :], x2b[:, g * CH:(g + 1) * CH], psM,
                                    ALU.subtract)
                    V.tensor_tensor(Rb[:, g * CH:(g + 1) * CH], sqscr[:, :], psRS,
                                    ALU.mult)
            P.vector(f)
            x2_used[tc] = cV.n
            w_done = cV.n

            bc = emit_lnstats(Rb)

            def f(V, bc=bc):
                cT.wait(V, bc)
                for g in range(NG):
                    V.tensor_tensor(sqscr[:, :], Rb[:, g * CH:(g + 1) * CH], psM,
                                    ALU.subtract)
                    V.tensor_tensor(sqscr[:, :], sqscr[:, :], psRS, ALU.mult)
                    V.tensor_scalar(Lb[:, g * CH:(g + 1) * CH], sqscr[:, :],
                                    vecs[:, 0 + g:1 + g], vecs[:, 8 + g:9 + g],
                                    ALU.mult, ALU.add)
            P.vector(f)
            ln1_done = cV.n

            def f(V, ln1_done=ln1_done):
                for g in range(NG):
                    L = Lb[:, g * CH:(g + 1) * CH]
                    dd = sqscr[:, :]
                    V.tensor_sub(dd[:, 1:CH], L[:, 1:CH], L[:, 0:CH - 1])
                    V.tensor_sub(dd[:, 0:1], L[:, 0:1], tail1[:, g:g + 1])
                    for mi, col in ((0, 32), (1, 40), (2, 48)):
                        V.scalar_tensor_tensor(MIXB[mi][:, g * CH:(g + 1) * CH],
                                               dd, vecs[:, col + g:col + g + 1], L,
                                               ALU.mult, ALU.add)
                    V.tensor_copy(tail1[:, g:g + 1], L[:, CH - 1:CH])
            P.vector(f)
            mix1_done = cV.n

            kvr_dst = (K16, V16, None)
            for wi, (wd, dstb) in enumerate(((wk_d, K16), (wv_d, V16), (wr_d, None))):
                for half in range(1):
                    pass
                slot, wn, pidx = load_piece(wd.ap())
                ev_done = []
                for mt in range(NG):
                    def f(Te, slot=slot, wn=wn, mt=mt, mix1_done=mix1_done, wi=wi):
                        Te.eng.wait_ge(sWr[slot].sem, wn)
                        if mt == 0:
                            cV.wait(Te, mix1_done)
                            cV.wait(Te)
                            cA.wait(Te)
                        if mt >= 2:
                            ct_ev, val = ev_done[mt - 2]
                            ct_ev.wait(Te, val)
                        pd = pool[4] if mt % 2 == 0 else pool[5]
                        for r in range(NG):
                            Te.matmul(pd[:, :],
                                      wring[slot][:, r * 1024 + mt * 128: r * 1024 + (mt + 1) * 128],
                                      MIXB[wi][:, r * CH:(r + 1) * CH],
                                      start=(r == 0), stop=(r == NG - 1))
                    P.tensor(f)
                    mmd = cT.n

                    if dstb is not None:
                        def f(V, mt=mt, mmd=mmd, dstb=dstb):
                            cT.wait(V, mmd)
                            pd = pool[4] if mt % 2 == 0 else pool[5]
                            V.tensor_copy(dstb[:, mt * CH:(mt + 1) * CH], pd[:, :])
                        P.vector(f)
                        ev_done.append((cV, cV.n))
                    else:
                        def f(A, mt=mt, mmd=mmd):
                            cT.wait(A, mmd)
                            pd = pool[4] if mt % 2 == 0 else pool[5]
                            A.activation(out=SR8[:, mt * CH:(mt + 1) * CH], in_=pd[:, :],
                                         func=AF.Sigmoid)
                        P.scalar(f)
                        ev_done.append((cA, cA.n))
                piece_done(pidx)
            kvr_done_v = cV.n
            kvr_done_a = cA.n
            slot_o, wn_o, pidx_o = load_piece(wo_d.ap())

            for g in range(NG):
                def f(A, g=g, kvr_done_v=kvr_done_v):
                    if g == 0:
                        cV.wait(A, kvr_done_v)
                    else:
                        cV.wait(A)
                    A.activation(out=scr["ek"][:, :], in_=K16[:, g * CH:(g + 1) * CH],
                                 func=AF.Exp)
                    A.activation(out=scr["ekB"][:, :], in_=K16[:, g * CH:(g + 1) * CH],
                                 func=AF.Exp, bias=vecs[:, 80 + g:81 + g])
                P.scalar(f)
                ek_done = cA.n

                def f(V, g=g, ek_done=ek_done):
                    cA.wait(V, ek_done)
                    V.tensor_single_scalar(decays[:, :], ones512[:, :],
                                           edec[:, g:g + 1], ALU.mult)
                    V.tensor_mul(scr["ekv"][:, :], scr["ek"][:, :],
                                 V16[:, g * CH:(g + 1) * CH])
                    V.eng.wait_ge(cV.sem, cV.n)
                    V.tensor_tensor_scan(scr["Pp"][:, :], decays[:, :],
                                         scr["ekv"][:, :], 0.0, ALU.mult, ALU.add)
                    V.tensor_tensor_scan(scr["Qq"][:, :], decays[:, :],
                                         scr["ek"][:, :], 0.0, ALU.mult, ALU.add)
                    V.tensor_tensor_scan(scr["cum"][:, :], decays[:, :],
                                         zeros512[:, :], 1.0, ALU.mult, ALU.add)
                    V.scalar_tensor_tensor(scr["Pp"][:, :], scr["cum"][:, :],
                                           Pc[:, g:g + 1], scr["Pp"][:, :],
                                           ALU.mult, ALU.add)
                    V.scalar_tensor_tensor(scr["Qq"][:, :], scr["cum"][:, :],
                                           Qc[:, g:g + 1], scr["Qq"][:, :],
                                           ALU.mult, ALU.add)
                    V.tensor_copy(scr["Psh"][:, 1:CH], scr["Pp"][:, 0:CH - 1])
                    V.tensor_copy(scr["Psh"][:, 0:1], Pc[:, g:g + 1])
                    V.tensor_copy(scr["Qsh"][:, 1:CH], scr["Qq"][:, 0:CH - 1])
                    V.tensor_copy(scr["Qsh"][:, 0:1], Qc[:, g:g + 1])
                    V.tensor_copy(Pc[:, g:g + 1], scr["Pp"][:, CH - 1:CH])
                    V.tensor_copy(Qc[:, g:g + 1], scr["Qq"][:, CH - 1:CH])
                    V.tensor_mul(scr["numv"][:, :], scr["ekB"][:, :],
                                 V16[:, g * CH:(g + 1) * CH])
                    V.tensor_add(scr["numv"][:, :], scr["numv"][:, :], scr["Psh"][:, :])
                    V.tensor_add(scr["Qsh"][:, :], scr["ekB"][:, :], scr["Qsh"][:, :])
                    V.reciprocal(scr["Qsh"][:, :], scr["Qsh"][:, :])
                    V.tensor_mul(scr["numv"][:, :], scr["numv"][:, :], scr["Qsh"][:, :])
                    V.tensor_mul(G8[:, g * CH:(g + 1) * CH], scr["numv"][:, :],
                                 SR8[:, g * CH:(g + 1) * CH])
                P.vector(f)
            wkv_done = cV.n

            if debug and tc == 0:
                def f(sync, wkv_done=wkv_done):
                    sync.wait_ge(cV.sem, wkv_done)
                    sS.inc(sync.dma_start(k_dbg.ap(), K16[:, :]), 16)
                    sS.inc(sync.dma_start(y_dbg.ap(), G8[:, :]), 16)
                P.sync(f)

            ev_done = []
            for mt in range(NG):
                def f(Te, mt=mt, slot_o=slot_o, wn_o=wn_o, wkv_done=wkv_done):
                    Te.eng.wait_ge(sWr[slot_o].sem, wn_o)
                    if mt == 0:
                        cV.wait(Te, wkv_done)
                    if mt >= 2:
                        cA.wait(Te, ev_done[mt - 2])
                    pd = pool[4] if mt % 2 == 0 else pool[5]
                    for r in range(NG):
                        Te.matmul(pd[:, :],
                                  wring[slot_o][:, r * 1024 + mt * 128: r * 1024 + (mt + 1) * 128],
                                  G8[:, r * CH:(r + 1) * CH],
                                  start=(r == 0), stop=(r == NG - 1))
                P.tensor(f)
                mmd = cT.n

                def f(A, mt=mt, mmd=mmd):
                    cT.wait(A, mmd)
                    pd = pool[4] if mt % 2 == 0 else pool[5]
                    A.activation(out=XH[:, mt * CH:(mt + 1) * CH], in_=pd[:, :],
                                 func=AF.Identity, scale=0.5)
                P.scalar(f)
                ev_done.append(cA.n)
            piece_done(pidx_o)
            xh_done = cA.n

            xh_dump_sS = 0
            if debug and tc == 0:
                def f(sync, xh_done=xh_done):
                    sync.wait_ge(cA.sem, xh_done)
                    sS.inc(sync.dma_start(xh_dbg.ap(), XH[:, :]), 16)
                P.sync(f)
                xh_dump_sS = sS.n

            xhv = XH[:, :].rearrange("p (g t) -> p g t", g=NG)

            def f(V, xh_done=xh_done, xh_dump_sS=xh_dump_sS):
                cA.wait(V, xh_done)
                if xh_dump_sS:
                    V.eng.wait_ge(sS.sem, xh_dump_sS)
                for t in range(CH):
                    V.scalar_tensor_tensor(xhv[:, :, t], v1s[:, :], 0.5, xhv[:, :, t],
                                           ALU.mult, ALU.add)
                    V.scalar_tensor_tensor(v1s[:, :], xhv[:, :, t], 1.0, xhv[:, :, t],
                                           ALU.is_lt, ALU.mult)
            P.vector(f)

            def f(V):
                for g in range(NG):
                    V.tensor_single_scalar(s1f[:, :], XH[:, g * CH:(g + 1) * CH],
                                           1.0, ALU.is_ge)
                    V.tensor_add(Rb[:, g * CH:(g + 1) * CH],
                                 Rb[:, g * CH:(g + 1) * CH], s1f[:, :])
            P.vector(f)
            xmid_done = cV.n

            bc = emit_lnstats(Rb)

            def f(V, bc=bc):
                cT.wait(V, bc)
                for g in range(NG):
                    V.tensor_tensor(sqscr[:, :], Rb[:, g * CH:(g + 1) * CH], psM,
                                    ALU.subtract)
                    V.tensor_tensor(sqscr[:, :], sqscr[:, :], psRS, ALU.mult)
                    V.tensor_scalar(Lb[:, g * CH:(g + 1) * CH], sqscr[:, :],
                                    vecs[:, 16 + g:17 + g], vecs[:, 24 + g:25 + g],
                                    ALU.mult, ALU.add)
            P.vector(f)

            def f(V):
                for g in range(NG):
                    L = Lb[:, g * CH:(g + 1) * CH]
                    dd = sqscr[:, :]
                    V.tensor_sub(dd[:, 1:CH], L[:, 1:CH], L[:, 0:CH - 1])
                    V.tensor_sub(dd[:, 0:1], L[:, 0:1], tail2[:, g:g + 1])
                    for mi, col in ((0, 56), (1, 64)):
                        V.scalar_tensor_tensor(MIXB[mi][:, g * CH:(g + 1) * CH],
                                               dd, vecs[:, col + g:col + g + 1], L,
                                               ALU.mult, ALU.add)
                    V.tensor_copy(tail2[:, g:g + 1], L[:, CH - 1:CH])
            P.vector(f)
            mix2_done = cV.n

            ev_done = []
            for pc in range(4):
                slot, wn, pidx = load_piece(fwk_d.ap()[:, pc * 8192:(pc + 1) * 8192])
                for mo in range(8):
                    mt = pc * 8 + mo

                    def f(Te, slot=slot, wn=wn, mo=mo, mt=mt, mix2_done=mix2_done):
                        Te.eng.wait_ge(sWr[slot].sem, wn)
                        if mt == 0:
                            cV.wait(Te, mix2_done)
                            cA.wait(Te)
                        if mt >= 2:
                            cA.wait(Te, ev_done[mt - 2])
                        pd = pool[4] if mt % 2 == 0 else pool[5]
                        for r in range(NG):
                            Te.matmul(pd[:, :],
                                      wring[slot][:, mo * 1024 + r * 128: mo * 1024 + (r + 1) * 128],
                                      MIXB[0][:, r * CH:(r + 1) * CH],
                                      start=(r == 0), stop=(r == NG - 1))
                    P.tensor(f)
                    mmd = cT.n

                    def f(A, mt=mt, mmd=mmd):
                        cT.wait(A, mmd)
                        pd = pool[4] if mt % 2 == 0 else pool[5]
                        A.activation(out=sqscr[:, :], in_=pd[:, :], func=AF.Relu)
                        A.activation(out=KK[:, mt * CH:(mt + 1) * CH], in_=sqscr[:, :],
                                     func=AF.Square)
                    P.scalar(f)
                    ev_done.append(cA.n)
                piece_done(pidx)
            kk_done = cA.n

            slot_r, wn_r, pidx_r = load_piece(fwr_d.ap())
            ev_done = []
            for mt in range(NG):
                def f(Te, slot_r=slot_r, wn_r=wn_r, mt=mt, kk_done=kk_done):
                    Te.eng.wait_ge(sWr[slot_r].sem, wn_r)
                    if mt == 0:
                        cA.wait(Te, kk_done)
                    if mt >= 2:
                        cA.wait(Te, ev_done[mt - 2])
                    pd = pool[4] if mt % 2 == 0 else pool[5]
                    for r in range(NG):
                        Te.matmul(pd[:, :],
                                  wring[slot_r][:, r * 1024 + mt * 128: r * 1024 + (mt + 1) * 128],
                                  MIXB[1][:, r * CH:(r + 1) * CH],
                                  start=(r == 0), stop=(r == NG - 1))
                P.tensor(f)
                mmd = cT.n

                def f(A, mt=mt, mmd=mmd):
                    cT.wait(A, mmd)
                    pd = pool[4] if mt % 2 == 0 else pool[5]
                    A.activation(out=SR8[:, mt * CH:(mt + 1) * CH], in_=pd[:, :],
                                 func=AF.Sigmoid)
                P.scalar(f)
                ev_done.append(cA.n)
            piece_done(pidx_r)
            csig_done = cA.n

            ev_done = []
            for pc in range(4):
                slot, wn, pidx = load_piece(fwv_d.ap()[:, pc * 8192:(pc + 1) * 8192])
                for mo in range(2):
                    mt = pc * 2 + mo

                    def f(Te, slot=slot, wn=wn, mo=mo, mt=mt, csig_done=csig_done):
                        Te.eng.wait_ge(sWr[slot].sem, wn)
                        if mt == 0:
                            cA.wait(Te, csig_done)
                        if mt >= 2:
                            cV.wait(Te, ev_done[mt - 2])
                        pd = pool[4] if mt % 2 == 0 else pool[5]
                        for r in range(32):
                            Te.matmul(pd[:, :],
                                      wring[slot][:, mo * 4096 + r * 128: mo * 4096 + (r + 1) * 128],
                                      KK[:, r * CH:(r + 1) * CH],
                                      start=(r == 0), stop=(r == 31))
                    P.tensor(f)
                    mmd = cT.n

                    def f(V, mt=mt, mmd=mmd, xh_dump_sS=xh_dump_sS):
                        if mt == 0 and xh_dump_sS:
                            V.eng.wait_ge(sS.sem, xh_dump_sS)
                        cT.wait(V, mmd)
                        pd = pool[4] if mt % 2 == 0 else pool[5]
                        V.scalar_tensor_tensor(XH[:, mt * CH:(mt + 1) * CH], pd[:, :],
                                               0.5, SR8[:, mt * CH:(mt + 1) * CH],
                                               ALU.mult, ALU.mult)
                    P.vector(f)
                    ev_done.append(cV.n)
                piece_done(pidx)
            xh2_done = cV.n

            def f(V, xh2_done=xh2_done):
                cV.wait(V, xh2_done)
                for t in range(CH):
                    V.scalar_tensor_tensor(xhv[:, :, t], v2s[:, :], 0.5, xhv[:, :, t],
                                           ALU.mult, ALU.add)
                    V.scalar_tensor_tensor(v2s[:, :], xhv[:, :, t], 1.0, xhv[:, :, t],
                                           ALU.is_lt, ALU.mult)
                for g in range(NG):
                    V.tensor_single_scalar(s1f[:, :], XH[:, g * CH:(g + 1) * CH],
                                           1.0, ALU.is_ge)
                    V.tensor_add(Rb[:, g * CH:(g + 1) * CH],
                                 Rb[:, g * CH:(g + 1) * CH], s1f[:, :])
            P.vector(f)

            lif2_done = cV.n

            def f(sync, lif2_done=lif2_done, tc=tc, lsem=lsem):
                sync.wait_ge(cV.sem, lif2_done)
                for g in range(NG):
                    lsem.inc(sync.dma_start(
                        XH[:, g * CH:(g + 1) * CH],
                        x2T_d.ap()[g * 128:(g + 1) * 128, tc * CH:(tc + 1) * CH]), 16)
            P.sync(f)
            x2re_ld = lsem.n

            def f(V, x2re_ld=x2re_ld, lsem=lsem):
                V.eng.wait_ge(lsem.sem, x2re_ld)
                for g in range(NG):
                    V.tensor_add(Rb[:, g * CH:(g + 1) * CH],
                                 Rb[:, g * CH:(g + 1) * CH], XH[:, g * CH:(g + 1) * CH])
            P.vector(f)
            x3_done = cV.n

            if debug and tc == 0:
                def f(sync, x3_done=x3_done):
                    sync.wait_ge(cV.sem, x3_done)
                    sS.inc(sync.dma_start(x3_dbg.ap(), Rb[:, :]), 16)
                P.sync(f)
                dumpstate["x3"] = sS.n

            bc = emit_lnstats(Rb)

            def f(V, bc=bc):
                cT.wait(V, bc)
                for g in range(NG):
                    V.tensor_tensor(sqscr[:, :], Rb[:, g * CH:(g + 1) * CH], psM,
                                    ALU.subtract)
                    V.tensor_tensor(Lb[:, g * CH:(g + 1) * CH], sqscr[:, :], psRS,
                                    ALU.mult)
            P.vector(f)
            u_done = cV.n

            bc = emit_lnstats(Lb, sq_wait_extra=u_done)

            def f(V, bc=bc):
                cT.wait(V, bc)
                for g in range(NG):
                    V.tensor_tensor(sqscr[:, :], Lb[:, g * CH:(g + 1) * CH], psM,
                                    ALU.subtract)
                    V.tensor_tensor(sqscr[:, :], sqscr[:, :], psRS, ALU.mult)
                    V.tensor_scalar(sqscr[:, :], sqscr[:, :],
                                    vecs[:, 88 + g:89 + g], vecs[:, 96 + g:97 + g],
                                    ALU.mult, ALU.add)
                    V.tensor_add(K16[:, g * CH:(g + 1) * CH],
                                 Rb[:, g * CH:(g + 1) * CH], sqscr[:, :])
            P.vector(f)
            out_done = cV.n

            tpcp_trail = []
            for tt in range(CH // 128):
                ob = outnat[tt % 2]
                osem = sSo[tt % 2]
                for g in range(NG):
                    pd = pool[6] if g % 2 == 0 else pool[7]

                    def f(Te, tt=tt, g=g, pd=pd, out_done=out_done):
                        if tt == 0 and g == 0:
                            cV.wait(Te, out_done)
                        if len(tpcp_trail) >= 2:
                            Te.eng.wait_ge(cV.sem, tpcp_trail[-2])
                        Te.transpose(pd[:, 0:128],
                                     K16[:, g * CH + tt * 128: g * CH + (tt + 1) * 128],
                                     eyeF[:, :])
                    P.tensor(f)
                    tpd = cT.n

                    def f(V, g=g, pd=pd, tpd=tpd, ob=ob, osem=osem):
                        cT.wait(V, tpd)
                        if g == 0:
                            V.eng.wait_ge(osem.sem, osem.n)
                        V.tensor_copy(ob[:, g * 128:(g + 1) * 128], pd[:, 0:128])
                        tpcp_trail.append(cV.n)
                    P.vector(f)
                tcp_done = cV.n

                def f(sync, tt=tt, tc=tc, ob=ob, osem=osem, tcp_done=tcp_done):
                    sync.wait_ge(cV.sem, tcp_done)
                    osem.inc(sync.dma_start(
                        out_d.ap()[tc * CH + tt * 128: tc * CH + (tt + 1) * 128, :],
                        ob[:, :]), 16)
                P.sync(f)
            trans_done[tc] = cT.n

        def f(sync):
            for j in range(2):
                sync.wait_ge(sSo[j].sem, sSo[j].n)
            sync.wait_ge(sS.sem, sS.n)
        P.sync(f)
        pctx.close()




def build_C(T, debug=False):
    nc = bass.Bass("TRN2", target_bir_lowering=False)
    NCH = T // CH
    NW = 2

    x2T_d = nc.dram_tensor("x2T", [D, T], F32, kind="ExternalInput")
    wk_d = nc.dram_tensor("wk_img", [128, 8192], F32, kind="ExternalInput")
    wv_d = nc.dram_tensor("wv_img", [128, 8192], F32, kind="ExternalInput")
    wr_d = nc.dram_tensor("wr_img", [128, 8192], F32, kind="ExternalInput")
    wo_d = nc.dram_tensor("wo_img", [128, 8192], F32, kind="ExternalInput")
    fwr_d = nc.dram_tensor("fwr_img", [128, 8192], F32, kind="ExternalInput")
    fwk_d = nc.dram_tensor("fwk_img", [128, 32768], F32, kind="ExternalInput")
    fwv_d = nc.dram_tensor("fwv_img", [128, 32768], F32, kind="ExternalInput")
    vecs_d = nc.dram_tensor("vimgs", [13, 128, 8], F32, kind="ExternalInput")
    out_d = nc.dram_tensor("outn", [T, D], F32, kind="ExternalOutput")
    if debug:
        k_dbg = nc.dram_tensor("k_dbg", [128, NG * CH], F32, kind="ExternalOutput")
        y_dbg = nc.dram_tensor("y_dbg", [128, NG * CH], BF16, kind="ExternalOutput")
        xh_dbg = nc.dram_tensor("xh_dbg", [128, NG * CH], F32, kind="ExternalOutput")
        x3_dbg = nc.dram_tensor("x3_dbg", [128, NG * CH], F32, kind="ExternalOutput")

    d = {"x2T": x2T_d, "wk_img": wk_d, "wv_img": wv_d, "wr_img": wr_d, "wo_img": wo_d, "fwr_img": fwr_d, "fwk_img": fwk_d, "fwv_img": fwv_d, "vimgs": vecs_d, "outn": out_d}
    if debug:
        d.update({"k_dbg": k_dbg, "y_dbg": y_dbg, "xh_dbg": xh_dbg, "x3_dbg": x3_dbg})
    with ExitStack() as semctx:
        emit_C(nc, semctx, d, T, debug=debug)
    return nc


def _consts_A():
    eyeF = np.eye(128, dtype=np.float32)
    eyeB = np.eye(128).astype(BF)
    expand16 = np.kron(np.eye(16, dtype=np.float32), np.ones((1, 64), np.float32))
    trils = np.zeros((128, 4 * CH), np.float32)
    for j in range(4):
        p = np.arange(128)[:, None]
        q = np.arange(CH)[None, :]
        trils[:, j * CH:(j + 1) * CH] = (j * 128 + p <= q)
    return eyeF, eyeB, expand16, trils.astype(BF)


def _in_map_A(ins, x_full, b, T, consts):
    eyeF, eyeB, expand16, trils = consts
    wpdup = np.concatenate(
        [np.asarray(ins["L0_attn_W_pred"]).transpose(1, 0, 2).reshape(64, 1024)] * 2,
        axis=0).astype(np.float32)
    return {
        "x0": np.ascontiguousarray(x_full[b, :T], dtype=np.float32),
        "w_k": np.asarray(ins["L0_attn_W_K"], np.float32),
        "w_q": np.asarray(ins["L0_attn_W_Q"], np.float32),
        "w_v": np.asarray(ins["L0_attn_W_V"], np.float32),
        "wpdup": wpdup, "eyeF": eyeF, "eyeB": eyeB, "expand16": expand16,
        "trils": trils,
    }


def run_A(x_full, ins, T, n_cores=NCORES, debug=False):
    consts = _consts_A()
    nc = build_A(T, debug=debug)
    in_maps = [_in_map_A(ins, x_full, b, T, consts) for b in range(n_cores)]
    res = run_bass_kernel_spmd(nc, in_maps, core_ids=list(range(n_cores)))
    if debug:
        return res.results
    return np.stack([r["x1T"] for r in res.results])


def _prep_B(ins):
    eg = np.asarray(ins["L0_ffn_eg_w"], np.float32)
    eu = np.asarray(ins["L0_ffn_eu_w"], np.float32)
    ed = np.asarray(ins["L0_ffn_ed_w"], np.float32)
    E = eg.shape[0]
    moe = np.empty((E, 128, 12288), np.float32)
    for e in range(E):
        moe[e, :, 0:4096] = _img(eg[e].T)
        moe[e, :, 4096:8192] = _img(eu[e].T)
        moe[e, :, 8192:12288] = _img(ed[e].T)
    sg = _img(np.asarray(ins["L0_ffn_sg_w"], np.float32).T)
    su = _img(np.asarray(ins["L0_ffn_su_w"], np.float32).T)
    sd = _img(np.asarray(ins["L0_ffn_sd_w"], np.float32).T)
    gate = _img(np.asarray(ins["L0_ffn_gate_w"], np.float32).T)
    gbias = np.ascontiguousarray(
        np.asarray(ins["L0_ffn_gate_bias"], np.float32).reshape(64, 1))
    sel = np.kron(np.eye(64), np.ones((1, 128))).astype(BF)
    return {"moe_img": moe, "sg_img": sg, "su_img": su, "sd_img": sd,
            "gate_img": gate, "gate_bias": gbias, "sel_img": sel,
            "eyeF": np.eye(128, dtype=np.float32)}


def run_B(x1T_all, ins, T, n_cores=NCORES, debug=False):
    base = _prep_B(ins)
    nc = build_B(T, debug=debug)
    in_maps = []
    for b in range(n_cores):
        m = dict(base)
        m["x1T"] = np.ascontiguousarray(x1T_all[b], dtype=np.float32)
        in_maps.append(m)
    res = run_bass_kernel_spmd(nc, in_maps, core_ids=list(range(n_cores)))
    if debug:
        return res.results
    return np.stack([r["x2T"] for r in res.results])


def _prep_C2(ins):
    g = lambda k: np.asarray(ins[k], np.float32)
    names = ["L1_attn_ln1_s", "L1_attn_ln1_b", "L1_attn_ln2_s", "L1_attn_ln2_b",
             "L1_attn_tmk", "L1_attn_tmv", "L1_attn_tmr", "L1_attn_f_tmk",
             "L1_attn_f_tmr", "L1_attn_w_decay", "L1_attn_u_first",
             "L1_ffn_ln_g", "L1_ffn_ln_b"]
    vimgs = np.stack([_vec_img(g(k)) for k in names])
    return {
        "wk_img": _img(g("L1_attn_Wk")), "wv_img": _img(g("L1_attn_Wv")),
        "wr_img": _img(g("L1_attn_Wr")), "wo_img": _img(g("L1_attn_Wo")),
        "fwr_img": _img(g("L1_attn_fWr")),
        "fwk_img": _img_mmajor(g("L1_attn_fWk")),
        "fwv_img": _img_mmajor(g("L1_attn_fWv")),
        "vimgs": vimgs,
        "eyeFc_in": np.eye(128, dtype=np.float32),
    }


def run_C(x2T_all, ins, T, n_cores=NCORES, debug=False):
    base = _prep_C2(ins)
    nc = build_C(T, debug=debug)
    in_maps = []
    for b in range(n_cores):
        m = dict(base)
        m["x2T"] = np.ascontiguousarray(x2T_all[b], dtype=np.float32)
        in_maps.append(m)
    res = run_bass_kernel_spmd(nc, in_maps, core_ids=list(range(n_cores)))
    if debug:
        return res.results
    return np.stack([r["outn"] for r in res.results])


def build_ALL(T):
    nc = bass.Bass("TRN2", target_bir_lowering=False)
    d = {}
    d["x0"] = nc.dram_tensor("x0", [T, D], F32, kind="ExternalInput")
    d["w_k"] = nc.dram_tensor("w_k", [16, 16], F32, kind="ExternalInput")
    d["w_q"] = nc.dram_tensor("w_q", [16, 16], F32, kind="ExternalInput")
    d["w_v"] = nc.dram_tensor("w_v", [16, 16], F32, kind="ExternalInput")
    d["wpdup"] = nc.dram_tensor("wpdup", [128, 1024], F32, kind="ExternalInput")
    d["eyeF"] = nc.dram_tensor("eyeF", [128, 128], F32, kind="ExternalInput")
    d["eyeB"] = nc.dram_tensor("eyeB", [128, 128], BF16, kind="ExternalInput")
    d["expand16"] = nc.dram_tensor("expand16", [16, 1024], F32, kind="ExternalInput")
    d["trils"] = nc.dram_tensor("trils", [128, 4 * CH], BF16, kind="ExternalInput")
    d["x1T"] = nc.dram_tensor("x1T", [D, T], F32, kind="Internal")
    d["zT_scratch"] = nc.dram_tensor("zT_scratch", [D, T], BF16, kind="Internal")
    d["moe_img"] = nc.dram_tensor("moe_img", [64, 128, 12288], F32, kind="ExternalInput")
    d["sg_img"] = nc.dram_tensor("sg_img", [128, 8192], F32, kind="ExternalInput")
    d["su_img"] = nc.dram_tensor("su_img", [128, 8192], F32, kind="ExternalInput")
    d["sd_img"] = nc.dram_tensor("sd_img", [128, 8192], F32, kind="ExternalInput")
    d["gate_img"] = nc.dram_tensor("gate_img", [128, 512], F32, kind="ExternalInput")
    d["gate_bias"] = nc.dram_tensor("gate_bias", [64, 1], F32, kind="ExternalInput")
    d["sel_img"] = nc.dram_tensor("sel_img", [64, 8192], BF16, kind="ExternalInput")
    d["x2T"] = nc.dram_tensor("x2T", [D, T], F32, kind="Internal")
    d["wk_img"] = nc.dram_tensor("wk_img", [128, 8192], F32, kind="ExternalInput")
    d["wv_img"] = nc.dram_tensor("wv_img", [128, 8192], F32, kind="ExternalInput")
    d["wr_img"] = nc.dram_tensor("wr_img", [128, 8192], F32, kind="ExternalInput")
    d["wo_img"] = nc.dram_tensor("wo_img", [128, 8192], F32, kind="ExternalInput")
    d["fwr_img"] = nc.dram_tensor("fwr_img", [128, 8192], F32, kind="ExternalInput")
    d["fwk_img"] = nc.dram_tensor("fwk_img", [128, 32768], F32, kind="ExternalInput")
    d["fwv_img"] = nc.dram_tensor("fwv_img", [128, 32768], F32, kind="ExternalInput")
    d["vimgs"] = nc.dram_tensor("vimgs", [13, 128, 8], F32, kind="ExternalInput")
    d["outn"] = nc.dram_tensor("outn", [T, D], F32, kind="ExternalOutput")
    with ExitStack() as semctx:
        emit_A(nc, semctx, d, T)
        emit_B(nc, semctx, d, T)
        emit_C(nc, semctx, d, T)
    return nc


def kernel(**ins):
    x0 = np.asarray(ins["L0_attn_x"], dtype=np.float32)
    T = x0.shape[1]
    nc = build_ALL(T)
    consts = _consts_A()
    baseB = _prep_B(ins)
    baseC = _prep_C2(ins)
    del baseC["eyeFc_in"]
    in_maps = []
    for b in range(NCORES):
        m = dict(baseB)
        m.update(baseC)
        m.update(_in_map_A(ins, x0, b, T, consts))
        m["eyeFc_in"] = consts[0]
        in_maps.append(m)
    res = run_bass_kernel_spmd(nc, in_maps, core_ids=list(range(NCORES)))
    return np.stack([r["outn"] for r in res.results])


# version 06
# speedup vs baseline: 1.7574x; 1.0016x over previous
import numpy as np
import ml_dtypes
from contextlib import ExitStack

import concourse.bass as bass
import concourse.mybir as mybir
from concourse.bass_utils import run_bass_kernel_spmd

F32 = mybir.dt.float32
BF16 = mybir.dt.bfloat16
U8 = mybir.dt.uint8
AF = mybir.ActivationFunctionType
ALU = mybir.AluOpType
AX = mybir.AxisListType

NCORES = 8
D = 1024
NG = 8
CH = 512
BF = ml_dtypes.bfloat16

_SER_OPS = {
    "tensor_copy", "tensor_tensor", "tensor_add", "tensor_sub", "tensor_mul",
    "tensor_max", "tensor_scalar", "tensor_single_scalar", "tensor_scalar_add",
    "tensor_scalar_sub", "tensor_scalar_mul", "tensor_scalar_max", "tensor_scalar_min",
    "tensor_relu", "tensor_reduce", "reduce_max", "reduce_sum", "scalar_tensor_tensor",
    "tensor_tensor_scan", "memset", "memzero", "reciprocal", "select", "copy_predicated",
    "activation", "sign", "square", "sqrt", "mul", "copy", "add", "iota",
}
_PE_OPS = {"matmul", "transpose", "ldweights"}


class Ct:
    def __init__(self, ctx, nc, name):
        self.sem = ctx.enter_context(nc.semaphore(name))
        self.n = 0

    def inc(self, instr, k=1):
        instr.then_inc(self.sem, k)
        self.n += k
        return instr

    def wait(self, eng_or_ew, val=None):
        eng = eng_or_ew.eng if isinstance(eng_or_ew, EW) else eng_or_ew
        eng.wait_ge(self.sem, self.n if val is None else val)


class EW:
    def __init__(self, ct, serialize=True):
        self.ct = ct
        self.serialize = serialize
        self.eng = None

    def bind(self, eng):
        self.eng = eng
        return self

    def __getattr__(self, name):
        eng = object.__getattribute__(self, "eng")
        fn = getattr(eng, name)
        if name in _SER_OPS or name in _PE_OPS:
            ct = self.ct
            ser = self.serialize and name in _SER_OPS

            def wrap(*a, **k):
                if ser:
                    eng.wait_ge(ct.sem, ct.n)
                ins = fn(*a, **k)
                ct.inc(ins)
                return ins
            return wrap
        return fn


class Phase:
    def __init__(self, nc, ctx, prefix="", blk_ctx=None):
        self.nc = nc
        self.blk = (blk_ctx if blk_ctx is not None else ctx).enter_context(nc.Block())
        self.prefix = prefix
        self.sL = Ct(ctx, nc, prefix + "sL")
        self.sG = Ct(ctx, nc, prefix + "sG")
        self.sS = Ct(ctx, nc, prefix + "sS")
        self.cT = Ct(ctx, nc, prefix + "cT")
        self.cV = Ct(ctx, nc, prefix + "cV")
        self.cA = Ct(ctx, nc, prefix + "cA")
        self.cG = Ct(ctx, nc, prefix + "cG")
        self._ctx = ctx
        self._nc = nc
        self.T_ = EW(self.cT, serialize=False)
        self.V_ = EW(self.cV)
        self.A_ = EW(self.cA)
        self.G_ = EW(self.cG)
        self.Vf_ = EW(self.cV, serialize=False)
        self.Af_ = EW(self.cA, serialize=False)

    def ct(self, name):
        return Ct(self._ctx, self._nc, self.prefix + name)

    def tensor(self, f):
        self.blk.tensor(lambda eng: f(self.T_.bind(eng)))

    def vector(self, f):
        self.blk.vector(lambda eng: f(self.V_.bind(eng)))

    def vector_f(self, f):
        self.blk.vector(lambda eng: f(self.Vf_.bind(eng)))

    def scalar_f(self, f):
        self.blk.scalar(lambda eng: f(self.Af_.bind(eng)))

    def scalar(self, f):
        self.blk.scalar(lambda eng: f(self.A_.bind(eng)))

    def gpsimd(self, f):
        self.blk.gpsimd(lambda eng: f(self.G_.bind(eng)))

    def sync(self, f):
        self.blk.sync(f)


def _img(w):
    K_, M_ = w.shape
    return np.ascontiguousarray(
        w.reshape(K_ // 128, 128, M_).transpose(1, 0, 2).reshape(128, -1), dtype=np.float32)


def _img_mmajor(w):
    K_, M_ = w.shape
    t = w.reshape(K_ // 128, 128, M_ // 128, 128)
    return np.ascontiguousarray(t.transpose(1, 2, 0, 3).reshape(128, -1), dtype=np.float32)


def _apify(t):
    return t[:, :]


def _vec_img(v):
    return np.ascontiguousarray(np.asarray(v, np.float32).reshape(8, 128).T)



def emit_A(nc, semctx, d, T, debug=False):
    NT = T // 128
    NCH = T // CH
    QT = CH // 128
    x0_d = d["x0"]
    wk_d = d["w_k"]
    wq_d = d["w_q"]
    wv_d = d["w_v"]
    wp_d = d["wpdup"]
    eyeF_d = d["eyeF"]
    eyeB_d = d["eyeB"]
    exp16_d = d["expand16"]
    tril_d = d["trils"]
    x1T_d = d["x1T"]
    zT_d = d["zT_scratch"]
    xnT_dbg = d.get("xnT_dbg")
    mats_dbg = d.get("mats_dbg")
    kqv_dbg = d.get("kqv_dbg")
    zT_dbg = d.get("zT_dbg")
    with ExitStack() as ctx:
        sb = lambda name, shape, dt: ctx.enter_context(nc.sbuf_tensor(name + "_sb", shape, dt))

        xnT = sb("xnT", [128, NG * T], BF16)
        zstg = [sb(f"zstg{j}", [64, CH], BF16) for j in range(2)]
        zrow = sb("zrow", [128, D], BF16)
        eyeF = sb("eyeF", [128, 128], F32)
        eyeB = sb("eyeB", [128, 128], BF16)
        wp = sb("wp", [128, 1024], F32)
        exp16 = sb("exp16", [16, 1024], F32)
        trils = sb("trils", [128, 4 * CH], BF16)
        onescol_b = sb("onescol_b", [128, 1], BF16)
        ones1x64 = sb("ones1x64", [1, 64], F32)
        kT_i = sb("kT_i", [64, T], BF16)
        qT_i = sb("qT_i", [64, T], BF16)
        vT_i = sb("vT_i", [64, T], BF16)
        vnat = sb("vnat", [128, NT * 65], BF16)
        expS = [sb(f"expS{j}", [128, CH], BF16) for j in range(3)]
        rec_row = sb("rec_row", [1, CH], F32)
        ztmp = sb("ztmp", [64, CH], F32)
        mats = {nm: sb(f"mat_{nm}", [128, NG * 1024], BF16) for nm in ("K", "Q", "V")}
        pcols = {nm: sb(f"pcol_{nm}", [128, NG * 16], F32) for nm in ("K", "Q", "V")}
        wraw = {nm: sb(f"wraw_{nm}", [16, 16], F32) for nm in ("K", "Q", "V")}
        probsT = {nm: sb(f"probsT_{nm}", [16, 16], F32) for nm in ("K", "Q", "V")}
        scr16 = sb("scr16", [16, 16], F32)
        col16 = sb("col16", [16, 1], F32)
        x0nat = [sb(f"x0nat{j}", [128, D], F32) for j in range(2)]
        sqnat = sb("sqnat", [128, D], F32)
        statc = sb("statc", [128, 6], F32)
        xn_nat = sb("xn_nat", [128, D], BF16)
        x1row = [sb(f"x1row{j}", [128, D], F32) for j in range(2)]

        P = Phase(nc, semctx, "A_", blk_ctx=ctx)
        sL, sS, cT, cV, cA = P.sL, P.sS, P.cT, P.cV, P.cA
        sLx = [P.ct("sLx0"), P.ct("sLx1")]
        sSx = [P.ct("sSx0"), P.ct("sSx1")]
        sSz = [P.ct("sSz0"), P.ct("sSz1")]

        pctx = ExitStack()
        psf = lambda name, shape, dt=F32: pctx.enter_context(nc.psum_tensor(name + "_ps", shape, dt))
        psT0 = psf("psT0", [128, 128], BF16)
        psT1 = psf("psT1", [128, 128], BF16)
        psPC = psf("psPC", [128, 16], F32)
        ps16 = psf("ps16", [16, 16], F32)

        def f(sync):
            for dst, src in ((eyeF, eyeF_d), (eyeB, eyeB_d), (wp, wp_d),
                             (exp16, exp16_d), (trils, tril_d)):
                sL.inc(sync.dma_start(dst[:, :], src.ap()), 16)
            for nm, src in (("K", wk_d), ("Q", wq_d), ("V", wv_d)):
                sL.inc(sync.dma_start(wraw[nm][:, :], src.ap()), 16)
        P.sync(f)
        consts_loaded = sL.n

        def f(V):
            V.memset(onescol_b[:, :], 1.0)
            V.memset(ones1x64[:, :], 1.0)
            V.memset(vnat[:, :].rearrange("p (n c) -> p n c", c=65)[:, :, 64], 1.0)
        P.vector(f)

        tp1_trail = []
        tile_done = {}
        for i in range(NT):
            xb = x0nat[i % 2]

            lsem = sLx[i % 2]

            def f(sync, i=i, xb=xb, lsem=lsem):
                if i >= 2:
                    sync.wait_ge(cV.sem, tile_done[i - 2])
                lsem.inc(sync.dma_start(xb[:, :], x0_d.ap()[i * 128:(i + 1) * 128, :]), 16)
            P.sync(f)
            ld_now = lsem.n

            def f(A, xb=xb, ld_now=ld_now, lsem=lsem):
                A.eng.wait_ge(lsem.sem, ld_now)
                A.activation(out=sqnat[:, :], in_=xb[:, :], func=AF.Square)
            P.scalar(f)
            sq_done = cA.n

            def f(V, xb=xb, ld_now=ld_now, sq_done=sq_done, lsem=lsem):
                V.eng.wait_ge(lsem.sem, ld_now)
                V.tensor_reduce(statc[:, 0:1], xb[:, :], AX.X, ALU.add)
                cA.wait(V, sq_done)
                V.tensor_reduce(statc[:, 1:2], sqnat[:, :], AX.X, ALU.add)
                V.tensor_scalar_mul(statc[:, 2:3], statc[:, 0:1], 1.0 / D)
                V.tensor_mul(statc[:, 3:4], statc[:, 2:3], statc[:, 2:3])
                V.scalar_tensor_tensor(statc[:, 4:5], statc[:, 1:2], 1.0 / D,
                                       statc[:, 3:4], ALU.mult, ALU.subtract)
                V.tensor_scalar_add(statc[:, 4:5], statc[:, 4:5], 1e-5)
            P.vector(f)
            var_done = cV.n

            def f(A, var_done=var_done):
                cV.wait(A, var_done)
                A.activation(out=statc[:, 5:6], in_=statc[:, 4:5], func=AF.Sqrt)
            P.scalar(f)
            sqrt_done = cA.n

            def f(V, xb=xb, sqrt_done=sqrt_done):
                cA.wait(V, sqrt_done)
                V.reciprocal(statc[:, 4:5], statc[:, 5:6])
                V.tensor_scalar(xn_nat[:, :], xb[:, :], statc[:, 2:3],
                                statc[:, 4:5], ALU.subtract, ALU.mult)
            P.vector(f)
            xn_done = cV.n

            for g in range(NG):
                pst = psT0 if g % 2 == 0 else psT1

                def f(Te, g=g, pst=pst, xn_done=xn_done, i=i):
                    if i == 0 and g == 0:
                        Te.eng.wait_ge(sL.sem, consts_loaded)
                    cV.wait(Te, xn_done)
                    if len(tp1_trail) >= 2:
                        Te.eng.wait_ge(cV.sem, tp1_trail[-2])
                    Te.transpose(pst[:, :], xn_nat[:, g * 128:(g + 1) * 128], eyeB[:, :])
                P.tensor(f)
                tpn = cT.n

                def f(V, g=g, i=i, pst=pst, tpn=tpn):
                    cT.wait(V, tpn)
                    V.tensor_copy(xnT[:, g * T + i * 128: g * T + (i + 1) * 128], pst[:, :])
                    tp1_trail.append(cV.n)
                P.vector(f)
            tile_done[i] = cV.n

        for nm in ("K", "Q", "V"):
            def f(V, nm=nm):
                V.eng.wait_ge(sL.sem, consts_loaded)
                V.tensor_reduce(col16[:, :], wraw[nm][:, :], AX.X, ALU.max)
                V.tensor_scalar_mul(col16[:, :], col16[:, :], -1.0)
            P.vector(f)
            neg_done = cV.n

            def f(A, nm=nm, neg_done=neg_done):
                A.eng.wait_ge(sL.sem, consts_loaded)
                cV.wait(A, neg_done)
                A.activation(out=scr16[:, :], in_=wraw[nm][:, :], func=AF.Exp,
                             bias=col16[:, :], accum_out=col16[:, :])
            P.scalar(f)
            exp_done = cA.n

            def f(V, nm=nm, exp_done=exp_done):
                cA.wait(V, exp_done)
                V.reciprocal(col16[:, :], col16[:, :])
                V.tensor_single_scalar(scr16[:, :], scr16[:, :], col16[:, :], ALU.mult)
            P.vector(f)
            probs_done = cV.n

            def f(Te, probs_done=probs_done):
                cV.wait(Te, probs_done)
                Te.transpose(ps16[:, :], scr16[:, :], eyeF[0:16, 0:16])
            P.tensor(f)
            t16 = cT.n

            def f(V, nm=nm, t16=t16):
                cT.wait(V, t16)
                V.tensor_copy(probsT[nm][:, :], ps16[:, :])
            P.vector(f)
            pc_cp = cV.n

            for r in range(NG):
                def f(Te, nm=nm, r=r, pc_cp=pc_cp):
                    cV.wait(Te, pc_cp)
                    Te.matmul(psPC[:, :], exp16[:, r * 128:(r + 1) * 128],
                              probsT[nm][:, :], start=True, stop=True)
                P.tensor(f)
                pc_mm = cT.n

                def f(V, nm=nm, r=r, pc_mm=pc_mm):
                    cT.wait(V, pc_mm)
                    V.tensor_copy(pcols[nm][:, r * 16:(r + 1) * 16], psPC[:, :])
                P.vector(f)
                pc_cp = cV.n

        def f(V):
            for nm in ("K", "Q", "V"):
                for v in range(16):
                    pb = (v % 2) * 64
                    rr = v // 2
                    for i in range(16):
                        if nm == "Q":
                            src = wp[pb:pb + 64, i * 64:(i + 1) * 64]
                        else:
                            src = eyeF[pb:pb + 64, pb:pb + 64]
                        V.tensor_single_scalar(
                            mats[nm][pb:pb + 64, rr * 1024 + i * 64: rr * 1024 + (i + 1) * 64],
                            src, pcols[nm][pb:pb + 64, rr * 16 + i: rr * 16 + i + 1], ALU.mult)
        P.vector(f)
        mats_done = cV.n

        if debug:
            def f(sync):
                sync.wait_ge(cV.sem, mats_done)
                sS.inc(sync.dma_start(xnT_dbg.ap(), xnT[:, :]), 16)
                for mi, nm in enumerate(("K", "Q", "V")):
                    sS.inc(sync.dma_start(
                        mats_dbg.ap()[:, mi * NG * 1024:(mi + 1) * NG * 1024],
                        mats[nm][:, :]), 16)
            P.sync(f)

        pctx.close()
        pctx = ExitStack()
        psf = lambda name, shape, dt=F32: pctx.enter_context(nc.psum_tensor(name + "_ps", shape, dt))
        psKQV0 = psf("psKQV0", [64, CH])
        psKQV1 = psf("psKQV1", [64, CH])
        psTB0 = psf("psTB0", [128, 64], BF16)
        psTB1 = psf("psTB1", [128, 64], BF16)
        psS0 = psf("psS0", [128, CH])
        psS1 = psf("psS1", [128, CH])
        psS2 = psf("psS2", [128, CH])
        psZ = psf("psZ", [65, CH])

        evac_trail = []
        vt_trail = []
        zfin_trail = []
        kqv_dump_sS = None
        for hi in range(16):
            for pi, (nm, dst) in enumerate((("K", kT_i), ("Q", qT_i), ("V", vT_i))):
                for tc in range(NCH):
                    j = (pi * NCH + tc) % 2
                    pskv = psKQV0 if j == 0 else psKQV1

                    def f(Te, nm=nm, tc=tc, pskv=pskv, pi=pi, hi=hi):
                        if hi == 0 and pi == 0 and tc == 0:
                            cV.wait(Te, mats_done)
                        if len(evac_trail) >= 2:
                            Te.eng.wait_ge(cV.sem, evac_trail[-2])
                        for r in range(NG):
                            Te.matmul(
                                pskv[:, :],
                                mats[nm][:, r * 1024 + hi * 64: r * 1024 + (hi + 1) * 64],
                                xnT[:, r * T + tc * CH: r * T + (tc + 1) * CH],
                                start=(r == 0), stop=(r == NG - 1))
                    P.tensor(f)
                    mm_done = cT.n

                    def f(V, dst=dst, tc=tc, pskv=pskv, mm_done=mm_done,
                          hi=hi, pi=pi, kqv_dump_sS=kqv_dump_sS):
                        if hi == 1 and pi == 0 and tc == 0 and kqv_dump_sS is not None:
                            V.eng.wait_ge(sS.sem, kqv_dump_sS)
                        cT.wait(V, mm_done)
                        V.tensor_copy(dst[:, tc * CH:(tc + 1) * CH], pskv[:, :])
                        evac_trail.append(cV.n)
                    P.vector(f)

            if debug and hi == 0:
                kqv_done0 = cV.n

                def f(sync, kqv_done0=kqv_done0):
                    sync.wait_ge(cV.sem, kqv_done0)
                    for src_t, off in ((kT_i, 0), (qT_i, T), (vT_i, 2 * T)):
                        sS.inc(sync.dma_start(kqv_dbg.ap()[:, off:off + T], src_t[:, :]), 16)
                P.sync(f)
                kqv_dump_sS = sS.n

            kqv_done = cV.n
            for pt in range(NT):
                pst = psTB0 if pt % 2 == 0 else psTB1

                def f(Te, pt=pt, pst=pst, kqv_done=kqv_done):
                    if pt == 0:
                        cV.wait(Te, kqv_done)
                    if len(vt_trail) >= 2:
                        Te.eng.wait_ge(cV.sem, vt_trail[-2])
                    Te.transpose(pst[:, :], vT_i[:, pt * 128:(pt + 1) * 128], eyeB[0:64, 0:64])
                P.tensor(f)
                vt_mm = cT.n

                def f(V, pt=pt, pst=pst, vt_mm=vt_mm):
                    cT.wait(V, vt_mm)
                    V.tensor_copy(vnat[:, pt * 65:pt * 65 + 64], pst[:, :])
                    vt_trail.append(cV.n)
                P.vector_f(f)

            vnat_done = cV.n
            ZLAG = 2
            for qc in range(NCH):
                np_tiles = (qc + 1) * QT
                cA_base = cA.n
                exp_specs = []
                smm_list = []
                zmm_list = []
                zspecs = []

                def emit_z(spec, np_tiles=np_tiles):
                    pt, eb, diag, e_done, m_done = spec

                    def f(Te, pt=pt, eb=eb, diag=diag, np_tiles=np_tiles,
                          e_done=e_done, m_done=m_done):
                        if diag:
                            cV.wait(Te, m_done)
                        else:
                            cA.wait(Te, e_done)
                        if pt == 0 and zfin_trail:
                            Te.eng.wait_ge(cV.sem, zfin_trail[-1])
                        Te.matmul(psZ[:, :], vnat[:, pt * 65:pt * 65 + 65], eb[:, :],
                                  start=(pt == 0), stop=(pt == np_tiles - 1))
                    P.tensor(f)
                    zmm_list.append(cT.n)

                for pt in range(np_tiles):
                    pss = (psS0, psS1, psS2)[pt % 3]
                    eb = expS[pt % 3]
                    diag = pt >= qc * QT
                    e_done = cA_base + pt + 1

                    def f(Te, pt=pt, pss=pss, qc=qc, hi=hi, vnat_done=vnat_done,
                          e_done=e_done, cA_base=cA_base):
                        if qc == 0 and pt == 0:
                            cV.wait(Te, vnat_done)
                        if pt >= 3:
                            Te.eng.wait_ge(cA.sem, cA_base + pt - 2)
                        Te.matmul(pss[:, :], kT_i[:, pt * 128:(pt + 1) * 128],
                                  qT_i[:, qc * CH:(qc + 1) * CH], start=True, stop=True)
                    P.tensor(f)
                    smm_list.append(cT.n)
                    exp_specs.append((smm_list[-1], pss, eb, pt))

                    m_done = None
                    if diag:
                        def f(V, eb=eb, pt=pt, qc=qc, e_done=e_done):
                            cA.wait(V, e_done)
                            dpos = pt - qc * QT
                            V.tensor_mul(eb[:, :], eb[:, :], trils[:, dpos * CH:(dpos + 1) * CH])
                        P.vector(f)
                        m_done = cV.n

                    zspecs.append((pt, eb, diag, e_done, m_done))
                    if pt >= ZLAG:
                        emit_z(zspecs[pt - ZLAG])
                for pt in range(max(0, np_tiles - ZLAG), np_tiles):
                    emit_z(zspecs[pt])

                def f(A, exp_specs=exp_specs, zmm_list=zmm_list):
                    for smm, pss, eb, zi in exp_specs:
                        A.eng.wait_ge(cT.sem, smm)
                        if zi >= 3:
                            A.eng.wait_ge(cT.sem, zmm_list[zi - 3])
                        A.activation(out=eb[:, :], in_=pss[:, :], func=AF.Exp, scale=0.125)
                P.scalar_f(f)
                assert cA.n == cA_base + np_tiles

                zacc_done = cT.n

                def f(V, zacc_done=zacc_done):
                    cT.wait(V, zacc_done)
                    V.reciprocal(rec_row[:, :], psZ[64:65, :])
                    V.tensor_copy(ztmp[:, :], psZ[0:64, :])
                P.vector(f)
                ev_done = cV.n

                def f(Te, ev_done=ev_done):
                    cV.wait(Te, ev_done)
                    Te.matmul(psZ[0:64, :], ones1x64[:, :], rec_row[:, :], start=True, stop=True)
                P.tensor(f)
                bc_done = cT.n

                zslot = (hi * NCH + qc) % 2
                zs = zstg[zslot]
                zsem = sSz[zslot]

                def f(V, bc_done=bc_done, zs=zs, zsem=zsem):
                    cT.wait(V, bc_done)
                    V.eng.wait_ge(zsem.sem, zsem.n)
                    V.tensor_mul(zs[:, :], ztmp[:, :], psZ[0:64, :])
                    zfin_trail.append(cV.n)
                P.vector(f)
                zmul_done = cV.n

                def f(sync, qc=qc, hi=hi, zs=zs, zsem=zsem, zmul_done=zmul_done):
                    sync.wait_ge(cV.sem, zmul_done)
                    zsem.inc(sync.dma_start(
                        zT_d.ap()[hi * 64:(hi + 1) * 64, qc * CH:(qc + 1) * CH], zs[:, :]), 16)
                P.sync(f)

        attn_done = cV.n
        if debug:
            def f(sync):
                sync.wait_ge(sSz[0].sem, sSz[0].n)
                sync.wait_ge(sSz[1].sem, sSz[1].n)
                sS.inc(sync.dma_start(zT_dbg.ap(), zT_d.ap()), 16)
            P.sync(f)

        pctx.close()
        pctx = ExitStack()
        psf = lambda name, shape, dt=F32: pctx.enter_context(nc.psum_tensor(name + "_ps", shape, dt))
        psX0 = psf("psX0", [128, 128])
        psX1 = psf("psX1", [128, 128])

        add_trail = []
        row_done = {}
        st_issued = {}
        for i in range(NT):
            xb = x0nat[i % 2]
            xr = x1row[i % 2]

            lsem = sLx[i % 2]
            ssem = sSx[i % 2]

            def f(sync, i=i, xb=xb, lsem=lsem, ssem=ssem):
                if i == 0:
                    sync.wait_ge(cV.sem, attn_done)
                    sync.wait_ge(sSz[0].sem, sSz[0].n)
                    sync.wait_ge(sSz[1].sem, sSz[1].n)
                if i >= 2:
                    sync.wait_ge(ssem.sem, ssem.n)
                    sync.wait_ge(cV.sem, row_done[i - 2])
                lsem.inc(sync.dma_start(xb[:, :], x0_d.ap()[i * 128:(i + 1) * 128, :]), 16)
                if i >= 1:
                    sync.wait_ge(cV.sem, row_done[i - 1])
                for g in range(NG):
                    lsem.inc(sync.dma_start(
                        zrow[:, g * 128:(g + 1) * 128],
                        zT_d.ap()[g * 128:(g + 1) * 128, i * 128:(i + 1) * 128]), 16)
            P.sync(f)
            ld_now = lsem.n

            for g in range(NG):
                pst = psX0 if g % 2 == 0 else psX1

                def f(Te, g=g, pst=pst, xb=xb, ld_now=ld_now, lsem=lsem):
                    Te.eng.wait_ge(lsem.sem, ld_now)
                    if len(add_trail) >= 2:
                        Te.eng.wait_ge(cV.sem, add_trail[-2])
                    Te.transpose(pst[:, :], xb[:, g * 128:(g + 1) * 128], eyeF[:, :])
                P.tensor(f)
                x_mm = cT.n

                def f(V, g=g, i=i, pst=pst, xr=xr, x_mm=x_mm, ld_now=ld_now):
                    V.eng.wait_ge(lsem.sem, ld_now)
                    cT.wait(V, x_mm)
                    V.tensor_add(xr[:, g * 128:(g + 1) * 128], pst[:, :],
                                 zrow[:, g * 128:(g + 1) * 128])
                    add_trail.append(cV.n)
                P.vector(f)
            row_done[i] = cV.n

            def f(sync, i=i, xr=xr, ssem=ssem):
                sync.wait_ge(cV.sem, row_done[i])
                for g in range(NG):
                    ssem.inc(sync.dma_start(
                        x1T_d.ap()[g * 128:(g + 1) * 128, i * 128:(i + 1) * 128],
                        xr[:, g * 128:(g + 1) * 128]), 16)
            P.sync(f)

        def f(sync):
            sync.wait_ge(sSx[0].sem, sSx[0].n)
            sync.wait_ge(sSx[1].sem, sSx[1].n)
            sync.wait_ge(sS.sem, sS.n)
        P.sync(f)
        pctx.close()





def build_A(T, debug=False):
    nc = bass.Bass("TRN2", target_bir_lowering=False)
    NT = T // 128
    NCH = T // CH
    QT = CH // 128

    x0_d = nc.dram_tensor("x0", [T, D], F32, kind="ExternalInput")
    wk_d = nc.dram_tensor("w_k", [16, 16], F32, kind="ExternalInput")
    wq_d = nc.dram_tensor("w_q", [16, 16], F32, kind="ExternalInput")
    wv_d = nc.dram_tensor("w_v", [16, 16], F32, kind="ExternalInput")
    wp_d = nc.dram_tensor("wpdup", [128, 1024], F32, kind="ExternalInput")
    eyeF_d = nc.dram_tensor("eyeF", [128, 128], F32, kind="ExternalInput")
    eyeB_d = nc.dram_tensor("eyeB", [128, 128], BF16, kind="ExternalInput")
    exp16_d = nc.dram_tensor("expand16", [16, 1024], F32, kind="ExternalInput")
    tril_d = nc.dram_tensor("trils", [128, 4 * CH], BF16, kind="ExternalInput")
    x1T_d = nc.dram_tensor("x1T", [D, T], F32, kind="ExternalOutput")
    zT_d = nc.dram_tensor("zT_scratch", [D, T], BF16, kind="Internal")
    if debug:
        xnT_dbg = nc.dram_tensor("xnT_dbg", [128, NG * T], BF16, kind="ExternalOutput")
        mats_dbg = nc.dram_tensor("mats_dbg", [128, 3 * NG * 1024], BF16, kind="ExternalOutput")
        kqv_dbg = nc.dram_tensor("kqv_dbg", [64, 3 * T], BF16, kind="ExternalOutput")
        zT_dbg = nc.dram_tensor("zT_dbg", [D, T], BF16, kind="ExternalOutput")

    d = {"x0": x0_d, "w_k": wk_d, "w_q": wq_d, "w_v": wv_d, "wpdup": wp_d, "eyeF": eyeF_d, "eyeB": eyeB_d, "expand16": exp16_d, "trils": tril_d, "x1T": x1T_d, "zT_scratch": zT_d}
    if debug:
        d.update({"xnT_dbg": xnT_dbg, "mats_dbg": mats_dbg, "kqv_dbg": kqv_dbg, "zT_dbg": zT_dbg})
    with ExitStack() as semctx:
        emit_A(nc, semctx, d, T, debug=debug)
    return nc


def emit_B(nc, semctx, d, T, debug=False):
    NHALF = 2
    TH = T // NHALF
    NTCH = TH // CH
    E = 64
    x1T_d = d["x1T"]
    moe_d = d["moe_img"]
    sg_d = d["sg_img"]
    su_d = d["su_img"]
    sd_d = d["sd_img"]
    gate_d = d["gate_img"]
    gbias_d = d["gate_bias"]
    sel_d = d["sel_img"]
    eyeF_d = d["eyeF"]
    x2T_d = d["x2T"]
    xn_dbg = d.get("xn_dbg")
    comb_dbg = d.get("comb_dbg")
    sh_dbg = d.get("sh_dbg")
    with ExitStack() as ctx:
        sb = lambda name, shape, dt: ctx.enter_context(nc.sbuf_tensor(name + "_sb", shape, dt))

        acc = sb("acc", [128, NG * TH], F32)
        xnT = sb("xnTb", [128, NG * TH], BF16)
        combT = sb("combT", [64, TH], BF16)
        hs = sb("hs", [128, NG * CH], BF16)
        hc = sb("hc", [128, 4 * CH], BF16)
        h1 = sb("h1", [128, CH], BF16)
        silu_sb = [sb(f"silu{j}", [128, CH], BF16) for j in range(2)]
        wslot = [sb(f"wslot{j}", [128, 12288], BF16) for j in range(2)]
        selslot = [sb(f"selslot{j}", [64, 128], BF16) for j in range(2)]
        gatew = sb("gatew", [128, 512], F32)
        gsum = sb("gsum", [64, 1], F32)
        mrows = sb("mrows", [1, 2048], F32)
        rsrows = sb("rsrows", [1, 2048], F32)

        gbias = sb("gbias", [64, 1], F32)
        eyeF = sb("eyeFb", [128, 128], F32)
        onescol_f = sb("onescol_f", [128, 1], F32)
        ones1x128 = sb("ones1x128", [1, 128], F32)
        scrB = sb("scrB", [128, CH], F32)
        sq = scrB
        gmb = scrB[0:64, :]
        lgb = scrB[64:128, :]
        rowst = sb("rowst", [1, 3 * CH], F32)

        sc_tm = sb("sc_tm", [128, 4 * 64], F32)
        sgate = scrB[0:64, :]
        gr = _apify(sb("gr", [128, 8], F32))
        gso = _apify(sb("gso", [128, 8], F32))
        mcol = _apify(sb("mcol", [128, 1], F32))
        gmask = _apify(sb("gmask", [128, 8], F32))
        msk64 = _apify(sb("msk64", [128, 64], F32))
        msv = _apify(sb("msv", [128, 64], F32))
        mso = _apify(sb("mso", [128, 64], F32))
        eqv = _apify(sb("eqv", [128, 64], F32))
        cu = _apify(sb("cu", [128, 64], F32))
        dcol = _apify(sb("dcol", [128, 1], F32))
        comb_tm = _apify(sb("comb_tm", [128, 64], F32))

        P = Phase(nc, semctx, "B_", blk_ctx=ctx)
        sL, sS, cT, cV, cA = P.sL, P.sS, P.cT, P.cV, P.cA
        sG = P.sG
        sWs = [P.ct("sW0"), P.ct("sW1")]

        pctx = ExitStack()
        pool = [pctx.enter_context(nc.psum_tensor(f"pp{j}_ps", [128, CH], F32))
                for j in range(8)]
        psSUM = pool[0][0:1, :]
        psSQ = pool[1][0:1, :]
        psM = pool[2][:, :]
        psRS = pool[3][:, :]

        def f(sync):
            for dst, src in ((gbias, gbias_d), (eyeF, eyeF_d)):
                sL.inc(sync.dma_start(dst[:, :], src.ap()), 16)
            sL.inc(sync.dma_start(gatew[:, :], gate_d.ap()), 16)
        P.sync(f)

        consts_sL = sL.n

        def f(V):
            V.memset(onescol_f[:, :], 1.0)
            V.memset(ones1x128[:, :], 1.0)
        P.vector(f)
        memset_done = cV.n

        def f(Te):
            cV.wait(Te, memset_done)
            Te.eng.wait_ge(sL.sem, consts_sL)
            for r in range(NG):
                Te.matmul(pool[6][0:64, 0:1], gatew[:, r * 64:(r + 1) * 64],
                          onescol_f[:, :], start=(r == 0), stop=(r == NG - 1))
        P.tensor(f)
        gsum_mm = cT.n

        def f(V, gsum_mm=gsum_mm):
            cT.wait(V, gsum_mm)
            V.tensor_copy(gsum[:, :], pool[6][0:64, 0:1])
        P.vector(f)
        gsum_done = cV.n

        wfree = {}
        last_store = 0

        for h in range(NHALF):
            sLh = P.ct(f"sLh{h}")

            def f(sync, h=h, sLh=sLh):
                if h > 0:
                    sync.wait_ge(sS.sem, sS.n)
                for g in range(NG):
                    sLh.inc(sync.dma_start(
                        acc[:, g * TH:(g + 1) * TH],
                        x1T_d.ap()[g * 128:(g + 1) * 128, h * TH:(h + 1) * TH]), 16)
            P.sync(f)
            x1_ld = sLh.n

            prev_var_done = 0
            prev_xn_done = 0
            for tc in range(NTCH):
                sq_tgt = []
                mm_tgt = []
                an = cA.n
                for g in range(NG):
                    an += 1
                    sq_tgt.append(an)

                    def f(Te, tc=tc, g=g, x1_ld=x1_ld, sLh=sLh, h=h,
                          pvd=prev_var_done):
                        Te.eng.wait_ge(sLh.sem, x1_ld)
                        if h == 0 and tc == 0 and g == 0:
                            cV.wait(Te, memset_done)
                            Te.eng.wait_ge(sL.sem, consts_sL)
                        if g == 0 and pvd:
                            Te.eng.wait_ge(cV.sem, pvd)
                        cA.wait(Te, sq_tgt[g])
                        Te.matmul(psSUM, onescol_f[:, :],
                                  acc[:, g * TH + tc * CH: g * TH + (tc + 1) * CH],
                                  start=(g == 0), stop=(g == NG - 1))
                        Te.matmul(psSQ, onescol_f[:, :], sq[:, :],
                                  start=(g == 0), stop=(g == NG - 1))
                    P.tensor(f)
                    mm_tgt.append(cT.n)

                def f(A, tc=tc, x1_ld=x1_ld, sLh=sLh, pxd=prev_xn_done,
                      mm_tgt=mm_tgt):
                    for g in range(NG):
                        if g == 0:
                            A.eng.wait_ge(sLh.sem, x1_ld)
                            if pxd:
                                A.eng.wait_ge(cV.sem, pxd)
                        else:
                            A.eng.wait_ge(cT.sem, mm_tgt[g - 1])
                        A.activation(out=sq[:, :],
                                     in_=acc[:, g * TH + tc * CH: g * TH + (tc + 1) * CH],
                                     func=AF.Square)
                P.scalar_f(f)
                assert cA.n == sq_tgt[-1], (cA.n, sq_tgt[-1])
                sums_done = cT.n

                def f(V, sums_done=sums_done):
                    cT.wait(V, sums_done)
                    V.tensor_scalar_mul(rowst[:, 0:CH], psSUM, 1.0 / D)
                    V.tensor_mul(rowst[:, 2 * CH:3 * CH], rowst[:, 0:CH], rowst[:, 0:CH])
                    V.scalar_tensor_tensor(rowst[:, CH:2 * CH], psSQ, 1.0 / D,
                                           rowst[:, 2 * CH:3 * CH], ALU.mult, ALU.subtract)
                    V.tensor_scalar_add(rowst[:, CH:2 * CH], rowst[:, CH:2 * CH], 1e-5)
                P.vector(f)
                prev_var_done = var_done = cV.n

                def f(A, var_done=var_done):
                    cV.wait(A, var_done)
                    A.activation(out=rowst[:, 2 * CH:3 * CH], in_=rowst[:, CH:2 * CH],
                                 func=AF.Sqrt)
                P.scalar(f)
                sqrt_done = cA.n

                def f(V, sqrt_done=sqrt_done):
                    cA.wait(V, sqrt_done)
                    V.reciprocal(rowst[:, CH:2 * CH], rowst[:, 2 * CH:3 * CH])
                P.vector(f)
                rs_done = cV.n

                def f(Te, rs_done=rs_done):
                    cV.wait(Te, rs_done)
                    Te.matmul(psM, ones1x128[:, :], rowst[:, 0:CH], start=True, stop=True)
                    Te.matmul(psRS, ones1x128[:, :], rowst[:, CH:2 * CH],
                              start=True, stop=True)
                P.tensor(f)
                bc_done = cT.n

                def f(V, tc=tc, bc_done=bc_done):
                    cT.wait(V, bc_done)
                    V.tensor_copy(mrows[:, tc * CH:(tc + 1) * CH], rowst[:, 0:CH])
                    V.tensor_copy(rsrows[:, tc * CH:(tc + 1) * CH], rowst[:, CH:2 * CH])
                    for g in range(NG):
                        V.tensor_tensor(sq[:, :],
                                        acc[:, g * TH + tc * CH: g * TH + (tc + 1) * CH],
                                        psM, ALU.subtract)
                        V.tensor_tensor(xnT[:, g * TH + tc * CH: g * TH + (tc + 1) * CH],
                                        sq[:, :], psRS, ALU.mult)
                P.vector(f)
                prev_xn_done = cV.n
            xn_done_h = cV.n

            if debug and h == 0:
                def f(sync, xn_done_h=xn_done_h):
                    sync.wait_ge(cV.sem, xn_done_h)
                    sS.inc(sync.dma_start(xn_dbg.ap(), xnT[:, :]), 16)
                P.sync(f)

            last_sig = 0
            last_tr_cp = 0
            last_gm = 0
            for tc in range(NTCH):
                def f(Te, tc=tc, xn_done_h=xn_done_h, last_gm=last_gm, h=h):
                    cV.wait(Te, xn_done_h)
                    if h == 0 and tc == 0:
                        cV.wait(Te, gsum_done)
                    if last_gm:
                        Te.eng.wait_ge(cV.sem, last_gm)
                    for r in range(NG):
                        Te.matmul(pool[4][0:64, :], gatew[:, r * 64:(r + 1) * 64],
                                  acc[:, r * TH + tc * CH: r * TH + (tc + 1) * CH],
                                  start=(r == 0), stop=(r == NG - 1))
                    Te.matmul(pool[2][0:64, :], ones1x128[:, 0:64],
                              mrows[:, tc * CH:(tc + 1) * CH], start=True, stop=True)
                    Te.matmul(pool[3][0:64, :], ones1x128[:, 0:64],
                              rsrows[:, tc * CH:(tc + 1) * CH], start=True, stop=True)
                P.tensor(f)
                g_mm = cT.n

                def f(V, g_mm=g_mm):
                    cT.wait(V, g_mm)
                    V.tensor_single_scalar(gmb, pool[2][0:64, :], gsum[:, :], ALU.mult)
                    V.tensor_tensor(lgb, pool[4][0:64, :], gmb, ALU.subtract)
                    V.tensor_tensor(lgb, lgb, pool[3][0:64, :], ALU.mult)
                P.vector(f)
                last_gm = lg_done = cV.n

                def f(A, lg_done=lg_done):
                    cV.wait(A, lg_done)
                    A.activation(out=sgate, in_=lgb, func=AF.Sigmoid)
                P.scalar(f)
                last_sig = sig_done = cA.n

                def f(V, sig_done=sig_done):
                    cA.wait(V, sig_done)
                    V.tensor_scalar_add(sgate, sgate, gbias[:, :])
                P.vector(f)
                sb_done = cV.n

                for tt in range(4):
                    def f(Te, tt=tt, sb_done=sb_done, last_tr_cp=last_tr_cp):
                        cV.wait(Te, sb_done)
                        if last_tr_cp:
                            Te.eng.wait_ge(cV.sem, last_tr_cp)
                        Te.transpose(pool[5][:, 0:64], sgate[:, tt * 128:(tt + 1) * 128][0:64, :] if False else scrB[0:64, tt * 128:(tt + 1) * 128],
                                     eyeF[0:64, 0:64])
                    P.tensor(f)
                    tr_done = cT.n

                    def f(V, tt=tt, tr_done=tr_done):
                        cT.wait(V, tr_done)
                        V.tensor_copy(sc_tm[:, tt * 64:(tt + 1) * 64], pool[5][:, 0:64])
                    P.vector(f)
                    last_tr_cp = cV.n

                for tt in range(4):
                    def f(V, tt=tt):
                        sc = sc_tm[:, tt * 64:(tt + 1) * 64]
                        scv = sc.rearrange("p (g j) -> p g j", j=8)
                        V.tensor_reduce(gr.rearrange("p (g o) -> p g o", o=1),
                                        scv, AX.X, ALU.max)
                        V.tensor_copy(gso, gr)
                        for j in range(4):
                            V.reduce_max(mcol, gr, AX.X)
                            if j < 3:
                                V.tensor_single_scalar(gmask, gr, mcol,
                                                       ALU.is_ge)
                                V.scalar_tensor_tensor(gr, gmask, -1e30,
                                                       gr, ALU.mult, ALU.add)
                        V.tensor_single_scalar(gmask, gso, mcol, ALU.is_ge)
                        m64 = msk64.rearrange("p (g j) -> p g j", j=8)
                        for jj in range(8):
                            V.tensor_copy(m64[:, :, jj], gmask)
                        V.tensor_mul(msv, sc, msk64)
                        V.tensor_copy(mso, msv)
                        for j in range(6):
                            V.reduce_max(mcol, msv, AX.X)
                            if j < 5:
                                V.tensor_single_scalar(eqv, msv, mcol,
                                                       ALU.is_ge)
                                V.scalar_tensor_tensor(msv, eqv, -1e30,
                                                       msv, ALU.mult, ALU.add)
                        V.tensor_scalar_max(mcol, mcol, 0.0)
                        V.tensor_single_scalar(eqv, mso, mcol, ALU.is_ge)
                        V.tensor_mul(cu, eqv, mso)
                        V.reduce_sum(dcol, cu, AX.X)
                        V.tensor_scalar_add(dcol, dcol, 1e-8)
                        V.reciprocal(dcol, dcol)
                        V.tensor_single_scalar(comb_tm, cu, dcol, ALU.mult)
                    P.vector(f)
                    ct_done = cV.n

                    def f(Te, ct_done=ct_done, last_tr_cp=last_tr_cp):
                        cV.wait(Te, ct_done)
                        Te.eng.wait_ge(cV.sem, last_tr_cp)
                        Te.transpose(pool[5][0:64, 0:128], comb_tm, eyeF[:, :])
                    P.tensor(f)
                    cb_mm = cT.n

                    def f(V, tc=tc, tt=tt, cb_mm=cb_mm):
                        cT.wait(V, cb_mm)
                        V.tensor_copy(combT[:, tc * CH + tt * 128: tc * CH + (tt + 1) * 128],
                                      pool[5][0:64, 0:128])
                    P.vector(f)
                    last_tr_cp = cV.n
            comb_done_h = cV.n

            if debug and h == 0:
                def f(sync, comb_done_h=comb_done_h):
                    sync.wait_ge(cV.sem, comb_done_h)
                    sS.inc(sync.dma_start(comb_dbg.ap()[:, 0:TH], combT[:, :]), 16)
                P.sync(f)

            tfull = cT.n

            def f(G, tfull=tfull):
                G.eng.wait_ge(cT.sem, tfull)
                sG.inc(G.eng.dma_start(wslot[0][:, 0:8192], sg_d.ap()), 16)
                sG.inc(G.eng.dma_start(wslot[0][:, 8192:12288], su_d.ap()[:, 0:4096]), 16)
                sG.inc(G.eng.dma_start(wslot[1][:, 0:4096], su_d.ap()[:, 4096:8192]), 16)
                sG.inc(G.eng.dma_start(wslot[1][:, 4096:12288], sd_d.ap()), 16)
            P.gpsimd(f)
            shw_ld = sG.n

            def _su(r, mt):
                off = 8192 + r * 1024 + mt * 128
                if off < 12288:
                    return wslot[0][:, off:off + 128]
                return wslot[1][:, off - 12288:off - 12288 + 128]

            act_next = cA.n
            silu_tgt = []
            hmul_tgt = []
            for tc in range(NTCH):
                specs = []
                for mt in range(NG):
                    k = len(silu_tgt)
                    pg = pool[4] if k % 2 == 0 else pool[0]
                    pu = (pool[5], pool[1], pool[3])[k % 3]

                    def f(Te, tc=tc, mt=mt, k=k, pg=pg, comb_done_h=comb_done_h,
                          shw_ld=shw_ld):
                        if tc == 0 and mt == 0:
                            cV.wait(Te, comb_done_h)
                            Te.eng.wait_ge(sG.sem, shw_ld)
                        if k >= 2:
                            Te.eng.wait_ge(cA.sem, silu_tgt[k - 2])
                        for r in range(NG):
                            Te.matmul(pg[:, :],
                                      wslot[0][:, r * 1024 + mt * 128: r * 1024 + (mt + 1) * 128],
                                      xnT[:, r * TH + tc * CH: r * TH + (tc + 1) * CH],
                                      start=(r == 0), stop=(r == NG - 1))
                    P.tensor(f)
                    ga_done = cT.n
                    act_next += 1
                    silu_tgt.append(act_next)
                    specs.append((ga_done, pg, k))

                    def f(Te, tc=tc, mt=mt, k=k, pu=pu):
                        if k >= 3:
                            Te.eng.wait_ge(cV.sem, hmul_tgt[k - 3])
                        for r in range(NG):
                            Te.matmul(pu[:, :], _su(r, mt),
                                      xnT[:, r * TH + tc * CH: r * TH + (tc + 1) * CH],
                                      start=(r == 0), stop=(r == NG - 1))
                    P.tensor(f)
                    ua_done = cT.n

                    def f(V, mt=mt, k=k, ua_done=ua_done, pu=pu):
                        cT.wait(V, ua_done)
                        cA.wait(V, silu_tgt[k])
                        V.tensor_tensor(hs[:, mt * CH:(mt + 1) * CH],
                                        silu_sb[k % 2][:, :], pu[:, :], ALU.mult)
                    P.vector(f)
                    hmul_tgt.append(cV.n)
                hs_done = cV.n

                def f(A, specs=specs):
                    for ga, pg, k in specs:
                        A.eng.wait_ge(cT.sem, ga)
                        if k >= 2:
                            A.eng.wait_ge(cV.sem, hmul_tgt[k - 2])
                        A.activation(out=silu_sb[k % 2][:, :], in_=pg[:, :],
                                     func=AF.Silu)
                P.scalar_f(f)
                assert cA.n == silu_tgt[-1], (cA.n, silu_tgt[-1])

                ev_done = []
                for mt in range(NG):
                    def f(Te, tc=tc, mt=mt, hs_done=hs_done):
                        if mt == 0:
                            cV.wait(Te, hs_done)
                        if mt >= 2:
                            cV.wait(Te, ev_done[mt - 2])
                        pd = pool[6] if mt % 2 == 0 else pool[7]
                        for r in range(NG):
                            Te.matmul(pd[:, :],
                                      wslot[1][:, 4096 + r * 1024 + mt * 128: 4096 + r * 1024 + (mt + 1) * 128],
                                      hs[:, r * CH:(r + 1) * CH],
                                      start=(r == 0), stop=(r == NG - 1))
                    P.tensor(f)
                    da_done = cT.n

                    def f(V, tc=tc, mt=mt, da_done=da_done):
                        cT.wait(V, da_done)
                        pd = pool[6] if mt % 2 == 0 else pool[7]
                        a = acc[:, mt * TH + tc * CH: mt * TH + (tc + 1) * CH]
                        V.tensor_tensor(a, a, pd[:, :], ALU.add)
                    P.vector(f)
                    ev_done.append(cV.n)
            shared_done = cV.n

            sh_dump_sS = 0
            if debug and h == 0:
                def f(sync, shared_done=shared_done):
                    sync.wait_ge(cV.sem, shared_done)
                    for g in range(NG):
                        sS.inc(sync.dma_start(
                            sh_dbg.ap()[g * 128:(g + 1) * 128, :],
                            acc[:, g * TH:(g + 1) * TH]), 16)
                P.sync(f)
                sh_dump_sS = sS.n

            wfree[h * E + 0] = cT.n
            wfree[h * E + 1] = cT.n
            last_hc = 0
            for e in range(E):
                gi = h * E + e
                slot = gi % 2
                ws = wslot[slot]
                wsem = sWs[slot]

                sel_s = selslot[slot]

                def f(G, gi=gi, e=e, ws=ws, wsem=wsem, sel_s=sel_s):
                    if gi in wfree:
                        G.eng.wait_ge(cT.sem, wfree[gi])
                    wsem.inc(G.eng.dma_start(ws[:, :], moe_d.ap()[e]), 16)
                    wsem.inc(G.eng.dma_start(sel_s[:, :], sel_d.ap()[:, e * 128:(e + 1) * 128]), 16)
                P.gpsimd(f)
                w_ld = wsem.n

                especs = []
                for tc in range(NTCH):
                    def f(Te, e=e, tc=tc, w_ld=w_ld, wsem=wsem, shared_done=shared_done,
                          last_hc=last_hc, sel_s=sel_s):
                        if tc == 0:
                            Te.eng.wait_ge(wsem.sem, w_ld)
                            if e == 0:
                                cV.wait(Te, shared_done)
                        if last_hc:
                            Te.eng.wait_ge(cV.sem, last_hc)
                        Te.matmul(pool[2][:, :], sel_s[:, :],
                                  combT[:, tc * CH:(tc + 1) * CH], start=True, stop=True)
                    P.tensor(f)

                    for it in range(4):
                        k = len(silu_tgt)
                        pg = pool[4] if k % 2 == 0 else pool[0]
                        pu = (pool[5], pool[1], pool[3])[k % 3]

                        def f(Te, tc=tc, it=it, ws=ws, k=k, pg=pg):
                            Te.eng.wait_ge(cA.sem, silu_tgt[k - 2])
                            for r in range(NG):
                                Te.matmul(pg[:, :],
                                          ws[:, r * 512 + it * 128: r * 512 + (it + 1) * 128],
                                          xnT[:, r * TH + tc * CH: r * TH + (tc + 1) * CH],
                                          start=(r == 0), stop=(r == NG - 1))
                        P.tensor(f)
                        gmm = cT.n
                        act_next += 1
                        silu_tgt.append(act_next)
                        especs.append((gmm, pg, k))

                        def f(Te, tc=tc, it=it, ws=ws, k=k, pu=pu):
                            Te.eng.wait_ge(cV.sem, hmul_tgt[k - 3])
                            for r in range(NG):
                                Te.matmul(
                                    pu[:, :],
                                    ws[:, 4096 + r * 512 + it * 128: 4096 + r * 512 + (it + 1) * 128],
                                    xnT[:, r * TH + tc * CH: r * TH + (tc + 1) * CH],
                                    start=(r == 0), stop=(r == NG - 1))
                        P.tensor(f)
                        umm = cT.n

                        def f(V, it=it, k=k, umm=umm, pu=pu):
                            cT.wait(V, umm)
                            cA.wait(V, silu_tgt[k])
                            V.tensor_tensor(h1[:, :], silu_sb[k % 2][:, :], pu[:, :], ALU.mult)
                            V.tensor_tensor(hc[:, it * CH:(it + 1) * CH], h1[:, :],
                                            pool[2][:, :], ALU.mult)
                        P.vector(f)
                        hmul_tgt.append(cV.n)
                    last_hc = hc_done = cV.n

                    ev_done = []
                    for mt in range(NG):
                        def f(Te, tc=tc, mt=mt, ws=ws, hc_done=hc_done):
                            if mt == 0:
                                cV.wait(Te, hc_done)
                            if mt >= 2:
                                cV.wait(Te, ev_done[mt - 2])
                            pd = pool[6] if mt % 2 == 0 else pool[7]
                            for r in range(4):
                                Te.matmul(
                                    pd[:, :],
                                    ws[:, 8192 + r * 1024 + mt * 128: 8192 + r * 1024 + (mt + 1) * 128],
                                    hc[:, r * CH:(r + 1) * CH],
                                    start=(r == 0), stop=(r == 3))
                        P.tensor(f)
                        dmm = cT.n

                        def f(V, tc=tc, mt=mt, dmm=dmm, e=e, sh_dump_sS=sh_dump_sS):
                            if e == 0 and tc == 0 and mt == 0 and sh_dump_sS:
                                V.eng.wait_ge(sS.sem, sh_dump_sS)
                            cT.wait(V, dmm)
                            pd = pool[6] if mt % 2 == 0 else pool[7]
                            a = acc[:, mt * TH + tc * CH: mt * TH + (tc + 1) * CH]
                            V.tensor_tensor(a, a, pd[:, :], ALU.add)
                        P.vector(f)
                        ev_done.append(cV.n)

                def f(A, especs=especs):
                    for gmm, pg, k in especs:
                        A.eng.wait_ge(cT.sem, gmm)
                        A.eng.wait_ge(cV.sem, hmul_tgt[k - 2])
                        A.activation(out=silu_sb[k % 2][:, :], in_=pg[:, :],
                                     func=AF.Silu)
                P.scalar_f(f)
                assert cA.n == silu_tgt[-1], (cA.n, silu_tgt[-1])
                wfree[gi + 2] = cT.n

            experts_done = cV.n

            def f(sync, h=h, experts_done=experts_done):
                sync.wait_ge(cV.sem, experts_done)
                for g in range(NG):
                    sS.inc(sync.dma_start(
                        x2T_d.ap()[g * 128:(g + 1) * 128, h * TH:(h + 1) * TH],
                        acc[:, g * TH:(g + 1) * TH]), 16)
            P.sync(f)

        def f(sync):
            sync.wait_ge(sS.sem, sS.n)
        P.sync(f)
        pctx.close()




def build_B(T, debug=False):
    nc = bass.Bass("TRN2", target_bir_lowering=False)
    NHALF = 2
    TH = T // NHALF
    NTCH = TH // CH
    E = 64

    x1T_d = nc.dram_tensor("x1T", [D, T], F32, kind="ExternalInput")
    moe_d = nc.dram_tensor("moe_img", [E, 128, 12288], F32, kind="ExternalInput")
    sg_d = nc.dram_tensor("sg_img", [128, 8192], F32, kind="ExternalInput")
    su_d = nc.dram_tensor("su_img", [128, 8192], F32, kind="ExternalInput")
    sd_d = nc.dram_tensor("sd_img", [128, 8192], F32, kind="ExternalInput")
    gate_d = nc.dram_tensor("gate_img", [128, 512], F32, kind="ExternalInput")
    gbias_d = nc.dram_tensor("gate_bias", [64, 1], F32, kind="ExternalInput")
    sel_d = nc.dram_tensor("sel_img", [64, 8192], BF16, kind="ExternalInput")
    eyeF_d = nc.dram_tensor("eyeF", [128, 128], F32, kind="ExternalInput")
    x2T_d = nc.dram_tensor("x2T", [D, T], F32, kind="ExternalOutput")
    if debug:
        xn_dbg = nc.dram_tensor("xn_dbg", [128, NG * TH], BF16, kind="ExternalOutput")
        comb_dbg = nc.dram_tensor("comb_dbg", [64, T], BF16, kind="ExternalOutput")
        sh_dbg = nc.dram_tensor("sh_dbg", [D, TH], F32, kind="ExternalOutput")

    d = {"x1T": x1T_d, "moe_img": moe_d, "sg_img": sg_d, "su_img": su_d, "sd_img": sd_d, "gate_img": gate_d, "gate_bias": gbias_d, "sel_img": sel_d, "eyeF": eyeF_d, "x2T": x2T_d}
    if debug:
        d.update({"xn_dbg": xn_dbg, "comb_dbg": comb_dbg, "sh_dbg": sh_dbg})
    with ExitStack() as semctx:
        emit_B(nc, semctx, d, T, debug=debug)
    return nc


def emit_C(nc, semctx, d, T, debug=False):
    NCH = T // CH
    NW = 2
    x2T_d = d["x2T"]
    wk_d = d["wk_img"]
    wv_d = d["wv_img"]
    wr_d = d["wr_img"]
    wo_d = d["wo_img"]
    fwr_d = d["fwr_img"]
    fwk_d = d["fwk_img"]
    fwv_d = d["fwv_img"]
    vecs_d = d["vimgs"]
    out_d = d["outn"]
    k_dbg = d.get("k_dbg")
    y_dbg = d.get("y_dbg")
    xh_dbg = d.get("xh_dbg")
    x3_dbg = d.get("x3_dbg")
    with ExitStack() as ctx:
        sb = lambda name, shape, dt: ctx.enter_context(nc.sbuf_tensor(name + "_sb", shape, dt))

        GW = NG * CH
        Rb = sb("Rb", [128, GW], F32)
        K16 = sb("K16", [128, GW], F32)
        x2b = K16
        XH = sb("XH", [128, GW], F32)
        MIXB = [sb(f"mixb{j}", [128, GW], BF16) for j in range(3)]
        SR8 = sb("SR8", [128, GW], BF16)
        G8 = sb("G8", [128, GW], BF16)
        KK = sb("KK", [128, 32 * CH], BF16)
        V16 = KK[:, 0:8192].bitcast(F32)
        Lb = KK[:, 8192:16384].bitcast(F32)
        SU = sb("SU", [128, GW], U8)
        s1f = sb("s1f", [128, CH], F32)
        decays = sb("decays", [128, CH], F32)
        ones512 = sb("ones512", [128, CH], F32)
        zeros512 = sb("zeros512", [128, CH], F32)
        zeros8 = sb("zeros8", [128, 8], F32)
        sqscr = sb("sqscr", [128, CH], F32)
        scr = {nm: sb(nm, [128, CH], F32)
               for nm in ("ek", "ekB", "ekv", "Pp", "Qq", "cum", "Psh", "Qsh", "numv")}
        rowst = sb("rowstC", [1, 2 * CH], F32)
        vecs = sb("vecs", [128, 104], F32)
        edec = sb("edec", [128, 8], F32)
        Pc = sb("Pc", [128, 8], F32)
        Qc = sb("Qc", [128, 8], F32)
        v1s = sb("v1s", [128, 8], F32)
        v2s = sb("v2s", [128, 8], F32)
        tail1 = sb("tail1", [128, 8], F32)
        tail2 = sb("tail2", [128, 8], F32)
        onescol_f = sb("onescol_fc", [128, 1], F32)
        ones1x128 = sb("ones1x128c", [1, 128], F32)
        eyeF = sb("eyeFc", [128, 128], F32)
        wring = [sb(f"wring{j}", [128, 8192], BF16) for j in range(NW)]
        outnat = [sb(f"outnat{j}", [128, D], F32) for j in range(2)]

        P = Phase(nc, semctx, "C_", blk_ctx=ctx)
        sL, sS, cT, cV, cA = P.sL, P.sS, P.cT, P.cV, P.cA
        sG = P.sG
        sWr = [P.ct(f"sWr{j}") for j in range(NW)]
        sLc = [P.ct("sLc0"), P.ct("sLc1")]
        sSo = [P.ct("sSo0"), P.ct("sSo1")]

        pctx = ExitStack()
        pool = [pctx.enter_context(nc.psum_tensor(f"pc{j}_ps", [128, CH], F32))
                for j in range(8)]
        psSUM = pool[0][0:1, :]
        psSQ = pool[1][0:1, :]
        psM = pool[2][:, :]
        psRS = pool[3][:, :]

        ring_state = {"i": 0}
        ring_free_at = {}

        def load_piece(src_ap):
            i = ring_state["i"]
            slot = i % NW
            ring_state["i"] = i + 1

            def f(G, i=i, slot=slot, src_ap=src_ap):
                if i >= NW:
                    G.eng.wait_ge(cT.sem, ring_free_at[i - NW])
                sWr[slot].inc(G.eng.dma_start(wring[slot][:, :], src_ap), 16)
            P.gpsimd(f)
            return slot, sWr[slot].n, i

        def piece_done(i):
            ring_free_at[i] = cT.n

        def f(sync):
            for j in range(13):
                sL.inc(sync.dma_start(vecs[:, j * 8:(j + 1) * 8], vecs_d.ap()[j]), 16)
            sL.inc(sync.dma_start(eyeF[:, :], eyeF_dC.ap()), 16)
        eyeF_dC = nc.dram_tensor("eyeFc_in", [128, 128], F32, kind="ExternalInput")
        P.sync(f)
        init_ld = sL.n

        def f(V):
            V.memset(ones512[:, :], 1.0)
            V.memset(zeros512[:, :], 0.0)
            V.memset(zeros8[:, :], 0.0)
            V.memset(Pc[:, :], 0.0)
            V.memset(Qc[:, :], 0.0)
            V.memset(v1s[:, :], 0.0)
            V.memset(v2s[:, :], 0.0)
            V.memset(tail1[:, :], 0.0)
            V.memset(tail2[:, :], 0.0)
            V.memset(onescol_f[:, :], 1.0)
            V.memset(ones1x128[:, :], 1.0)
            V.eng.wait_ge(sL.sem, init_ld)
            V.tensor_scalar_add(vecs[:, 32:72], vecs[:, 32:72], -1.0)
        P.vector(f)
        memset_done = cV.n

        def f(A):
            A.eng.wait_ge(sL.sem, init_ld)
            cV.wait(A, memset_done)
            A.activation(out=edec[:, :], in_=vecs[:, 72:80], func=AF.Exp)
            A.activation(out=edec[:, :], in_=edec[:, :], func=AF.Exp, scale=-1.0)
        P.scalar(f)
        dc_done = cA.n

        decays_done = dc_done

        def emit_lnstats(srcb, sq_wait_extra=None):
            cv_src = cV.n
            sq_tgt = []
            mm_tgt = []
            an = cA.n
            for g in range(NG):
                an += 1
                sq_tgt.append(an)

                def f(Te, g=g):
                    cA.wait(Te, sq_tgt[g])
                    if g == 0:
                        cV.wait(Te)
                    Te.matmul(psSUM, onescol_f[:, :], srcb[:, g * CH:(g + 1) * CH],
                              start=(g == 0), stop=(g == NG - 1))
                    Te.matmul(psSQ, onescol_f[:, :], sqscr[:, :],
                              start=(g == 0), stop=(g == NG - 1))
                P.tensor(f)
                mm_tgt.append(cT.n)

            def f(A, cv_src=cv_src, mm_tgt=mm_tgt):
                for g in range(NG):
                    if g == 0:
                        cV.wait(A, cv_src)
                    else:
                        A.eng.wait_ge(cT.sem, mm_tgt[g - 1])
                    A.activation(out=sqscr[:, :], in_=srcb[:, g * CH:(g + 1) * CH],
                                 func=AF.Square)
            P.scalar_f(f)
            assert cA.n == sq_tgt[-1], (cA.n, sq_tgt[-1])
            sums_done = cT.n

            def f(V, sums_done=sums_done):
                cT.wait(V, sums_done)
                V.tensor_scalar_mul(rowst[:, 0:CH], psSUM, 1.0 / D)
                V.tensor_mul(rowst[:, CH:2 * CH], rowst[:, 0:CH], rowst[:, 0:CH])
                V.scalar_tensor_tensor(rowst[:, CH:2 * CH], psSQ, 1.0 / D,
                                       rowst[:, CH:2 * CH], ALU.mult, ALU.subtract)
                V.tensor_scalar_add(rowst[:, CH:2 * CH], rowst[:, CH:2 * CH], 1e-5)
            P.vector(f)
            vd = cV.n

            def f(A, vd=vd):
                cV.wait(A, vd)
                A.activation(out=rowst[:, CH:2 * CH], in_=rowst[:, CH:2 * CH], func=AF.Sqrt)
            P.scalar(f)
            sqd2 = cA.n

            def f(V, sqd2=sqd2):
                cA.wait(V, sqd2)
                V.reciprocal(rowst[:, CH:2 * CH], rowst[:, CH:2 * CH])
            P.vector(f)
            rd = cV.n

            def f(Te, rd=rd):
                cV.wait(Te, rd)
                Te.matmul(psM, ones1x128[:, :], rowst[:, 0:CH], start=True, stop=True)
                Te.matmul(psRS, ones1x128[:, :], rowst[:, CH:2 * CH], start=True, stop=True)
            P.tensor(f)
            return cT.n

        def mm8(dst_psum, lhs_slices, rhs_buf, extra_wait=None):
            def f(Te):
                if extra_wait is not None:
                    extra_wait(Te)
                for r in range(NG):
                    Te.matmul(dst_psum, lhs_slices[r], rhs_buf[:, r * CH:(r + 1) * CH],
                              start=(r == 0), stop=(r == NG - 1))
            P.tensor(f)
            return cT.n

        dumpstate = {"x3": 0}
        for tc in range(NCH):
            lsem = sLc[tc % 2]

            def f(sync, tc=tc, lsem=lsem):
                if tc >= 1:
                    sync.wait_ge(cV.sem, x2_used[tc - 1])
                    sync.wait_ge(cT.sem, trans_done[tc - 1])
                for g in range(NG):
                    lsem.inc(sync.dma_start(
                        x2b[:, g * CH:(g + 1) * CH],
                        x2T_d.ap()[g * 128:(g + 1) * 128, tc * CH:(tc + 1) * CH]), 16)
            if tc == 0:
                x2_used = {}
                trans_done = {}
            P.sync(f)
            x2_ld = lsem.n

            def f(Te, x2_ld=x2_ld, lsem=lsem, tc=tc):
                Te.eng.wait_ge(lsem.sem, x2_ld)
                if tc == 0:
                    cA.wait(Te, decays_done)
                    Te.eng.wait_ge(sL.sem, init_ld)
            P.tensor(f)

            def f(A, x2_ld=x2_ld, lsem=lsem):
                A.eng.wait_ge(lsem.sem, x2_ld)
            P.scalar(f)

            def f(V, x2_ld=x2_ld, lsem=lsem):
                V.eng.wait_ge(lsem.sem, x2_ld)
            P.vector(f)

            bc = emit_lnstats(x2b)

            def f(V, bc=bc):
                if dumpstate["x3"]:
                    V.eng.wait_ge(sS.sem, dumpstate["x3"])
                cT.wait(V, bc)
                for g in range(NG):
                    V.tensor_tensor(sqscr[:, :], x2b[:, g * CH:(g + 1) * CH], psM,
                                    ALU.subtract)
                    V.tensor_tensor(Rb[:, g * CH:(g + 1) * CH], sqscr[:, :], psRS,
                                    ALU.mult)
            P.vector(f)
            x2_used[tc] = cV.n
            w_done = cV.n

            bc = emit_lnstats(Rb)

            def f(V, bc=bc):
                cT.wait(V, bc)
                for g in range(NG):
                    V.tensor_tensor(sqscr[:, :], Rb[:, g * CH:(g + 1) * CH], psM,
                                    ALU.subtract)
                    V.tensor_tensor(sqscr[:, :], sqscr[:, :], psRS, ALU.mult)
                    V.tensor_scalar(Lb[:, g * CH:(g + 1) * CH], sqscr[:, :],
                                    vecs[:, 0 + g:1 + g], vecs[:, 8 + g:9 + g],
                                    ALU.mult, ALU.add)
            P.vector(f)
            ln1_done = cV.n

            def f(V, ln1_done=ln1_done):
                for g in range(NG):
                    L = Lb[:, g * CH:(g + 1) * CH]
                    dd = sqscr[:, :]
                    V.tensor_sub(dd[:, 1:CH], L[:, 1:CH], L[:, 0:CH - 1])
                    V.tensor_sub(dd[:, 0:1], L[:, 0:1], tail1[:, g:g + 1])
                    for mi, col in ((0, 32), (1, 40), (2, 48)):
                        V.scalar_tensor_tensor(MIXB[mi][:, g * CH:(g + 1) * CH],
                                               dd, vecs[:, col + g:col + g + 1], L,
                                               ALU.mult, ALU.add)
                    V.tensor_copy(tail1[:, g:g + 1], L[:, CH - 1:CH])
            P.vector(f)
            mix1_done = cV.n

            kvr_dst = (K16, V16, None)
            for wi, (wd, dstb) in enumerate(((wk_d, K16), (wv_d, V16), (wr_d, None))):
                for half in range(1):
                    pass
                slot, wn, pidx = load_piece(wd.ap())
                ev_done = []
                for mt in range(NG):
                    def f(Te, slot=slot, wn=wn, mt=mt, mix1_done=mix1_done, wi=wi):
                        Te.eng.wait_ge(sWr[slot].sem, wn)
                        if mt == 0:
                            cV.wait(Te, mix1_done)
                            cV.wait(Te)
                            cA.wait(Te)
                        if mt >= 2:
                            ct_ev, val = ev_done[mt - 2]
                            ct_ev.wait(Te, val)
                        pd = pool[4] if mt % 2 == 0 else pool[5]
                        for r in range(NG):
                            Te.matmul(pd[:, :],
                                      wring[slot][:, r * 1024 + mt * 128: r * 1024 + (mt + 1) * 128],
                                      MIXB[wi][:, r * CH:(r + 1) * CH],
                                      start=(r == 0), stop=(r == NG - 1))
                    P.tensor(f)
                    mmd = cT.n

                    if dstb is not None:
                        def f(V, mt=mt, mmd=mmd, dstb=dstb):
                            cT.wait(V, mmd)
                            pd = pool[4] if mt % 2 == 0 else pool[5]
                            V.tensor_copy(dstb[:, mt * CH:(mt + 1) * CH], pd[:, :])
                        P.vector(f)
                        ev_done.append((cV, cV.n))
                    else:
                        def f(A, mt=mt, mmd=mmd):
                            cT.wait(A, mmd)
                            pd = pool[4] if mt % 2 == 0 else pool[5]
                            A.activation(out=SR8[:, mt * CH:(mt + 1) * CH], in_=pd[:, :],
                                         func=AF.Sigmoid)
                        P.scalar(f)
                        ev_done.append((cA, cA.n))
                piece_done(pidx)
            kvr_done_v = cV.n
            kvr_done_a = cA.n
            slot_o, wn_o, pidx_o = load_piece(wo_d.ap())

            for g in range(NG):
                def f(A, g=g, kvr_done_v=kvr_done_v):
                    if g == 0:
                        cV.wait(A, kvr_done_v)
                    else:
                        cV.wait(A)
                    A.activation(out=scr["ek"][:, :], in_=K16[:, g * CH:(g + 1) * CH],
                                 func=AF.Exp)
                    A.activation(out=scr["ekB"][:, :], in_=K16[:, g * CH:(g + 1) * CH],
                                 func=AF.Exp, bias=vecs[:, 80 + g:81 + g])
                P.scalar(f)
                ek_done = cA.n

                def f(V, g=g, ek_done=ek_done):
                    cA.wait(V, ek_done)
                    V.tensor_single_scalar(decays[:, :], ones512[:, :],
                                           edec[:, g:g + 1], ALU.mult)
                    V.tensor_mul(scr["ekv"][:, :], scr["ek"][:, :],
                                 V16[:, g * CH:(g + 1) * CH])
                    V.eng.wait_ge(cV.sem, cV.n)
                    V.tensor_tensor_scan(scr["Pp"][:, :], decays[:, :],
                                         scr["ekv"][:, :], 0.0, ALU.mult, ALU.add)
                    V.tensor_tensor_scan(scr["Qq"][:, :], decays[:, :],
                                         scr["ek"][:, :], 0.0, ALU.mult, ALU.add)
                    V.tensor_tensor_scan(scr["cum"][:, :], decays[:, :],
                                         zeros512[:, :], 1.0, ALU.mult, ALU.add)
                    V.scalar_tensor_tensor(scr["Pp"][:, :], scr["cum"][:, :],
                                           Pc[:, g:g + 1], scr["Pp"][:, :],
                                           ALU.mult, ALU.add)
                    V.scalar_tensor_tensor(scr["Qq"][:, :], scr["cum"][:, :],
                                           Qc[:, g:g + 1], scr["Qq"][:, :],
                                           ALU.mult, ALU.add)
                    V.tensor_copy(scr["Psh"][:, 1:CH], scr["Pp"][:, 0:CH - 1])
                    V.tensor_copy(scr["Psh"][:, 0:1], Pc[:, g:g + 1])
                    V.tensor_copy(scr["Qsh"][:, 1:CH], scr["Qq"][:, 0:CH - 1])
                    V.tensor_copy(scr["Qsh"][:, 0:1], Qc[:, g:g + 1])
                    V.tensor_copy(Pc[:, g:g + 1], scr["Pp"][:, CH - 1:CH])
                    V.tensor_copy(Qc[:, g:g + 1], scr["Qq"][:, CH - 1:CH])
                    V.tensor_mul(scr["numv"][:, :], scr["ekB"][:, :],
                                 V16[:, g * CH:(g + 1) * CH])
                    V.tensor_add(scr["numv"][:, :], scr["numv"][:, :], scr["Psh"][:, :])
                    V.tensor_add(scr["Qsh"][:, :], scr["ekB"][:, :], scr["Qsh"][:, :])
                    V.reciprocal(scr["Qsh"][:, :], scr["Qsh"][:, :])
                    V.tensor_mul(scr["numv"][:, :], scr["numv"][:, :], scr["Qsh"][:, :])
                    V.tensor_mul(G8[:, g * CH:(g + 1) * CH], scr["numv"][:, :],
                                 SR8[:, g * CH:(g + 1) * CH])
                P.vector(f)
            wkv_done = cV.n

            if debug and tc == 0:
                def f(sync, wkv_done=wkv_done):
                    sync.wait_ge(cV.sem, wkv_done)
                    sS.inc(sync.dma_start(k_dbg.ap(), K16[:, :]), 16)
                    sS.inc(sync.dma_start(y_dbg.ap(), G8[:, :]), 16)
                P.sync(f)

            ev_done = []
            for mt in range(NG):
                def f(Te, mt=mt, slot_o=slot_o, wn_o=wn_o, wkv_done=wkv_done):
                    Te.eng.wait_ge(sWr[slot_o].sem, wn_o)
                    if mt == 0:
                        cV.wait(Te, wkv_done)
                    if mt >= 2:
                        cA.wait(Te, ev_done[mt - 2])
                    pd = pool[4] if mt % 2 == 0 else pool[5]
                    for r in range(NG):
                        Te.matmul(pd[:, :],
                                  wring[slot_o][:, r * 1024 + mt * 128: r * 1024 + (mt + 1) * 128],
                                  G8[:, r * CH:(r + 1) * CH],
                                  start=(r == 0), stop=(r == NG - 1))
                P.tensor(f)
                mmd = cT.n

                def f(A, mt=mt, mmd=mmd):
                    cT.wait(A, mmd)
                    pd = pool[4] if mt % 2 == 0 else pool[5]
                    A.activation(out=XH[:, mt * CH:(mt + 1) * CH], in_=pd[:, :],
                                 func=AF.Identity, scale=0.5)
                P.scalar(f)
                ev_done.append(cA.n)
            piece_done(pidx_o)
            xh_done = cA.n

            xh_dump_sS = 0
            if debug and tc == 0:
                def f(sync, xh_done=xh_done):
                    sync.wait_ge(cA.sem, xh_done)
                    sS.inc(sync.dma_start(xh_dbg.ap(), XH[:, :]), 16)
                P.sync(f)
                xh_dump_sS = sS.n

            xhv = XH[:, :].rearrange("p (g t) -> p g t", g=NG)

            def f(V, xh_done=xh_done, xh_dump_sS=xh_dump_sS):
                cA.wait(V, xh_done)
                if xh_dump_sS:
                    V.eng.wait_ge(sS.sem, xh_dump_sS)
                for t in range(CH):
                    V.scalar_tensor_tensor(xhv[:, :, t], v1s[:, :], 0.5, xhv[:, :, t],
                                           ALU.mult, ALU.add)
                    V.scalar_tensor_tensor(v1s[:, :], xhv[:, :, t], 1.0, xhv[:, :, t],
                                           ALU.is_lt, ALU.mult)
            P.vector(f)

            def f(V):
                for g in range(NG):
                    V.tensor_single_scalar(s1f[:, :], XH[:, g * CH:(g + 1) * CH],
                                           1.0, ALU.is_ge)
                    V.tensor_add(Rb[:, g * CH:(g + 1) * CH],
                                 Rb[:, g * CH:(g + 1) * CH], s1f[:, :])
            P.vector(f)
            xmid_done = cV.n

            bc = emit_lnstats(Rb)

            def f(V, bc=bc):
                cT.wait(V, bc)
                for g in range(NG):
                    V.tensor_tensor(sqscr[:, :], Rb[:, g * CH:(g + 1) * CH], psM,
                                    ALU.subtract)
                    V.tensor_tensor(sqscr[:, :], sqscr[:, :], psRS, ALU.mult)
                    V.tensor_scalar(Lb[:, g * CH:(g + 1) * CH], sqscr[:, :],
                                    vecs[:, 16 + g:17 + g], vecs[:, 24 + g:25 + g],
                                    ALU.mult, ALU.add)
            P.vector(f)

            def f(V):
                for g in range(NG):
                    L = Lb[:, g * CH:(g + 1) * CH]
                    dd = sqscr[:, :]
                    V.tensor_sub(dd[:, 1:CH], L[:, 1:CH], L[:, 0:CH - 1])
                    V.tensor_sub(dd[:, 0:1], L[:, 0:1], tail2[:, g:g + 1])
                    for mi, col in ((0, 56), (1, 64)):
                        V.scalar_tensor_tensor(MIXB[mi][:, g * CH:(g + 1) * CH],
                                               dd, vecs[:, col + g:col + g + 1], L,
                                               ALU.mult, ALU.add)
                    V.tensor_copy(tail2[:, g:g + 1], L[:, CH - 1:CH])
            P.vector(f)
            mix2_done = cV.n

            ev_done = []
            for pc in range(4):
                slot, wn, pidx = load_piece(fwk_d.ap()[:, pc * 8192:(pc + 1) * 8192])
                for mo in range(8):
                    mt = pc * 8 + mo

                    def f(Te, slot=slot, wn=wn, mo=mo, mt=mt, mix2_done=mix2_done):
                        Te.eng.wait_ge(sWr[slot].sem, wn)
                        if mt == 0:
                            cV.wait(Te, mix2_done)
                            cA.wait(Te)
                        if mt >= 2:
                            cA.wait(Te, ev_done[mt - 2])
                        pd = pool[4] if mt % 2 == 0 else pool[5]
                        for r in range(NG):
                            Te.matmul(pd[:, :],
                                      wring[slot][:, mo * 1024 + r * 128: mo * 1024 + (r + 1) * 128],
                                      MIXB[0][:, r * CH:(r + 1) * CH],
                                      start=(r == 0), stop=(r == NG - 1))
                    P.tensor(f)
                    mmd = cT.n

                    def f(A, mt=mt, mmd=mmd):
                        cT.wait(A, mmd)
                        pd = pool[4] if mt % 2 == 0 else pool[5]
                        A.activation(out=sqscr[:, :], in_=pd[:, :], func=AF.Relu)
                        A.activation(out=KK[:, mt * CH:(mt + 1) * CH], in_=sqscr[:, :],
                                     func=AF.Square)
                    P.scalar(f)
                    ev_done.append(cA.n)
                piece_done(pidx)
            kk_done = cA.n

            slot_r, wn_r, pidx_r = load_piece(fwr_d.ap())
            ev_done = []
            for mt in range(NG):
                def f(Te, slot_r=slot_r, wn_r=wn_r, mt=mt, kk_done=kk_done):
                    Te.eng.wait_ge(sWr[slot_r].sem, wn_r)
                    if mt == 0:
                        cA.wait(Te, kk_done)
                    if mt >= 2:
                        cA.wait(Te, ev_done[mt - 2])
                    pd = pool[4] if mt % 2 == 0 else pool[5]
                    for r in range(NG):
                        Te.matmul(pd[:, :],
                                  wring[slot_r][:, r * 1024 + mt * 128: r * 1024 + (mt + 1) * 128],
                                  MIXB[1][:, r * CH:(r + 1) * CH],
                                  start=(r == 0), stop=(r == NG - 1))
                P.tensor(f)
                mmd = cT.n

                def f(A, mt=mt, mmd=mmd):
                    cT.wait(A, mmd)
                    pd = pool[4] if mt % 2 == 0 else pool[5]
                    A.activation(out=SR8[:, mt * CH:(mt + 1) * CH], in_=pd[:, :],
                                 func=AF.Sigmoid)
                P.scalar(f)
                ev_done.append(cA.n)
            piece_done(pidx_r)
            csig_done = cA.n

            ev_done = []
            for pc in range(4):
                slot, wn, pidx = load_piece(fwv_d.ap()[:, pc * 8192:(pc + 1) * 8192])
                for mo in range(2):
                    mt = pc * 2 + mo

                    def f(Te, slot=slot, wn=wn, mo=mo, mt=mt, csig_done=csig_done):
                        Te.eng.wait_ge(sWr[slot].sem, wn)
                        if mt == 0:
                            cA.wait(Te, csig_done)
                        if mt >= 2:
                            cV.wait(Te, ev_done[mt - 2])
                        pd = pool[4] if mt % 2 == 0 else pool[5]
                        for r in range(32):
                            Te.matmul(pd[:, :],
                                      wring[slot][:, mo * 4096 + r * 128: mo * 4096 + (r + 1) * 128],
                                      KK[:, r * CH:(r + 1) * CH],
                                      start=(r == 0), stop=(r == 31))
                    P.tensor(f)
                    mmd = cT.n

                    def f(V, mt=mt, mmd=mmd, xh_dump_sS=xh_dump_sS):
                        if mt == 0 and xh_dump_sS:
                            V.eng.wait_ge(sS.sem, xh_dump_sS)
                        cT.wait(V, mmd)
                        pd = pool[4] if mt % 2 == 0 else pool[5]
                        V.scalar_tensor_tensor(XH[:, mt * CH:(mt + 1) * CH], pd[:, :],
                                               0.5, SR8[:, mt * CH:(mt + 1) * CH],
                                               ALU.mult, ALU.mult)
                    P.vector(f)
                    ev_done.append(cV.n)
                piece_done(pidx)
            xh2_done = cV.n

            def f(V, xh2_done=xh2_done):
                cV.wait(V, xh2_done)
                for t in range(CH):
                    V.scalar_tensor_tensor(xhv[:, :, t], v2s[:, :], 0.5, xhv[:, :, t],
                                           ALU.mult, ALU.add)
                    V.scalar_tensor_tensor(v2s[:, :], xhv[:, :, t], 1.0, xhv[:, :, t],
                                           ALU.is_lt, ALU.mult)
                for g in range(NG):
                    V.tensor_single_scalar(s1f[:, :], XH[:, g * CH:(g + 1) * CH],
                                           1.0, ALU.is_ge)
                    V.tensor_add(Rb[:, g * CH:(g + 1) * CH],
                                 Rb[:, g * CH:(g + 1) * CH], s1f[:, :])
            P.vector(f)

            lif2_done = cV.n

            def f(sync, lif2_done=lif2_done, tc=tc, lsem=lsem):
                sync.wait_ge(cV.sem, lif2_done)
                for g in range(NG):
                    lsem.inc(sync.dma_start(
                        XH[:, g * CH:(g + 1) * CH],
                        x2T_d.ap()[g * 128:(g + 1) * 128, tc * CH:(tc + 1) * CH]), 16)
            P.sync(f)
            x2re_ld = lsem.n

            def f(V, x2re_ld=x2re_ld, lsem=lsem):
                V.eng.wait_ge(lsem.sem, x2re_ld)
                for g in range(NG):
                    V.tensor_add(Rb[:, g * CH:(g + 1) * CH],
                                 Rb[:, g * CH:(g + 1) * CH], XH[:, g * CH:(g + 1) * CH])
            P.vector(f)
            x3_done = cV.n

            if debug and tc == 0:
                def f(sync, x3_done=x3_done):
                    sync.wait_ge(cV.sem, x3_done)
                    sS.inc(sync.dma_start(x3_dbg.ap(), Rb[:, :]), 16)
                P.sync(f)
                dumpstate["x3"] = sS.n

            bc = emit_lnstats(Rb)

            def f(V, bc=bc):
                cT.wait(V, bc)
                for g in range(NG):
                    V.tensor_tensor(sqscr[:, :], Rb[:, g * CH:(g + 1) * CH], psM,
                                    ALU.subtract)
                    V.tensor_tensor(Lb[:, g * CH:(g + 1) * CH], sqscr[:, :], psRS,
                                    ALU.mult)
            P.vector(f)
            u_done = cV.n

            bc = emit_lnstats(Lb, sq_wait_extra=u_done)

            def f(V, bc=bc):
                cT.wait(V, bc)
                for g in range(NG):
                    V.tensor_tensor(sqscr[:, :], Lb[:, g * CH:(g + 1) * CH], psM,
                                    ALU.subtract)
                    V.tensor_tensor(sqscr[:, :], sqscr[:, :], psRS, ALU.mult)
                    V.tensor_scalar(sqscr[:, :], sqscr[:, :],
                                    vecs[:, 88 + g:89 + g], vecs[:, 96 + g:97 + g],
                                    ALU.mult, ALU.add)
                    V.tensor_add(K16[:, g * CH:(g + 1) * CH],
                                 Rb[:, g * CH:(g + 1) * CH], sqscr[:, :])
            P.vector(f)
            out_done = cV.n

            tpcp_trail = []
            for tt in range(CH // 128):
                ob = outnat[tt % 2]
                osem = sSo[tt % 2]
                for g in range(NG):
                    pd = pool[6] if g % 2 == 0 else pool[7]

                    def f(Te, tt=tt, g=g, pd=pd, out_done=out_done):
                        if tt == 0 and g == 0:
                            cV.wait(Te, out_done)
                        if len(tpcp_trail) >= 2:
                            Te.eng.wait_ge(cV.sem, tpcp_trail[-2])
                        Te.transpose(pd[:, 0:128],
                                     K16[:, g * CH + tt * 128: g * CH + (tt + 1) * 128],
                                     eyeF[:, :])
                    P.tensor(f)
                    tpd = cT.n

                    def f(V, g=g, pd=pd, tpd=tpd, ob=ob, osem=osem):
                        cT.wait(V, tpd)
                        if g == 0:
                            V.eng.wait_ge(osem.sem, osem.n)
                        V.tensor_copy(ob[:, g * 128:(g + 1) * 128], pd[:, 0:128])
                        tpcp_trail.append(cV.n)
                    P.vector(f)
                tcp_done = cV.n

                def f(sync, tt=tt, tc=tc, ob=ob, osem=osem, tcp_done=tcp_done):
                    sync.wait_ge(cV.sem, tcp_done)
                    osem.inc(sync.dma_start(
                        out_d.ap()[tc * CH + tt * 128: tc * CH + (tt + 1) * 128, :],
                        ob[:, :]), 16)
                P.sync(f)
            trans_done[tc] = cT.n

        def f(sync):
            for j in range(2):
                sync.wait_ge(sSo[j].sem, sSo[j].n)
            sync.wait_ge(sS.sem, sS.n)
        P.sync(f)
        pctx.close()




def build_C(T, debug=False):
    nc = bass.Bass("TRN2", target_bir_lowering=False)
    NCH = T // CH
    NW = 2

    x2T_d = nc.dram_tensor("x2T", [D, T], F32, kind="ExternalInput")
    wk_d = nc.dram_tensor("wk_img", [128, 8192], F32, kind="ExternalInput")
    wv_d = nc.dram_tensor("wv_img", [128, 8192], F32, kind="ExternalInput")
    wr_d = nc.dram_tensor("wr_img", [128, 8192], F32, kind="ExternalInput")
    wo_d = nc.dram_tensor("wo_img", [128, 8192], F32, kind="ExternalInput")
    fwr_d = nc.dram_tensor("fwr_img", [128, 8192], F32, kind="ExternalInput")
    fwk_d = nc.dram_tensor("fwk_img", [128, 32768], F32, kind="ExternalInput")
    fwv_d = nc.dram_tensor("fwv_img", [128, 32768], F32, kind="ExternalInput")
    vecs_d = nc.dram_tensor("vimgs", [13, 128, 8], F32, kind="ExternalInput")
    out_d = nc.dram_tensor("outn", [T, D], F32, kind="ExternalOutput")
    if debug:
        k_dbg = nc.dram_tensor("k_dbg", [128, NG * CH], F32, kind="ExternalOutput")
        y_dbg = nc.dram_tensor("y_dbg", [128, NG * CH], BF16, kind="ExternalOutput")
        xh_dbg = nc.dram_tensor("xh_dbg", [128, NG * CH], F32, kind="ExternalOutput")
        x3_dbg = nc.dram_tensor("x3_dbg", [128, NG * CH], F32, kind="ExternalOutput")

    d = {"x2T": x2T_d, "wk_img": wk_d, "wv_img": wv_d, "wr_img": wr_d, "wo_img": wo_d, "fwr_img": fwr_d, "fwk_img": fwk_d, "fwv_img": fwv_d, "vimgs": vecs_d, "outn": out_d}
    if debug:
        d.update({"k_dbg": k_dbg, "y_dbg": y_dbg, "xh_dbg": xh_dbg, "x3_dbg": x3_dbg})
    with ExitStack() as semctx:
        emit_C(nc, semctx, d, T, debug=debug)
    return nc


def _consts_A():
    eyeF = np.eye(128, dtype=np.float32)
    eyeB = np.eye(128).astype(BF)
    expand16 = np.kron(np.eye(16, dtype=np.float32), np.ones((1, 64), np.float32))
    trils = np.zeros((128, 4 * CH), np.float32)
    for j in range(4):
        p = np.arange(128)[:, None]
        q = np.arange(CH)[None, :]
        trils[:, j * CH:(j + 1) * CH] = (j * 128 + p <= q)
    return eyeF, eyeB, expand16, trils.astype(BF)


def _in_map_A(ins, x_full, b, T, consts):
    eyeF, eyeB, expand16, trils = consts
    wpdup = np.concatenate(
        [np.asarray(ins["L0_attn_W_pred"]).transpose(1, 0, 2).reshape(64, 1024)] * 2,
        axis=0).astype(np.float32)
    return {
        "x0": np.ascontiguousarray(x_full[b, :T], dtype=np.float32),
        "w_k": np.asarray(ins["L0_attn_W_K"], np.float32),
        "w_q": np.asarray(ins["L0_attn_W_Q"], np.float32),
        "w_v": np.asarray(ins["L0_attn_W_V"], np.float32),
        "wpdup": wpdup, "eyeF": eyeF, "eyeB": eyeB, "expand16": expand16,
        "trils": trils,
    }


def run_A(x_full, ins, T, n_cores=NCORES, debug=False):
    consts = _consts_A()
    nc = build_A(T, debug=debug)
    in_maps = [_in_map_A(ins, x_full, b, T, consts) for b in range(n_cores)]
    res = run_bass_kernel_spmd(nc, in_maps, core_ids=list(range(n_cores)))
    if debug:
        return res.results
    return np.stack([r["x1T"] for r in res.results])


def _prep_B(ins):
    eg = np.asarray(ins["L0_ffn_eg_w"], np.float32)
    eu = np.asarray(ins["L0_ffn_eu_w"], np.float32)
    ed = np.asarray(ins["L0_ffn_ed_w"], np.float32)
    E = eg.shape[0]
    moe = np.empty((E, 128, 12288), np.float32)
    for e in range(E):
        moe[e, :, 0:4096] = _img(eg[e].T)
        moe[e, :, 4096:8192] = _img(eu[e].T)
        moe[e, :, 8192:12288] = _img(ed[e].T)
    sg = _img(np.asarray(ins["L0_ffn_sg_w"], np.float32).T)
    su = _img(np.asarray(ins["L0_ffn_su_w"], np.float32).T)
    sd = _img(np.asarray(ins["L0_ffn_sd_w"], np.float32).T)
    gate = _img(np.asarray(ins["L0_ffn_gate_w"], np.float32).T)
    gbias = np.ascontiguousarray(
        np.asarray(ins["L0_ffn_gate_bias"], np.float32).reshape(64, 1))
    sel = np.kron(np.eye(64), np.ones((1, 128))).astype(BF)
    return {"moe_img": moe, "sg_img": sg, "su_img": su, "sd_img": sd,
            "gate_img": gate, "gate_bias": gbias, "sel_img": sel,
            "eyeF": np.eye(128, dtype=np.float32)}


def run_B(x1T_all, ins, T, n_cores=NCORES, debug=False):
    base = _prep_B(ins)
    nc = build_B(T, debug=debug)
    in_maps = []
    for b in range(n_cores):
        m = dict(base)
        m["x1T"] = np.ascontiguousarray(x1T_all[b], dtype=np.float32)
        in_maps.append(m)
    res = run_bass_kernel_spmd(nc, in_maps, core_ids=list(range(n_cores)))
    if debug:
        return res.results
    return np.stack([r["x2T"] for r in res.results])


def _prep_C2(ins):
    g = lambda k: np.asarray(ins[k], np.float32)
    names = ["L1_attn_ln1_s", "L1_attn_ln1_b", "L1_attn_ln2_s", "L1_attn_ln2_b",
             "L1_attn_tmk", "L1_attn_tmv", "L1_attn_tmr", "L1_attn_f_tmk",
             "L1_attn_f_tmr", "L1_attn_w_decay", "L1_attn_u_first",
             "L1_ffn_ln_g", "L1_ffn_ln_b"]
    vimgs = np.stack([_vec_img(g(k)) for k in names])
    return {
        "wk_img": _img(g("L1_attn_Wk")), "wv_img": _img(g("L1_attn_Wv")),
        "wr_img": _img(g("L1_attn_Wr")), "wo_img": _img(g("L1_attn_Wo")),
        "fwr_img": _img(g("L1_attn_fWr")),
        "fwk_img": _img_mmajor(g("L1_attn_fWk")),
        "fwv_img": _img_mmajor(g("L1_attn_fWv")),
        "vimgs": vimgs,
        "eyeFc_in": np.eye(128, dtype=np.float32),
    }


def run_C(x2T_all, ins, T, n_cores=NCORES, debug=False):
    base = _prep_C2(ins)
    nc = build_C(T, debug=debug)
    in_maps = []
    for b in range(n_cores):
        m = dict(base)
        m["x2T"] = np.ascontiguousarray(x2T_all[b], dtype=np.float32)
        in_maps.append(m)
    res = run_bass_kernel_spmd(nc, in_maps, core_ids=list(range(n_cores)))
    if debug:
        return res.results
    return np.stack([r["outn"] for r in res.results])


def build_ALL(T):
    nc = bass.Bass("TRN2", target_bir_lowering=False)
    d = {}
    d["x0"] = nc.dram_tensor("x0", [T, D], F32, kind="ExternalInput")
    d["w_k"] = nc.dram_tensor("w_k", [16, 16], F32, kind="ExternalInput")
    d["w_q"] = nc.dram_tensor("w_q", [16, 16], F32, kind="ExternalInput")
    d["w_v"] = nc.dram_tensor("w_v", [16, 16], F32, kind="ExternalInput")
    d["wpdup"] = nc.dram_tensor("wpdup", [128, 1024], F32, kind="ExternalInput")
    d["eyeF"] = nc.dram_tensor("eyeF", [128, 128], F32, kind="ExternalInput")
    d["eyeB"] = nc.dram_tensor("eyeB", [128, 128], BF16, kind="ExternalInput")
    d["expand16"] = nc.dram_tensor("expand16", [16, 1024], F32, kind="ExternalInput")
    d["trils"] = nc.dram_tensor("trils", [128, 4 * CH], BF16, kind="ExternalInput")
    d["x1T"] = nc.dram_tensor("x1T", [D, T], F32, kind="Internal")
    d["zT_scratch"] = nc.dram_tensor("zT_scratch", [D, T], BF16, kind="Internal")
    d["moe_img"] = nc.dram_tensor("moe_img", [64, 128, 12288], F32, kind="ExternalInput")
    d["sg_img"] = nc.dram_tensor("sg_img", [128, 8192], F32, kind="ExternalInput")
    d["su_img"] = nc.dram_tensor("su_img", [128, 8192], F32, kind="ExternalInput")
    d["sd_img"] = nc.dram_tensor("sd_img", [128, 8192], F32, kind="ExternalInput")
    d["gate_img"] = nc.dram_tensor("gate_img", [128, 512], F32, kind="ExternalInput")
    d["gate_bias"] = nc.dram_tensor("gate_bias", [64, 1], F32, kind="ExternalInput")
    d["sel_img"] = nc.dram_tensor("sel_img", [64, 8192], BF16, kind="ExternalInput")
    d["x2T"] = nc.dram_tensor("x2T", [D, T], F32, kind="Internal")
    d["wk_img"] = nc.dram_tensor("wk_img", [128, 8192], F32, kind="ExternalInput")
    d["wv_img"] = nc.dram_tensor("wv_img", [128, 8192], F32, kind="ExternalInput")
    d["wr_img"] = nc.dram_tensor("wr_img", [128, 8192], F32, kind="ExternalInput")
    d["wo_img"] = nc.dram_tensor("wo_img", [128, 8192], F32, kind="ExternalInput")
    d["fwr_img"] = nc.dram_tensor("fwr_img", [128, 8192], F32, kind="ExternalInput")
    d["fwk_img"] = nc.dram_tensor("fwk_img", [128, 32768], F32, kind="ExternalInput")
    d["fwv_img"] = nc.dram_tensor("fwv_img", [128, 32768], F32, kind="ExternalInput")
    d["vimgs"] = nc.dram_tensor("vimgs", [13, 128, 8], F32, kind="ExternalInput")
    d["outn"] = nc.dram_tensor("outn", [T, D], F32, kind="ExternalOutput")
    with ExitStack() as semctx:
        emit_A(nc, semctx, d, T)
        emit_B(nc, semctx, d, T)
        emit_C(nc, semctx, d, T)
    return nc


def kernel(**ins):
    x0 = np.asarray(ins["L0_attn_x"], dtype=np.float32)
    T = x0.shape[1]
    nc = build_ALL(T)
    consts = _consts_A()
    baseB = _prep_B(ins)
    baseC = _prep_C2(ins)
    del baseC["eyeFc_in"]
    in_maps = []
    for b in range(NCORES):
        m = dict(baseB)
        m.update(baseC)
        m.update(_in_map_A(ins, x0, b, T, consts))
        m["eyeFc_in"] = consts[0]
        in_maps.append(m)
    res = run_bass_kernel_spmd(nc, in_maps, core_ids=list(range(NCORES)))
    return np.stack([r["outn"] for r in res.results])


# version 07 (final)
# speedup vs baseline: 1.8156x; 1.0331x over previous
import numpy as np
import ml_dtypes
from contextlib import ExitStack

import concourse.bass as bass
import concourse.mybir as mybir
from concourse.bass_utils import run_bass_kernel_spmd

F32 = mybir.dt.float32
BF16 = mybir.dt.bfloat16
U8 = mybir.dt.uint8
AF = mybir.ActivationFunctionType
ALU = mybir.AluOpType
AX = mybir.AxisListType

NCORES = 8
D = 1024
NG = 8
CH = 512
BF = ml_dtypes.bfloat16

_SER_OPS = {
    "tensor_copy", "tensor_tensor", "tensor_add", "tensor_sub", "tensor_mul",
    "tensor_max", "tensor_scalar", "tensor_single_scalar", "tensor_scalar_add",
    "tensor_scalar_sub", "tensor_scalar_mul", "tensor_scalar_max", "tensor_scalar_min",
    "tensor_relu", "tensor_reduce", "reduce_max", "reduce_sum", "scalar_tensor_tensor",
    "tensor_tensor_scan", "memset", "memzero", "reciprocal", "select", "copy_predicated",
    "activation", "sign", "square", "sqrt", "mul", "copy", "add", "iota",
}
_PE_OPS = {"matmul", "transpose", "ldweights"}


class Ct:
    def __init__(self, ctx, nc, name):
        self.sem = ctx.enter_context(nc.semaphore(name))
        self.n = 0

    def inc(self, instr, k=1):
        instr.then_inc(self.sem, k)
        self.n += k
        return instr

    def wait(self, eng_or_ew, val=None):
        eng = eng_or_ew.eng if isinstance(eng_or_ew, EW) else eng_or_ew
        eng.wait_ge(self.sem, self.n if val is None else val)


class EW:
    def __init__(self, ct, serialize=True):
        self.ct = ct
        self.serialize = serialize
        self.eng = None

    def bind(self, eng):
        self.eng = eng
        return self

    def __getattr__(self, name):
        eng = object.__getattribute__(self, "eng")
        fn = getattr(eng, name)
        if name in _SER_OPS or name in _PE_OPS:
            ct = self.ct
            ser = self.serialize and name in _SER_OPS

            def wrap(*a, **k):
                if ser:
                    eng.wait_ge(ct.sem, ct.n)
                ins = fn(*a, **k)
                ct.inc(ins)
                return ins
            return wrap
        return fn


class Phase:
    def __init__(self, nc, ctx, prefix="", blk_ctx=None):
        self.nc = nc
        self.blk = (blk_ctx if blk_ctx is not None else ctx).enter_context(nc.Block())
        self.prefix = prefix
        self.sL = Ct(ctx, nc, prefix + "sL")
        self.sG = Ct(ctx, nc, prefix + "sG")
        self.sS = Ct(ctx, nc, prefix + "sS")
        self.cT = Ct(ctx, nc, prefix + "cT")
        self.cV = Ct(ctx, nc, prefix + "cV")
        self.cA = Ct(ctx, nc, prefix + "cA")
        self.cG = Ct(ctx, nc, prefix + "cG")
        self._ctx = ctx
        self._nc = nc
        self.T_ = EW(self.cT, serialize=False)
        self.V_ = EW(self.cV)
        self.A_ = EW(self.cA)
        self.G_ = EW(self.cG)
        self.Vf_ = EW(self.cV, serialize=False)
        self.Af_ = EW(self.cA, serialize=False)

    def ct(self, name):
        return Ct(self._ctx, self._nc, self.prefix + name)

    def tensor(self, f):
        self.blk.tensor(lambda eng: f(self.T_.bind(eng)))

    def vector(self, f):
        self.blk.vector(lambda eng: f(self.V_.bind(eng)))

    def vector_f(self, f):
        self.blk.vector(lambda eng: f(self.Vf_.bind(eng)))

    def scalar_f(self, f):
        self.blk.scalar(lambda eng: f(self.Af_.bind(eng)))

    def scalar(self, f):
        self.blk.scalar(lambda eng: f(self.A_.bind(eng)))

    def gpsimd(self, f):
        self.blk.gpsimd(lambda eng: f(self.G_.bind(eng)))

    def sync(self, f):
        self.blk.sync(f)


def _img(w):
    K_, M_ = w.shape
    return np.ascontiguousarray(
        w.reshape(K_ // 128, 128, M_).transpose(1, 0, 2).reshape(128, -1), dtype=np.float32)


def _img_mmajor(w):
    K_, M_ = w.shape
    t = w.reshape(K_ // 128, 128, M_ // 128, 128)
    return np.ascontiguousarray(t.transpose(1, 2, 0, 3).reshape(128, -1), dtype=np.float32)


def _apify(t):
    return t[:, :]


def _vec_img(v):
    return np.ascontiguousarray(np.asarray(v, np.float32).reshape(8, 128).T)



def emit_A(nc, semctx, d, T, debug=False):
    NT = T // 128
    NCH = T // CH
    QT = CH // 128
    x0_d = d["x0"]
    wk_d = d["w_k"]
    wq_d = d["w_q"]
    wv_d = d["w_v"]
    wp_d = d["wpdup"]
    eyeF_d = d["eyeF"]
    eyeB_d = d["eyeB"]
    exp16_d = d["expand16"]
    tril_d = d["trils"]
    x1T_d = d["x1T"]
    zT_d = d["zT_scratch"]
    xnT_dbg = d.get("xnT_dbg")
    mats_dbg = d.get("mats_dbg")
    kqv_dbg = d.get("kqv_dbg")
    zT_dbg = d.get("zT_dbg")
    with ExitStack() as ctx:
        sb = lambda name, shape, dt: ctx.enter_context(nc.sbuf_tensor(name + "_sb", shape, dt))

        xnT = sb("xnT", [128, NG * T], BF16)
        zstg = [sb(f"zstg{j}", [64, CH], BF16) for j in range(2)]
        zrow = sb("zrow", [128, D], BF16)
        eyeF = sb("eyeF", [128, 128], F32)
        eyeB = sb("eyeB", [128, 128], BF16)
        wp = sb("wp", [128, 1024], F32)
        exp16 = sb("exp16", [16, 1024], F32)
        trils = sb("trils", [128, 4 * CH], BF16)
        onescol_b = sb("onescol_b", [128, 1], BF16)
        ones1x64 = sb("ones1x64", [1, 64], F32)
        kT_i = sb("kT_i", [64, T], BF16)
        qT_i = sb("qT_i", [64, T], BF16)
        vT_i = sb("vT_i", [64, T], BF16)
        vnat = sb("vnat", [128, NT * 65], BF16)
        expS = [sb(f"expS{j}", [128, CH], BF16) for j in range(3)]
        rec_row = sb("rec_row", [1, CH], F32)
        ztmp = sb("ztmp", [64, CH], F32)
        mats = {nm: sb(f"mat_{nm}", [128, NG * 1024], BF16) for nm in ("K", "Q", "V")}
        pcols = {nm: sb(f"pcol_{nm}", [128, NG * 16], F32) for nm in ("K", "Q", "V")}
        wraw = {nm: sb(f"wraw_{nm}", [16, 16], F32) for nm in ("K", "Q", "V")}
        probsT = {nm: sb(f"probsT_{nm}", [16, 16], F32) for nm in ("K", "Q", "V")}
        scr16 = sb("scr16", [16, 16], F32)
        col16 = sb("col16", [16, 1], F32)
        x0nat = [sb(f"x0nat{j}", [128, D], F32) for j in range(2)]
        sqnat = sb("sqnat", [128, D], F32)
        statc = sb("statc", [128, 6], F32)
        xn_nat = sb("xn_nat", [128, D], BF16)
        x1row = [sb(f"x1row{j}", [128, D], F32) for j in range(2)]

        P = Phase(nc, semctx, "A_", blk_ctx=ctx)
        sL, sS, cT, cV, cA = P.sL, P.sS, P.cT, P.cV, P.cA
        sLx = [P.ct("sLx0"), P.ct("sLx1")]
        sSx = [P.ct("sSx0"), P.ct("sSx1")]
        sSz = [P.ct("sSz0"), P.ct("sSz1")]

        pctx = ExitStack()
        psf = lambda name, shape, dt=F32: pctx.enter_context(nc.psum_tensor(name + "_ps", shape, dt))
        psT0 = psf("psT0", [128, 128], BF16)
        psT1 = psf("psT1", [128, 128], BF16)
        psPC = psf("psPC", [128, 16], F32)
        ps16 = psf("ps16", [16, 16], F32)

        def f(sync):
            for dst, src in ((eyeF, eyeF_d), (eyeB, eyeB_d), (wp, wp_d),
                             (exp16, exp16_d), (trils, tril_d)):
                sL.inc(sync.dma_start(dst[:, :], src.ap()), 16)
            for nm, src in (("K", wk_d), ("Q", wq_d), ("V", wv_d)):
                sL.inc(sync.dma_start(wraw[nm][:, :], src.ap()), 16)
        P.sync(f)
        consts_loaded = sL.n

        def f(V):
            V.memset(onescol_b[:, :], 1.0)
            V.memset(ones1x64[:, :], 1.0)
            V.memset(vnat[:, :].rearrange("p (n c) -> p n c", c=65)[:, :, 64], 1.0)
        P.vector(f)

        tp1_trail = []
        tile_done = {}
        for i in range(NT):
            xb = x0nat[i % 2]

            lsem = sLx[i % 2]

            def f(sync, i=i, xb=xb, lsem=lsem):
                if i >= 2:
                    sync.wait_ge(cV.sem, tile_done[i - 2])
                lsem.inc(sync.dma_start(xb[:, :], x0_d.ap()[i * 128:(i + 1) * 128, :]), 16)
            P.sync(f)
            ld_now = lsem.n

            def f(A, xb=xb, ld_now=ld_now, lsem=lsem):
                A.eng.wait_ge(lsem.sem, ld_now)
                A.activation(out=sqnat[:, :], in_=xb[:, :], func=AF.Square)
            P.scalar(f)
            sq_done = cA.n

            def f(V, xb=xb, ld_now=ld_now, sq_done=sq_done, lsem=lsem):
                V.eng.wait_ge(lsem.sem, ld_now)
                V.tensor_reduce(statc[:, 0:1], xb[:, :], AX.X, ALU.add)
                cA.wait(V, sq_done)
                V.tensor_reduce(statc[:, 1:2], sqnat[:, :], AX.X, ALU.add)
                V.tensor_scalar_mul(statc[:, 2:3], statc[:, 0:1], 1.0 / D)
                V.tensor_mul(statc[:, 3:4], statc[:, 2:3], statc[:, 2:3])
                V.scalar_tensor_tensor(statc[:, 4:5], statc[:, 1:2], 1.0 / D,
                                       statc[:, 3:4], ALU.mult, ALU.subtract)
                V.tensor_scalar_add(statc[:, 4:5], statc[:, 4:5], 1e-5)
            P.vector(f)
            var_done = cV.n

            def f(A, var_done=var_done):
                cV.wait(A, var_done)
                A.activation(out=statc[:, 5:6], in_=statc[:, 4:5], func=AF.Sqrt)
            P.scalar(f)
            sqrt_done = cA.n

            def f(V, xb=xb, sqrt_done=sqrt_done):
                cA.wait(V, sqrt_done)
                V.reciprocal(statc[:, 4:5], statc[:, 5:6])
                V.tensor_scalar(xn_nat[:, :], xb[:, :], statc[:, 2:3],
                                statc[:, 4:5], ALU.subtract, ALU.mult)
            P.vector(f)
            xn_done = cV.n

            for g in range(NG):
                pst = psT0 if g % 2 == 0 else psT1

                def f(Te, g=g, pst=pst, xn_done=xn_done, i=i):
                    if i == 0 and g == 0:
                        Te.eng.wait_ge(sL.sem, consts_loaded)
                    cV.wait(Te, xn_done)
                    if len(tp1_trail) >= 2:
                        Te.eng.wait_ge(cV.sem, tp1_trail[-2])
                    Te.transpose(pst[:, :], xn_nat[:, g * 128:(g + 1) * 128], eyeB[:, :])
                P.tensor(f)
                tpn = cT.n

                def f(V, g=g, i=i, pst=pst, tpn=tpn):
                    cT.wait(V, tpn)
                    V.tensor_copy(xnT[:, g * T + i * 128: g * T + (i + 1) * 128], pst[:, :])
                    tp1_trail.append(cV.n)
                P.vector(f)
            tile_done[i] = cV.n

        for nm in ("K", "Q", "V"):
            def f(V, nm=nm):
                V.eng.wait_ge(sL.sem, consts_loaded)
                V.tensor_reduce(col16[:, :], wraw[nm][:, :], AX.X, ALU.max)
                V.tensor_scalar_mul(col16[:, :], col16[:, :], -1.0)
            P.vector(f)
            neg_done = cV.n

            def f(A, nm=nm, neg_done=neg_done):
                A.eng.wait_ge(sL.sem, consts_loaded)
                cV.wait(A, neg_done)
                A.activation(out=scr16[:, :], in_=wraw[nm][:, :], func=AF.Exp,
                             bias=col16[:, :], accum_out=col16[:, :])
            P.scalar(f)
            exp_done = cA.n

            def f(V, nm=nm, exp_done=exp_done):
                cA.wait(V, exp_done)
                V.reciprocal(col16[:, :], col16[:, :])
                V.tensor_single_scalar(scr16[:, :], scr16[:, :], col16[:, :], ALU.mult)
            P.vector(f)
            probs_done = cV.n

            def f(Te, probs_done=probs_done):
                cV.wait(Te, probs_done)
                Te.transpose(ps16[:, :], scr16[:, :], eyeF[0:16, 0:16])
            P.tensor(f)
            t16 = cT.n

            def f(V, nm=nm, t16=t16):
                cT.wait(V, t16)
                V.tensor_copy(probsT[nm][:, :], ps16[:, :])
            P.vector(f)
            pc_cp = cV.n

            for r in range(NG):
                def f(Te, nm=nm, r=r, pc_cp=pc_cp):
                    cV.wait(Te, pc_cp)
                    Te.matmul(psPC[:, :], exp16[:, r * 128:(r + 1) * 128],
                              probsT[nm][:, :], start=True, stop=True)
                P.tensor(f)
                pc_mm = cT.n

                def f(V, nm=nm, r=r, pc_mm=pc_mm):
                    cT.wait(V, pc_mm)
                    V.tensor_copy(pcols[nm][:, r * 16:(r + 1) * 16], psPC[:, :])
                P.vector(f)
                pc_cp = cV.n

        def f(V):
            for nm in ("K", "Q", "V"):
                for v in range(16):
                    pb = (v % 2) * 64
                    rr = v // 2
                    for i in range(16):
                        if nm == "Q":
                            src = wp[pb:pb + 64, i * 64:(i + 1) * 64]
                        else:
                            src = eyeF[pb:pb + 64, pb:pb + 64]
                        V.tensor_single_scalar(
                            mats[nm][pb:pb + 64, rr * 1024 + i * 64: rr * 1024 + (i + 1) * 64],
                            src, pcols[nm][pb:pb + 64, rr * 16 + i: rr * 16 + i + 1], ALU.mult)
        P.vector(f)
        mats_done = cV.n

        if debug:
            def f(sync):
                sync.wait_ge(cV.sem, mats_done)
                sS.inc(sync.dma_start(xnT_dbg.ap(), xnT[:, :]), 16)
                for mi, nm in enumerate(("K", "Q", "V")):
                    sS.inc(sync.dma_start(
                        mats_dbg.ap()[:, mi * NG * 1024:(mi + 1) * NG * 1024],
                        mats[nm][:, :]), 16)
            P.sync(f)

        pctx.close()
        pctx = ExitStack()
        psf = lambda name, shape, dt=F32: pctx.enter_context(nc.psum_tensor(name + "_ps", shape, dt))
        psKQV0 = psf("psKQV0", [64, CH])
        psKQV1 = psf("psKQV1", [64, CH])
        psTB0 = psf("psTB0", [128, 64], BF16)
        psTB1 = psf("psTB1", [128, 64], BF16)
        psS0 = psf("psS0", [128, CH])
        psS1 = psf("psS1", [128, CH])
        psS2 = psf("psS2", [128, CH])
        psZ = psf("psZ", [65, CH])

        evac_trail = []
        vt_trail = []
        zfin_trail = []
        kqv_dump_sS = None
        for hi in range(16):
            for pi, (nm, dst) in enumerate((("K", kT_i), ("Q", qT_i), ("V", vT_i))):
                for tc in range(NCH):
                    j = (pi * NCH + tc) % 2
                    pskv = psKQV0 if j == 0 else psKQV1

                    def f(Te, nm=nm, tc=tc, pskv=pskv, pi=pi, hi=hi):
                        if hi == 0 and pi == 0 and tc == 0:
                            cV.wait(Te, mats_done)
                        if len(evac_trail) >= 2:
                            Te.eng.wait_ge(cV.sem, evac_trail[-2])
                        for r in range(NG):
                            Te.matmul(
                                pskv[:, :],
                                mats[nm][:, r * 1024 + hi * 64: r * 1024 + (hi + 1) * 64],
                                xnT[:, r * T + tc * CH: r * T + (tc + 1) * CH],
                                start=(r == 0), stop=(r == NG - 1))
                    P.tensor(f)
                    mm_done = cT.n

                    def f(V, dst=dst, tc=tc, pskv=pskv, mm_done=mm_done,
                          hi=hi, pi=pi, kqv_dump_sS=kqv_dump_sS):
                        if hi == 1 and pi == 0 and tc == 0 and kqv_dump_sS is not None:
                            V.eng.wait_ge(sS.sem, kqv_dump_sS)
                        cT.wait(V, mm_done)
                        V.tensor_copy(dst[:, tc * CH:(tc + 1) * CH], pskv[:, :])
                        evac_trail.append(cV.n)
                    P.vector(f)

            if debug and hi == 0:
                kqv_done0 = cV.n

                def f(sync, kqv_done0=kqv_done0):
                    sync.wait_ge(cV.sem, kqv_done0)
                    for src_t, off in ((kT_i, 0), (qT_i, T), (vT_i, 2 * T)):
                        sS.inc(sync.dma_start(kqv_dbg.ap()[:, off:off + T], src_t[:, :]), 16)
                P.sync(f)
                kqv_dump_sS = sS.n

            kqv_done = cV.n
            for pt in range(NT):
                pst = psTB0 if pt % 2 == 0 else psTB1

                def f(Te, pt=pt, pst=pst, kqv_done=kqv_done):
                    if pt == 0:
                        cV.wait(Te, kqv_done)
                    if len(vt_trail) >= 2:
                        Te.eng.wait_ge(cV.sem, vt_trail[-2])
                    Te.transpose(pst[:, :], vT_i[:, pt * 128:(pt + 1) * 128], eyeB[0:64, 0:64])
                P.tensor(f)
                vt_mm = cT.n

                def f(V, pt=pt, pst=pst, vt_mm=vt_mm):
                    cT.wait(V, vt_mm)
                    V.tensor_copy(vnat[:, pt * 65:pt * 65 + 64], pst[:, :])
                    vt_trail.append(cV.n)
                P.vector_f(f)

            vnat_done = cV.n
            ZLAG = 2
            for qc in range(NCH):
                np_tiles = (qc + 1) * QT
                cA_base = cA.n
                exp_specs = []
                smm_list = []
                zmm_list = []
                zspecs = []

                def emit_z(spec, np_tiles=np_tiles):
                    pt, eb, diag, e_done, m_done = spec

                    def f(Te, pt=pt, eb=eb, diag=diag, np_tiles=np_tiles,
                          e_done=e_done, m_done=m_done):
                        if diag:
                            cV.wait(Te, m_done)
                        else:
                            cA.wait(Te, e_done)
                        if pt == 0 and zfin_trail:
                            Te.eng.wait_ge(cV.sem, zfin_trail[-1])
                        Te.matmul(psZ[:, :], vnat[:, pt * 65:pt * 65 + 65], eb[:, :],
                                  start=(pt == 0), stop=(pt == np_tiles - 1))
                    P.tensor(f)
                    zmm_list.append(cT.n)

                for pt in range(np_tiles):
                    pss = (psS0, psS1, psS2)[pt % 3]
                    eb = expS[pt % 3]
                    diag = pt >= qc * QT
                    e_done = cA_base + pt + 1

                    def f(Te, pt=pt, pss=pss, qc=qc, hi=hi, vnat_done=vnat_done,
                          e_done=e_done, cA_base=cA_base):
                        if qc == 0 and pt == 0:
                            cV.wait(Te, vnat_done)
                        if pt >= 3:
                            Te.eng.wait_ge(cA.sem, cA_base + pt - 2)
                        Te.matmul(pss[:, :], kT_i[:, pt * 128:(pt + 1) * 128],
                                  qT_i[:, qc * CH:(qc + 1) * CH], start=True, stop=True)
                    P.tensor(f)
                    smm_list.append(cT.n)
                    exp_specs.append((smm_list[-1], pss, eb, pt))

                    m_done = None
                    if diag:
                        def f(V, eb=eb, pt=pt, qc=qc, e_done=e_done):
                            cA.wait(V, e_done)
                            dpos = pt - qc * QT
                            V.tensor_mul(eb[:, :], eb[:, :], trils[:, dpos * CH:(dpos + 1) * CH])
                        P.vector(f)
                        m_done = cV.n

                    zspecs.append((pt, eb, diag, e_done, m_done))
                    if pt >= ZLAG:
                        emit_z(zspecs[pt - ZLAG])
                for pt in range(max(0, np_tiles - ZLAG), np_tiles):
                    emit_z(zspecs[pt])

                def f(A, exp_specs=exp_specs, zmm_list=zmm_list):
                    for smm, pss, eb, zi in exp_specs:
                        A.eng.wait_ge(cT.sem, smm)
                        if zi >= 3:
                            A.eng.wait_ge(cT.sem, zmm_list[zi - 3])
                        A.activation(out=eb[:, :], in_=pss[:, :], func=AF.Exp, scale=0.125)
                P.scalar_f(f)
                assert cA.n == cA_base + np_tiles

                zacc_done = cT.n

                def f(V, zacc_done=zacc_done):
                    cT.wait(V, zacc_done)
                    V.reciprocal(rec_row[:, :], psZ[64:65, :])
                    V.tensor_copy(ztmp[:, :], psZ[0:64, :])
                P.vector(f)
                ev_done = cV.n

                def f(Te, ev_done=ev_done):
                    cV.wait(Te, ev_done)
                    Te.matmul(psZ[0:64, :], ones1x64[:, :], rec_row[:, :], start=True, stop=True)
                P.tensor(f)
                bc_done = cT.n

                zslot = (hi * NCH + qc) % 2
                zs = zstg[zslot]
                zsem = sSz[zslot]

                def f(V, bc_done=bc_done, zs=zs, zsem=zsem):
                    cT.wait(V, bc_done)
                    V.eng.wait_ge(zsem.sem, zsem.n)
                    V.tensor_mul(zs[:, :], ztmp[:, :], psZ[0:64, :])
                    zfin_trail.append(cV.n)
                P.vector(f)
                zmul_done = cV.n

                def f(sync, qc=qc, hi=hi, zs=zs, zsem=zsem, zmul_done=zmul_done):
                    sync.wait_ge(cV.sem, zmul_done)
                    zsem.inc(sync.dma_start(
                        zT_d.ap()[hi * 64:(hi + 1) * 64, qc * CH:(qc + 1) * CH], zs[:, :]), 16)
                P.sync(f)

        attn_done = cV.n
        if debug:
            def f(sync):
                sync.wait_ge(sSz[0].sem, sSz[0].n)
                sync.wait_ge(sSz[1].sem, sSz[1].n)
                sS.inc(sync.dma_start(zT_dbg.ap(), zT_d.ap()), 16)
            P.sync(f)

        pctx.close()
        pctx = ExitStack()
        psf = lambda name, shape, dt=F32: pctx.enter_context(nc.psum_tensor(name + "_ps", shape, dt))
        psX0 = psf("psX0", [128, 128])
        psX1 = psf("psX1", [128, 128])

        add_trail = []
        row_done = {}
        st_issued = {}
        for i in range(NT):
            xb = x0nat[i % 2]
            xr = x1row[i % 2]

            lsem = sLx[i % 2]
            ssem = sSx[i % 2]

            def f(sync, i=i, xb=xb, lsem=lsem, ssem=ssem):
                if i == 0:
                    sync.wait_ge(cV.sem, attn_done)
                    sync.wait_ge(sSz[0].sem, sSz[0].n)
                    sync.wait_ge(sSz[1].sem, sSz[1].n)
                if i >= 2:
                    sync.wait_ge(ssem.sem, ssem.n)
                    sync.wait_ge(cV.sem, row_done[i - 2])
                lsem.inc(sync.dma_start(xb[:, :], x0_d.ap()[i * 128:(i + 1) * 128, :]), 16)
                if i >= 1:
                    sync.wait_ge(cV.sem, row_done[i - 1])
                for g in range(NG):
                    lsem.inc(sync.dma_start(
                        zrow[:, g * 128:(g + 1) * 128],
                        zT_d.ap()[g * 128:(g + 1) * 128, i * 128:(i + 1) * 128]), 16)
            P.sync(f)
            ld_now = lsem.n

            for g in range(NG):
                pst = psX0 if g % 2 == 0 else psX1

                def f(Te, g=g, pst=pst, xb=xb, ld_now=ld_now, lsem=lsem):
                    Te.eng.wait_ge(lsem.sem, ld_now)
                    if len(add_trail) >= 2:
                        Te.eng.wait_ge(cV.sem, add_trail[-2])
                    Te.transpose(pst[:, :], xb[:, g * 128:(g + 1) * 128], eyeF[:, :])
                P.tensor(f)
                x_mm = cT.n

                def f(V, g=g, i=i, pst=pst, xr=xr, x_mm=x_mm, ld_now=ld_now):
                    V.eng.wait_ge(lsem.sem, ld_now)
                    cT.wait(V, x_mm)
                    V.tensor_add(xr[:, g * 128:(g + 1) * 128], pst[:, :],
                                 zrow[:, g * 128:(g + 1) * 128])
                    add_trail.append(cV.n)
                P.vector(f)
            row_done[i] = cV.n

            def f(sync, i=i, xr=xr, ssem=ssem):
                sync.wait_ge(cV.sem, row_done[i])
                for g in range(NG):
                    ssem.inc(sync.dma_start(
                        x1T_d.ap()[g * 128:(g + 1) * 128, i * 128:(i + 1) * 128],
                        xr[:, g * 128:(g + 1) * 128]), 16)
            P.sync(f)

        def f(sync):
            sync.wait_ge(sSx[0].sem, sSx[0].n)
            sync.wait_ge(sSx[1].sem, sSx[1].n)
            sync.wait_ge(sS.sem, sS.n)
        P.sync(f)
        pctx.close()





def build_A(T, debug=False):
    nc = bass.Bass("TRN2", target_bir_lowering=False)
    NT = T // 128
    NCH = T // CH
    QT = CH // 128

    x0_d = nc.dram_tensor("x0", [T, D], F32, kind="ExternalInput")
    wk_d = nc.dram_tensor("w_k", [16, 16], F32, kind="ExternalInput")
    wq_d = nc.dram_tensor("w_q", [16, 16], F32, kind="ExternalInput")
    wv_d = nc.dram_tensor("w_v", [16, 16], F32, kind="ExternalInput")
    wp_d = nc.dram_tensor("wpdup", [128, 1024], F32, kind="ExternalInput")
    eyeF_d = nc.dram_tensor("eyeF", [128, 128], F32, kind="ExternalInput")
    eyeB_d = nc.dram_tensor("eyeB", [128, 128], BF16, kind="ExternalInput")
    exp16_d = nc.dram_tensor("expand16", [16, 1024], F32, kind="ExternalInput")
    tril_d = nc.dram_tensor("trils", [128, 4 * CH], BF16, kind="ExternalInput")
    x1T_d = nc.dram_tensor("x1T", [D, T], F32, kind="ExternalOutput")
    zT_d = nc.dram_tensor("zT_scratch", [D, T], BF16, kind="Internal")
    if debug:
        xnT_dbg = nc.dram_tensor("xnT_dbg", [128, NG * T], BF16, kind="ExternalOutput")
        mats_dbg = nc.dram_tensor("mats_dbg", [128, 3 * NG * 1024], BF16, kind="ExternalOutput")
        kqv_dbg = nc.dram_tensor("kqv_dbg", [64, 3 * T], BF16, kind="ExternalOutput")
        zT_dbg = nc.dram_tensor("zT_dbg", [D, T], BF16, kind="ExternalOutput")

    d = {"x0": x0_d, "w_k": wk_d, "w_q": wq_d, "w_v": wv_d, "wpdup": wp_d, "eyeF": eyeF_d, "eyeB": eyeB_d, "expand16": exp16_d, "trils": tril_d, "x1T": x1T_d, "zT_scratch": zT_d}
    if debug:
        d.update({"xnT_dbg": xnT_dbg, "mats_dbg": mats_dbg, "kqv_dbg": kqv_dbg, "zT_dbg": zT_dbg})
    with ExitStack() as semctx:
        emit_A(nc, semctx, d, T, debug=debug)
    return nc


def emit_B(nc, semctx, d, T, debug=False):
    NHALF = 2
    TH = T // NHALF
    NTCH = TH // CH
    E = 64
    x1T_d = d["x1T"]
    moe_d = d["moe_img"]
    sg_d = d["sg_img"]
    su_d = d["su_img"]
    sd_d = d["sd_img"]
    gate_d = d["gate_img"]
    gbias_d = d["gate_bias"]
    sel_d = d["sel_img"]
    eyeF_d = d["eyeF"]
    x2T_d = d["x2T"]
    xn_dbg = d.get("xn_dbg")
    comb_dbg = d.get("comb_dbg")
    sh_dbg = d.get("sh_dbg")
    with ExitStack() as ctx:
        sb = lambda name, shape, dt: ctx.enter_context(nc.sbuf_tensor(name + "_sb", shape, dt))

        acc = sb("acc", [128, NG * TH], F32)
        xnT = sb("xnTb", [128, NG * TH], BF16)
        combT = sb("combT", [64, TH], BF16)
        hs = sb("hs", [128, NG * CH], BF16)
        hc = sb("hc", [128, 4 * CH], BF16)
        h1 = sb("h1", [128, CH], BF16)
        silu_sb = [sb(f"silu{j}", [128, CH], BF16) for j in range(2)]
        wslot = [sb(f"wslot{j}", [128, 12288], BF16) for j in range(2)]
        selslot = [sb(f"selslot{j}", [64, 128], BF16) for j in range(2)]
        gatew = sb("gatew", [128, 512], F32)
        gsum = sb("gsum", [64, 1], F32)
        mrows = sb("mrows", [1, 2048], F32)
        rsrows = sb("rsrows", [1, 2048], F32)

        gbias = sb("gbias", [64, 1], F32)
        eyeF = sb("eyeFb", [128, 128], F32)
        onescol_f = sb("onescol_f", [128, 1], F32)
        ones1x128 = sb("ones1x128", [1, 128], F32)
        scrB = sb("scrB", [128, CH], F32)
        sq = scrB
        gmb = scrB[0:64, :]
        lgb = scrB[64:128, :]
        rowst = sb("rowst", [1, 3 * CH], F32)

        sc_tm = sb("sc_tm", [128, 4 * 64], F32)
        sgate = scrB[0:64, :]
        gr = _apify(sb("gr", [128, 8], F32))
        gso = _apify(sb("gso", [128, 8], F32))
        mcol = _apify(sb("mcol", [128, 1], F32))
        gmask = _apify(sb("gmask", [128, 8], F32))
        msk64 = _apify(sb("msk64", [128, 64], F32))
        msv = _apify(sb("msv", [128, 64], F32))
        mso = _apify(sb("mso", [128, 64], F32))
        eqv = _apify(sb("eqv", [128, 64], F32))
        cu = _apify(sb("cu", [128, 64], F32))
        dcol = _apify(sb("dcol", [128, 1], F32))
        comb_tm = _apify(sb("comb_tm", [128, 64], F32))

        P = Phase(nc, semctx, "B_", blk_ctx=ctx)
        sL, sS, cT, cV, cA = P.sL, P.sS, P.cT, P.cV, P.cA
        sG = P.sG
        sWs = [P.ct("sW0"), P.ct("sW1")]

        pctx = ExitStack()
        pool = [pctx.enter_context(nc.psum_tensor(f"pp{j}_ps", [128, CH], F32))
                for j in range(8)]
        psSUM = pool[0][0:1, :]
        psSQ = pool[1][0:1, :]
        psM = pool[2][:, :]
        psRS = pool[3][:, :]

        def f(sync):
            for dst, src in ((gbias, gbias_d), (eyeF, eyeF_d)):
                sL.inc(sync.dma_start(dst[:, :], src.ap()), 16)
            sL.inc(sync.dma_start(gatew[:, :], gate_d.ap()), 16)
        P.sync(f)

        consts_sL = sL.n

        def f(V):
            V.memset(onescol_f[:, :], 1.0)
            V.memset(ones1x128[:, :], 1.0)
        P.vector(f)
        memset_done = cV.n

        def f(Te):
            cV.wait(Te, memset_done)
            Te.eng.wait_ge(sL.sem, consts_sL)
            for r in range(NG):
                Te.matmul(pool[6][0:64, 0:1], gatew[:, r * 64:(r + 1) * 64],
                          onescol_f[:, :], start=(r == 0), stop=(r == NG - 1))
        P.tensor(f)
        gsum_mm = cT.n

        def f(V, gsum_mm=gsum_mm):
            cT.wait(V, gsum_mm)
            V.tensor_copy(gsum[:, :], pool[6][0:64, 0:1])
        P.vector(f)
        gsum_done = cV.n

        wfree = {}
        last_store = 0

        for h in range(NHALF):
            sLh = P.ct(f"sLh{h}")

            def f(sync, h=h, sLh=sLh):
                if h > 0:
                    sync.wait_ge(sS.sem, sS.n)
                for g in range(NG):
                    sLh.inc(sync.dma_start(
                        acc[:, g * TH:(g + 1) * TH],
                        x1T_d.ap()[g * 128:(g + 1) * 128, h * TH:(h + 1) * TH]), 16)
            P.sync(f)
            x1_ld = sLh.n

            prev_var_done = 0
            prev_xn_done = 0
            for tc in range(NTCH):
                sq_tgt = []
                mm_tgt = []
                an = cA.n
                for g in range(NG):
                    an += 1
                    sq_tgt.append(an)

                    def f(Te, tc=tc, g=g, x1_ld=x1_ld, sLh=sLh, h=h,
                          pvd=prev_var_done):
                        Te.eng.wait_ge(sLh.sem, x1_ld)
                        if h == 0 and tc == 0 and g == 0:
                            cV.wait(Te, memset_done)
                            Te.eng.wait_ge(sL.sem, consts_sL)
                        if g == 0 and pvd:
                            Te.eng.wait_ge(cV.sem, pvd)
                        cA.wait(Te, sq_tgt[g])
                        Te.matmul(psSUM, onescol_f[:, :],
                                  acc[:, g * TH + tc * CH: g * TH + (tc + 1) * CH],
                                  start=(g == 0), stop=(g == NG - 1))
                        Te.matmul(psSQ, onescol_f[:, :], sq[:, :],
                                  start=(g == 0), stop=(g == NG - 1))
                    P.tensor(f)
                    mm_tgt.append(cT.n)

                def f(A, tc=tc, x1_ld=x1_ld, sLh=sLh, pxd=prev_xn_done,
                      mm_tgt=mm_tgt):
                    for g in range(NG):
                        if g == 0:
                            A.eng.wait_ge(sLh.sem, x1_ld)
                            if pxd:
                                A.eng.wait_ge(cV.sem, pxd)
                        else:
                            A.eng.wait_ge(cT.sem, mm_tgt[g - 1])
                        A.activation(out=sq[:, :],
                                     in_=acc[:, g * TH + tc * CH: g * TH + (tc + 1) * CH],
                                     func=AF.Square)
                P.scalar_f(f)
                assert cA.n == sq_tgt[-1], (cA.n, sq_tgt[-1])
                sums_done = cT.n

                def f(V, sums_done=sums_done):
                    cT.wait(V, sums_done)
                    V.tensor_scalar_mul(rowst[:, 0:CH], psSUM, 1.0 / D)
                    V.tensor_mul(rowst[:, 2 * CH:3 * CH], rowst[:, 0:CH], rowst[:, 0:CH])
                    V.scalar_tensor_tensor(rowst[:, CH:2 * CH], psSQ, 1.0 / D,
                                           rowst[:, 2 * CH:3 * CH], ALU.mult, ALU.subtract)
                    V.tensor_scalar_add(rowst[:, CH:2 * CH], rowst[:, CH:2 * CH], 1e-5)
                P.vector(f)
                prev_var_done = var_done = cV.n

                def f(A, var_done=var_done):
                    cV.wait(A, var_done)
                    A.activation(out=rowst[:, 2 * CH:3 * CH], in_=rowst[:, CH:2 * CH],
                                 func=AF.Sqrt)
                P.scalar(f)
                sqrt_done = cA.n

                def f(V, sqrt_done=sqrt_done):
                    cA.wait(V, sqrt_done)
                    V.reciprocal(rowst[:, CH:2 * CH], rowst[:, 2 * CH:3 * CH])
                P.vector(f)
                rs_done = cV.n

                def f(Te, rs_done=rs_done):
                    cV.wait(Te, rs_done)
                    Te.matmul(psM, ones1x128[:, :], rowst[:, 0:CH], start=True, stop=True)
                    Te.matmul(psRS, ones1x128[:, :], rowst[:, CH:2 * CH],
                              start=True, stop=True)
                P.tensor(f)
                bc_done = cT.n

                def f(V, tc=tc, bc_done=bc_done):
                    cT.wait(V, bc_done)
                    V.tensor_copy(mrows[:, tc * CH:(tc + 1) * CH], rowst[:, 0:CH])
                    V.tensor_copy(rsrows[:, tc * CH:(tc + 1) * CH], rowst[:, CH:2 * CH])
                    for g in range(NG):
                        V.tensor_tensor(sq[:, :],
                                        acc[:, g * TH + tc * CH: g * TH + (tc + 1) * CH],
                                        psM, ALU.subtract)
                        V.tensor_tensor(xnT[:, g * TH + tc * CH: g * TH + (tc + 1) * CH],
                                        sq[:, :], psRS, ALU.mult)
                P.vector(f)
                prev_xn_done = cV.n
            xn_done_h = cV.n

            if debug and h == 0:
                def f(sync, xn_done_h=xn_done_h):
                    sync.wait_ge(cV.sem, xn_done_h)
                    sS.inc(sync.dma_start(xn_dbg.ap(), xnT[:, :]), 16)
                P.sync(f)

            last_sig = 0
            last_tr_cp = 0
            last_gm = 0
            for tc in range(NTCH):
                def f(Te, tc=tc, xn_done_h=xn_done_h, last_gm=last_gm, h=h):
                    cV.wait(Te, xn_done_h)
                    if h == 0 and tc == 0:
                        cV.wait(Te, gsum_done)
                    if last_gm:
                        Te.eng.wait_ge(cV.sem, last_gm)
                    for r in range(NG):
                        Te.matmul(pool[4][0:64, :], gatew[:, r * 64:(r + 1) * 64],
                                  acc[:, r * TH + tc * CH: r * TH + (tc + 1) * CH],
                                  start=(r == 0), stop=(r == NG - 1))
                    Te.matmul(pool[2][0:64, :], ones1x128[:, 0:64],
                              mrows[:, tc * CH:(tc + 1) * CH], start=True, stop=True)
                    Te.matmul(pool[3][0:64, :], ones1x128[:, 0:64],
                              rsrows[:, tc * CH:(tc + 1) * CH], start=True, stop=True)
                P.tensor(f)
                g_mm = cT.n

                def f(V, g_mm=g_mm):
                    cT.wait(V, g_mm)
                    V.tensor_single_scalar(gmb, pool[2][0:64, :], gsum[:, :], ALU.mult)
                    V.tensor_tensor(lgb, pool[4][0:64, :], gmb, ALU.subtract)
                    V.tensor_tensor(lgb, lgb, pool[3][0:64, :], ALU.mult)
                P.vector(f)
                last_gm = lg_done = cV.n

                def f(A, lg_done=lg_done):
                    cV.wait(A, lg_done)
                    A.activation(out=sgate, in_=lgb, func=AF.Sigmoid)
                P.scalar(f)
                last_sig = sig_done = cA.n

                def f(V, sig_done=sig_done):
                    cA.wait(V, sig_done)
                    V.tensor_scalar_add(sgate, sgate, gbias[:, :])
                P.vector(f)
                sb_done = cV.n

                for tt in range(4):
                    def f(Te, tt=tt, sb_done=sb_done, last_tr_cp=last_tr_cp):
                        cV.wait(Te, sb_done)
                        if last_tr_cp:
                            Te.eng.wait_ge(cV.sem, last_tr_cp)
                        Te.transpose(pool[5][:, 0:64], sgate[:, tt * 128:(tt + 1) * 128][0:64, :] if False else scrB[0:64, tt * 128:(tt + 1) * 128],
                                     eyeF[0:64, 0:64])
                    P.tensor(f)
                    tr_done = cT.n

                    def f(V, tt=tt, tr_done=tr_done):
                        cT.wait(V, tr_done)
                        V.tensor_copy(sc_tm[:, tt * 64:(tt + 1) * 64], pool[5][:, 0:64])
                    P.vector(f)
                    last_tr_cp = cV.n

                for tt in range(4):
                    def f(V, tt=tt):
                        sc = sc_tm[:, tt * 64:(tt + 1) * 64]
                        scv = sc.rearrange("p (g j) -> p g j", j=8)
                        V.tensor_reduce(gr.rearrange("p (g o) -> p g o", o=1),
                                        scv, AX.X, ALU.max)
                        V.tensor_copy(gso, gr)
                        for j in range(4):
                            V.reduce_max(mcol, gr, AX.X)
                            if j < 3:
                                V.tensor_single_scalar(gmask, gr, mcol,
                                                       ALU.is_ge)
                                V.scalar_tensor_tensor(gr, gmask, -1e30,
                                                       gr, ALU.mult, ALU.add)
                        V.tensor_single_scalar(gmask, gso, mcol, ALU.is_ge)
                        m64 = msk64.rearrange("p (g j) -> p g j", j=8)
                        for jj in range(8):
                            V.tensor_copy(m64[:, :, jj], gmask)
                        V.tensor_mul(msv, sc, msk64)
                        V.tensor_copy(mso, msv)
                        for j in range(6):
                            V.reduce_max(mcol, msv, AX.X)
                            if j < 5:
                                V.tensor_single_scalar(eqv, msv, mcol,
                                                       ALU.is_ge)
                                V.scalar_tensor_tensor(msv, eqv, -1e30,
                                                       msv, ALU.mult, ALU.add)
                        V.tensor_scalar_max(mcol, mcol, 0.0)
                        V.tensor_single_scalar(eqv, mso, mcol, ALU.is_ge)
                        V.tensor_mul(cu, eqv, mso)
                        V.reduce_sum(dcol, cu, AX.X)
                        V.tensor_scalar_add(dcol, dcol, 1e-8)
                        V.reciprocal(dcol, dcol)
                        V.tensor_single_scalar(comb_tm, cu, dcol, ALU.mult)
                    P.vector(f)
                    ct_done = cV.n

                    def f(Te, ct_done=ct_done, last_tr_cp=last_tr_cp):
                        cV.wait(Te, ct_done)
                        Te.eng.wait_ge(cV.sem, last_tr_cp)
                        Te.transpose(pool[5][0:64, 0:128], comb_tm, eyeF[:, :])
                    P.tensor(f)
                    cb_mm = cT.n

                    def f(V, tc=tc, tt=tt, cb_mm=cb_mm):
                        cT.wait(V, cb_mm)
                        V.tensor_copy(combT[:, tc * CH + tt * 128: tc * CH + (tt + 1) * 128],
                                      pool[5][0:64, 0:128])
                    P.vector(f)
                    last_tr_cp = cV.n
            comb_done_h = cV.n

            if debug and h == 0:
                def f(sync, comb_done_h=comb_done_h):
                    sync.wait_ge(cV.sem, comb_done_h)
                    sS.inc(sync.dma_start(comb_dbg.ap()[:, 0:TH], combT[:, :]), 16)
                P.sync(f)

            tfull = cT.n

            def f(G, tfull=tfull):
                G.eng.wait_ge(cT.sem, tfull)
                sG.inc(G.eng.dma_start(wslot[0][:, 0:8192], sg_d.ap()), 16)
                sG.inc(G.eng.dma_start(wslot[0][:, 8192:12288], su_d.ap()[:, 0:4096]), 16)
                sG.inc(G.eng.dma_start(wslot[1][:, 0:4096], su_d.ap()[:, 4096:8192]), 16)
                sG.inc(G.eng.dma_start(wslot[1][:, 4096:12288], sd_d.ap()), 16)
            P.gpsimd(f)
            shw_ld = sG.n

            def _su(r, mt):
                off = 8192 + r * 1024 + mt * 128
                if off < 12288:
                    return wslot[0][:, off:off + 128]
                return wslot[1][:, off - 12288:off - 12288 + 128]

            act_next = cA.n
            silu_tgt = []
            hmul_tgt = []
            for tc in range(NTCH):
                specs = []
                for mt in range(NG):
                    k = len(silu_tgt)
                    pg = pool[4] if k % 2 == 0 else pool[0]
                    pu = (pool[5], pool[1], pool[3])[k % 3]

                    def f(Te, tc=tc, mt=mt, k=k, pg=pg, comb_done_h=comb_done_h,
                          shw_ld=shw_ld):
                        if tc == 0 and mt == 0:
                            cV.wait(Te, comb_done_h)
                            Te.eng.wait_ge(sG.sem, shw_ld)
                        if k >= 2:
                            Te.eng.wait_ge(cA.sem, silu_tgt[k - 2])
                        for r in range(NG):
                            Te.matmul(pg[:, :],
                                      wslot[0][:, r * 1024 + mt * 128: r * 1024 + (mt + 1) * 128],
                                      xnT[:, r * TH + tc * CH: r * TH + (tc + 1) * CH],
                                      start=(r == 0), stop=(r == NG - 1))
                    P.tensor(f)
                    ga_done = cT.n
                    act_next += 1
                    silu_tgt.append(act_next)
                    specs.append((ga_done, pg, k))

                    def f(Te, tc=tc, mt=mt, k=k, pu=pu):
                        if k >= 3:
                            Te.eng.wait_ge(cV.sem, hmul_tgt[k - 3])
                        for r in range(NG):
                            Te.matmul(pu[:, :], _su(r, mt),
                                      xnT[:, r * TH + tc * CH: r * TH + (tc + 1) * CH],
                                      start=(r == 0), stop=(r == NG - 1))
                    P.tensor(f)
                    ua_done = cT.n

                    def f(V, mt=mt, k=k, ua_done=ua_done, pu=pu):
                        cT.wait(V, ua_done)
                        cA.wait(V, silu_tgt[k])
                        V.tensor_tensor(hs[:, mt * CH:(mt + 1) * CH],
                                        silu_sb[k % 2][:, :], pu[:, :], ALU.mult)
                    P.vector(f)
                    hmul_tgt.append(cV.n)
                hs_done = cV.n

                def f(A, specs=specs):
                    for ga, pg, k in specs:
                        A.eng.wait_ge(cT.sem, ga)
                        if k >= 2:
                            A.eng.wait_ge(cV.sem, hmul_tgt[k - 2])
                        A.activation(out=silu_sb[k % 2][:, :], in_=pg[:, :],
                                     func=AF.Silu)
                P.scalar_f(f)
                assert cA.n == silu_tgt[-1], (cA.n, silu_tgt[-1])

                hs_part = hmul_tgt[-2]
                ev_done = []
                for mt in range(NG):
                    def f(Te, tc=tc, mt=mt, hs_done=hs_done, hs_part=hs_part):
                        if mt == 0:
                            cV.wait(Te, hs_part)
                        if mt >= 2:
                            cV.wait(Te, ev_done[mt - 2])
                        pd = pool[6] if mt % 2 == 0 else pool[7]
                        for r in range(NG):
                            if mt == 0 and r == NG - 1:
                                cV.wait(Te, hs_done)
                            Te.matmul(pd[:, :],
                                      wslot[1][:, 4096 + r * 1024 + mt * 128: 4096 + r * 1024 + (mt + 1) * 128],
                                      hs[:, r * CH:(r + 1) * CH],
                                      start=(r == 0), stop=(r == NG - 1))
                    P.tensor(f)
                    da_done = cT.n

                    def f(V, tc=tc, mt=mt, da_done=da_done):
                        cT.wait(V, da_done)
                        pd = pool[6] if mt % 2 == 0 else pool[7]
                        a = acc[:, mt * TH + tc * CH: mt * TH + (tc + 1) * CH]
                        V.tensor_tensor(a, a, pd[:, :], ALU.add)
                    P.vector(f)
                    ev_done.append(cV.n)
            shared_done = cV.n

            sh_dump_sS = 0
            if debug and h == 0:
                def f(sync, shared_done=shared_done):
                    sync.wait_ge(cV.sem, shared_done)
                    for g in range(NG):
                        sS.inc(sync.dma_start(
                            sh_dbg.ap()[g * 128:(g + 1) * 128, :],
                            acc[:, g * TH:(g + 1) * TH]), 16)
                P.sync(f)
                sh_dump_sS = sS.n

            wfree[h * E + 0] = cT.n
            wfree[h * E + 1] = cT.n
            last_hc = 0
            for e in range(E):
                gi = h * E + e
                slot = gi % 2
                ws = wslot[slot]
                wsem = sWs[slot]

                sel_s = selslot[slot]

                def f(G, gi=gi, e=e, ws=ws, wsem=wsem, sel_s=sel_s):
                    if gi in wfree:
                        G.eng.wait_ge(cT.sem, wfree[gi])
                    wsem.inc(G.eng.dma_start(ws[:, :], moe_d.ap()[e]), 16)
                    wsem.inc(G.eng.dma_start(sel_s[:, :], sel_d.ap()[:, e * 128:(e + 1) * 128]), 16)
                P.gpsimd(f)
                w_ld = wsem.n

                especs = []
                for tc in range(NTCH):
                    def f(Te, e=e, tc=tc, w_ld=w_ld, wsem=wsem, shared_done=shared_done,
                          last_hc=last_hc, sel_s=sel_s):
                        if tc == 0:
                            Te.eng.wait_ge(wsem.sem, w_ld)
                            if e == 0:
                                cV.wait(Te, shared_done)
                        if last_hc:
                            Te.eng.wait_ge(cV.sem, last_hc)
                        Te.matmul(pool[2][:, :], sel_s[:, :],
                                  combT[:, tc * CH:(tc + 1) * CH], start=True, stop=True)
                    P.tensor(f)

                    for it in range(4):
                        k = len(silu_tgt)
                        pg = pool[4] if k % 2 == 0 else pool[0]
                        pu = (pool[5], pool[1], pool[3])[k % 3]

                        def f(Te, tc=tc, it=it, ws=ws, k=k, pg=pg):
                            Te.eng.wait_ge(cA.sem, silu_tgt[k - 2])
                            for r in range(NG):
                                Te.matmul(pg[:, :],
                                          ws[:, r * 512 + it * 128: r * 512 + (it + 1) * 128],
                                          xnT[:, r * TH + tc * CH: r * TH + (tc + 1) * CH],
                                          start=(r == 0), stop=(r == NG - 1))
                        P.tensor(f)
                        gmm = cT.n
                        act_next += 1
                        silu_tgt.append(act_next)
                        especs.append((gmm, pg, k))

                        def f(Te, tc=tc, it=it, ws=ws, k=k, pu=pu):
                            Te.eng.wait_ge(cV.sem, hmul_tgt[k - 3])
                            for r in range(NG):
                                Te.matmul(
                                    pu[:, :],
                                    ws[:, 4096 + r * 512 + it * 128: 4096 + r * 512 + (it + 1) * 128],
                                    xnT[:, r * TH + tc * CH: r * TH + (tc + 1) * CH],
                                    start=(r == 0), stop=(r == NG - 1))
                        P.tensor(f)
                        umm = cT.n

                        def f(V, it=it, k=k, umm=umm, pu=pu):
                            cT.wait(V, umm)
                            cA.wait(V, silu_tgt[k])
                            V.tensor_tensor(h1[:, :], silu_sb[k % 2][:, :], pu[:, :], ALU.mult)
                            V.tensor_tensor(hc[:, it * CH:(it + 1) * CH], h1[:, :],
                                            pool[2][:, :], ALU.mult)
                        P.vector(f)
                        hmul_tgt.append(cV.n)
                    last_hc = hc_done = cV.n

                    hc_part = hmul_tgt[-2]
                    ev_done = []
                    for mt in range(NG):
                        def f(Te, tc=tc, mt=mt, ws=ws, hc_done=hc_done,
                              hc_part=hc_part):
                            if mt == 0:
                                cV.wait(Te, hc_part)
                            if mt >= 2:
                                cV.wait(Te, ev_done[mt - 2])
                            pd = pool[6] if mt % 2 == 0 else pool[7]
                            for r in range(4):
                                if mt == 0 and r == 3:
                                    cV.wait(Te, hc_done)
                                Te.matmul(
                                    pd[:, :],
                                    ws[:, 8192 + r * 1024 + mt * 128: 8192 + r * 1024 + (mt + 1) * 128],
                                    hc[:, r * CH:(r + 1) * CH],
                                    start=(r == 0), stop=(r == 3))
                        P.tensor(f)
                        dmm = cT.n

                        def f(V, tc=tc, mt=mt, dmm=dmm, e=e, sh_dump_sS=sh_dump_sS):
                            if e == 0 and tc == 0 and mt == 0 and sh_dump_sS:
                                V.eng.wait_ge(sS.sem, sh_dump_sS)
                            cT.wait(V, dmm)
                            pd = pool[6] if mt % 2 == 0 else pool[7]
                            a = acc[:, mt * TH + tc * CH: mt * TH + (tc + 1) * CH]
                            V.tensor_tensor(a, a, pd[:, :], ALU.add)
                        P.vector(f)
                        ev_done.append(cV.n)

                def f(A, especs=especs):
                    for gmm, pg, k in especs:
                        A.eng.wait_ge(cT.sem, gmm)
                        A.eng.wait_ge(cV.sem, hmul_tgt[k - 2])
                        A.activation(out=silu_sb[k % 2][:, :], in_=pg[:, :],
                                     func=AF.Silu)
                P.scalar_f(f)
                assert cA.n == silu_tgt[-1], (cA.n, silu_tgt[-1])
                wfree[gi + 2] = cT.n

            experts_done = cV.n

            def f(sync, h=h, experts_done=experts_done):
                sync.wait_ge(cV.sem, experts_done)
                for g in range(NG):
                    sS.inc(sync.dma_start(
                        x2T_d.ap()[g * 128:(g + 1) * 128, h * TH:(h + 1) * TH],
                        acc[:, g * TH:(g + 1) * TH]), 16)
            P.sync(f)

        def f(sync):
            sync.wait_ge(sS.sem, sS.n)
        P.sync(f)
        pctx.close()




def build_B(T, debug=False):
    nc = bass.Bass("TRN2", target_bir_lowering=False)
    NHALF = 2
    TH = T // NHALF
    NTCH = TH // CH
    E = 64

    x1T_d = nc.dram_tensor("x1T", [D, T], F32, kind="ExternalInput")
    moe_d = nc.dram_tensor("moe_img", [E, 128, 12288], F32, kind="ExternalInput")
    sg_d = nc.dram_tensor("sg_img", [128, 8192], F32, kind="ExternalInput")
    su_d = nc.dram_tensor("su_img", [128, 8192], F32, kind="ExternalInput")
    sd_d = nc.dram_tensor("sd_img", [128, 8192], F32, kind="ExternalInput")
    gate_d = nc.dram_tensor("gate_img", [128, 512], F32, kind="ExternalInput")
    gbias_d = nc.dram_tensor("gate_bias", [64, 1], F32, kind="ExternalInput")
    sel_d = nc.dram_tensor("sel_img", [64, 8192], BF16, kind="ExternalInput")
    eyeF_d = nc.dram_tensor("eyeF", [128, 128], F32, kind="ExternalInput")
    x2T_d = nc.dram_tensor("x2T", [D, T], F32, kind="ExternalOutput")
    if debug:
        xn_dbg = nc.dram_tensor("xn_dbg", [128, NG * TH], BF16, kind="ExternalOutput")
        comb_dbg = nc.dram_tensor("comb_dbg", [64, T], BF16, kind="ExternalOutput")
        sh_dbg = nc.dram_tensor("sh_dbg", [D, TH], F32, kind="ExternalOutput")

    d = {"x1T": x1T_d, "moe_img": moe_d, "sg_img": sg_d, "su_img": su_d, "sd_img": sd_d, "gate_img": gate_d, "gate_bias": gbias_d, "sel_img": sel_d, "eyeF": eyeF_d, "x2T": x2T_d}
    if debug:
        d.update({"xn_dbg": xn_dbg, "comb_dbg": comb_dbg, "sh_dbg": sh_dbg})
    with ExitStack() as semctx:
        emit_B(nc, semctx, d, T, debug=debug)
    return nc


def emit_C(nc, semctx, d, T, debug=False):
    NCH = T // CH
    NW = 2
    x2T_d = d["x2T"]
    wk_d = d["wk_img"]
    wv_d = d["wv_img"]
    wr_d = d["wr_img"]
    wo_d = d["wo_img"]
    fwr_d = d["fwr_img"]
    fwk_d = d["fwk_img"]
    fwv_d = d["fwv_img"]
    vecs_d = d["vimgs"]
    out_d = d["outn"]
    k_dbg = d.get("k_dbg")
    y_dbg = d.get("y_dbg")
    xh_dbg = d.get("xh_dbg")
    x3_dbg = d.get("x3_dbg")
    with ExitStack() as ctx:
        sb = lambda name, shape, dt: ctx.enter_context(nc.sbuf_tensor(name + "_sb", shape, dt))

        GW = NG * CH
        Rb = sb("Rb", [128, GW], F32)
        K16 = sb("K16", [128, GW], F32)
        x2b = K16
        XH = sb("XH", [128, GW], F32)
        MIXB = [sb(f"mixb{j}", [128, GW], BF16) for j in range(3)]
        SR8 = sb("SR8", [128, GW], BF16)
        G8 = sb("G8", [128, GW], BF16)
        KK = sb("KK", [128, 32 * CH], BF16)
        V16 = KK[:, 0:8192].bitcast(F32)
        Lb = KK[:, 8192:16384].bitcast(F32)
        SU = sb("SU", [128, GW], U8)
        s1f = sb("s1f", [128, CH], F32)
        decays = sb("decays", [128, CH], F32)
        ones512 = sb("ones512", [128, CH], F32)
        zeros512 = sb("zeros512", [128, CH], F32)
        zeros8 = sb("zeros8", [128, 8], F32)
        sqscr = sb("sqscr", [128, CH], F32)
        scr = {nm: sb(nm, [128, CH], F32)
               for nm in ("ek", "ekB", "ekv", "Pp", "Qq", "cum", "Psh", "Qsh", "numv")}
        rowst = sb("rowstC", [1, 2 * CH], F32)
        vecs = sb("vecs", [128, 104], F32)
        edec = sb("edec", [128, 8], F32)
        Pc = sb("Pc", [128, 8], F32)
        Qc = sb("Qc", [128, 8], F32)
        v1s = sb("v1s", [128, 8], F32)
        v2s = sb("v2s", [128, 8], F32)
        tail1 = sb("tail1", [128, 8], F32)
        tail2 = sb("tail2", [128, 8], F32)
        onescol_f = sb("onescol_fc", [128, 1], F32)
        ones1x128 = sb("ones1x128c", [1, 128], F32)
        eyeF = sb("eyeFc", [128, 128], F32)
        wring = [sb(f"wring{j}", [128, 8192], BF16) for j in range(NW)]
        outnat = [sb(f"outnat{j}", [128, D], F32) for j in range(2)]

        P = Phase(nc, semctx, "C_", blk_ctx=ctx)
        sL, sS, cT, cV, cA = P.sL, P.sS, P.cT, P.cV, P.cA
        sG = P.sG
        sWr = [P.ct(f"sWr{j}") for j in range(NW)]
        sLc = [P.ct("sLc0"), P.ct("sLc1")]
        sSo = [P.ct("sSo0"), P.ct("sSo1")]

        pctx = ExitStack()
        pool = [pctx.enter_context(nc.psum_tensor(f"pc{j}_ps", [128, CH], F32))
                for j in range(8)]
        psSUM = pool[0][0:1, :]
        psSQ = pool[1][0:1, :]
        psM = pool[2][:, :]
        psRS = pool[3][:, :]

        ring_state = {"i": 0}
        ring_free_at = {}

        def load_piece(src_ap):
            i = ring_state["i"]
            slot = i % NW
            ring_state["i"] = i + 1

            def f(G, i=i, slot=slot, src_ap=src_ap):
                if i >= NW:
                    G.eng.wait_ge(cT.sem, ring_free_at[i - NW])
                sWr[slot].inc(G.eng.dma_start(wring[slot][:, :], src_ap), 16)
            P.gpsimd(f)
            return slot, sWr[slot].n, i

        def piece_done(i):
            ring_free_at[i] = cT.n

        def f(sync):
            for j in range(13):
                sL.inc(sync.dma_start(vecs[:, j * 8:(j + 1) * 8], vecs_d.ap()[j]), 16)
            sL.inc(sync.dma_start(eyeF[:, :], eyeF_dC.ap()), 16)
        eyeF_dC = nc.dram_tensor("eyeFc_in", [128, 128], F32, kind="ExternalInput")
        P.sync(f)
        init_ld = sL.n

        def f(V):
            V.memset(ones512[:, :], 1.0)
            V.memset(zeros512[:, :], 0.0)
            V.memset(zeros8[:, :], 0.0)
            V.memset(Pc[:, :], 0.0)
            V.memset(Qc[:, :], 0.0)
            V.memset(v1s[:, :], 0.0)
            V.memset(v2s[:, :], 0.0)
            V.memset(tail1[:, :], 0.0)
            V.memset(tail2[:, :], 0.0)
            V.memset(onescol_f[:, :], 1.0)
            V.memset(ones1x128[:, :], 1.0)
            V.eng.wait_ge(sL.sem, init_ld)
            V.tensor_scalar_add(vecs[:, 32:72], vecs[:, 32:72], -1.0)
        P.vector(f)
        memset_done = cV.n

        def f(A):
            A.eng.wait_ge(sL.sem, init_ld)
            cV.wait(A, memset_done)
            A.activation(out=edec[:, :], in_=vecs[:, 72:80], func=AF.Exp)
            A.activation(out=edec[:, :], in_=edec[:, :], func=AF.Exp, scale=-1.0)
        P.scalar(f)
        dc_done = cA.n

        decays_done = dc_done

        def emit_lnstats(srcb, sq_wait_extra=None):
            cv_src = cV.n
            sq_tgt = []
            mm_tgt = []
            an = cA.n
            for g in range(NG):
                an += 1
                sq_tgt.append(an)

                def f(Te, g=g):
                    cA.wait(Te, sq_tgt[g])
                    if g == 0:
                        cV.wait(Te)
                    Te.matmul(psSUM, onescol_f[:, :], srcb[:, g * CH:(g + 1) * CH],
                              start=(g == 0), stop=(g == NG - 1))
                    Te.matmul(psSQ, onescol_f[:, :], sqscr[:, :],
                              start=(g == 0), stop=(g == NG - 1))
                P.tensor(f)
                mm_tgt.append(cT.n)

            def f(A, cv_src=cv_src, mm_tgt=mm_tgt):
                for g in range(NG):
                    if g == 0:
                        cV.wait(A, cv_src)
                    else:
                        A.eng.wait_ge(cT.sem, mm_tgt[g - 1])
                    A.activation(out=sqscr[:, :], in_=srcb[:, g * CH:(g + 1) * CH],
                                 func=AF.Square)
            P.scalar_f(f)
            assert cA.n == sq_tgt[-1], (cA.n, sq_tgt[-1])
            sums_done = cT.n

            def f(V, sums_done=sums_done):
                cT.wait(V, sums_done)
                V.tensor_scalar_mul(rowst[:, 0:CH], psSUM, 1.0 / D)
                V.tensor_mul(rowst[:, CH:2 * CH], rowst[:, 0:CH], rowst[:, 0:CH])
                V.scalar_tensor_tensor(rowst[:, CH:2 * CH], psSQ, 1.0 / D,
                                       rowst[:, CH:2 * CH], ALU.mult, ALU.subtract)
                V.tensor_scalar_add(rowst[:, CH:2 * CH], rowst[:, CH:2 * CH], 1e-5)
            P.vector(f)
            vd = cV.n

            def f(A, vd=vd):
                cV.wait(A, vd)
                A.activation(out=rowst[:, CH:2 * CH], in_=rowst[:, CH:2 * CH], func=AF.Sqrt)
            P.scalar(f)
            sqd2 = cA.n

            def f(V, sqd2=sqd2):
                cA.wait(V, sqd2)
                V.reciprocal(rowst[:, CH:2 * CH], rowst[:, CH:2 * CH])
            P.vector(f)
            rd = cV.n

            def f(Te, rd=rd):
                cV.wait(Te, rd)
                Te.matmul(psM, ones1x128[:, :], rowst[:, 0:CH], start=True, stop=True)
                Te.matmul(psRS, ones1x128[:, :], rowst[:, CH:2 * CH], start=True, stop=True)
            P.tensor(f)
            return cT.n

        def mm8(dst_psum, lhs_slices, rhs_buf, extra_wait=None):
            def f(Te):
                if extra_wait is not None:
                    extra_wait(Te)
                for r in range(NG):
                    Te.matmul(dst_psum, lhs_slices[r], rhs_buf[:, r * CH:(r + 1) * CH],
                              start=(r == 0), stop=(r == NG - 1))
            P.tensor(f)
            return cT.n

        dumpstate = {"x3": 0}
        for tc in range(NCH):
            lsem = sLc[tc % 2]

            def f(sync, tc=tc, lsem=lsem):
                if tc >= 1:
                    sync.wait_ge(cV.sem, x2_used[tc - 1])
                    sync.wait_ge(cT.sem, trans_done[tc - 1])
                for g in range(NG):
                    lsem.inc(sync.dma_start(
                        x2b[:, g * CH:(g + 1) * CH],
                        x2T_d.ap()[g * 128:(g + 1) * 128, tc * CH:(tc + 1) * CH]), 16)
            if tc == 0:
                x2_used = {}
                trans_done = {}
            P.sync(f)
            x2_ld = lsem.n

            def f(Te, x2_ld=x2_ld, lsem=lsem, tc=tc):
                Te.eng.wait_ge(lsem.sem, x2_ld)
                if tc == 0:
                    cA.wait(Te, decays_done)
                    Te.eng.wait_ge(sL.sem, init_ld)
            P.tensor(f)

            def f(A, x2_ld=x2_ld, lsem=lsem):
                A.eng.wait_ge(lsem.sem, x2_ld)
            P.scalar(f)

            def f(V, x2_ld=x2_ld, lsem=lsem):
                V.eng.wait_ge(lsem.sem, x2_ld)
            P.vector(f)

            bc = emit_lnstats(x2b)

            def f(V, bc=bc):
                if dumpstate["x3"]:
                    V.eng.wait_ge(sS.sem, dumpstate["x3"])
                cT.wait(V, bc)
                for g in range(NG):
                    V.tensor_tensor(sqscr[:, :], x2b[:, g * CH:(g + 1) * CH], psM,
                                    ALU.subtract)
                    V.tensor_tensor(Rb[:, g * CH:(g + 1) * CH], sqscr[:, :], psRS,
                                    ALU.mult)
            P.vector(f)
            x2_used[tc] = cV.n
            w_done = cV.n

            bc = emit_lnstats(Rb)

            def f(V, bc=bc):
                cT.wait(V, bc)
                for g in range(NG):
                    V.tensor_tensor(sqscr[:, :], Rb[:, g * CH:(g + 1) * CH], psM,
                                    ALU.subtract)
                    V.tensor_tensor(sqscr[:, :], sqscr[:, :], psRS, ALU.mult)
                    V.tensor_scalar(Lb[:, g * CH:(g + 1) * CH], sqscr[:, :],
                                    vecs[:, 0 + g:1 + g], vecs[:, 8 + g:9 + g],
                                    ALU.mult, ALU.add)
            P.vector(f)
            ln1_done = cV.n

            def f(V, ln1_done=ln1_done):
                for g in range(NG):
                    L = Lb[:, g * CH:(g + 1) * CH]
                    dd = sqscr[:, :]
                    V.tensor_sub(dd[:, 1:CH], L[:, 1:CH], L[:, 0:CH - 1])
                    V.tensor_sub(dd[:, 0:1], L[:, 0:1], tail1[:, g:g + 1])
                    for mi, col in ((0, 32), (1, 40), (2, 48)):
                        V.scalar_tensor_tensor(MIXB[mi][:, g * CH:(g + 1) * CH],
                                               dd, vecs[:, col + g:col + g + 1], L,
                                               ALU.mult, ALU.add)
                    V.tensor_copy(tail1[:, g:g + 1], L[:, CH - 1:CH])
            P.vector(f)
            mix1_done = cV.n

            kvr_dst = (K16, V16, None)
            for wi, (wd, dstb) in enumerate(((wk_d, K16), (wv_d, V16), (wr_d, None))):
                for half in range(1):
                    pass
                slot, wn, pidx = load_piece(wd.ap())
                ev_done = []
                for mt in range(NG):
                    def f(Te, slot=slot, wn=wn, mt=mt, mix1_done=mix1_done, wi=wi):
                        Te.eng.wait_ge(sWr[slot].sem, wn)
                        if mt == 0:
                            cV.wait(Te, mix1_done)
                            cV.wait(Te)
                            cA.wait(Te)
                        if mt >= 2:
                            ct_ev, val = ev_done[mt - 2]
                            ct_ev.wait(Te, val)
                        pd = pool[4] if mt % 2 == 0 else pool[5]
                        for r in range(NG):
                            Te.matmul(pd[:, :],
                                      wring[slot][:, r * 1024 + mt * 128: r * 1024 + (mt + 1) * 128],
                                      MIXB[wi][:, r * CH:(r + 1) * CH],
                                      start=(r == 0), stop=(r == NG - 1))
                    P.tensor(f)
                    mmd = cT.n

                    if dstb is not None:
                        def f(V, mt=mt, mmd=mmd, dstb=dstb):
                            cT.wait(V, mmd)
                            pd = pool[4] if mt % 2 == 0 else pool[5]
                            V.tensor_copy(dstb[:, mt * CH:(mt + 1) * CH], pd[:, :])
                        P.vector(f)
                        ev_done.append((cV, cV.n))
                    else:
                        def f(A, mt=mt, mmd=mmd):
                            cT.wait(A, mmd)
                            pd = pool[4] if mt % 2 == 0 else pool[5]
                            A.activation(out=SR8[:, mt * CH:(mt + 1) * CH], in_=pd[:, :],
                                         func=AF.Sigmoid)
                        P.scalar(f)
                        ev_done.append((cA, cA.n))
                piece_done(pidx)
            kvr_done_v = cV.n
            kvr_done_a = cA.n
            slot_o, wn_o, pidx_o = load_piece(wo_d.ap())

            for g in range(NG):
                def f(A, g=g, kvr_done_v=kvr_done_v):
                    if g == 0:
                        cV.wait(A, kvr_done_v)
                    else:
                        cV.wait(A)
                    A.activation(out=scr["ek"][:, :], in_=K16[:, g * CH:(g + 1) * CH],
                                 func=AF.Exp)
                    A.activation(out=scr["ekB"][:, :], in_=K16[:, g * CH:(g + 1) * CH],
                                 func=AF.Exp, bias=vecs[:, 80 + g:81 + g])
                P.scalar(f)
                ek_done = cA.n

                def f(V, g=g, ek_done=ek_done):
                    cA.wait(V, ek_done)
                    V.tensor_single_scalar(decays[:, :], ones512[:, :],
                                           edec[:, g:g + 1], ALU.mult)
                    V.tensor_mul(scr["ekv"][:, :], scr["ek"][:, :],
                                 V16[:, g * CH:(g + 1) * CH])
                    V.eng.wait_ge(cV.sem, cV.n)
                    V.tensor_tensor_scan(scr["Pp"][:, :], decays[:, :],
                                         scr["ekv"][:, :], 0.0, ALU.mult, ALU.add)
                    V.tensor_tensor_scan(scr["Qq"][:, :], decays[:, :],
                                         scr["ek"][:, :], 0.0, ALU.mult, ALU.add)
                    V.tensor_tensor_scan(scr["cum"][:, :], decays[:, :],
                                         zeros512[:, :], 1.0, ALU.mult, ALU.add)
                    V.scalar_tensor_tensor(scr["Pp"][:, :], scr["cum"][:, :],
                                           Pc[:, g:g + 1], scr["Pp"][:, :],
                                           ALU.mult, ALU.add)
                    V.scalar_tensor_tensor(scr["Qq"][:, :], scr["cum"][:, :],
                                           Qc[:, g:g + 1], scr["Qq"][:, :],
                                           ALU.mult, ALU.add)
                    V.tensor_copy(scr["Psh"][:, 1:CH], scr["Pp"][:, 0:CH - 1])
                    V.tensor_copy(scr["Psh"][:, 0:1], Pc[:, g:g + 1])
                    V.tensor_copy(scr["Qsh"][:, 1:CH], scr["Qq"][:, 0:CH - 1])
                    V.tensor_copy(scr["Qsh"][:, 0:1], Qc[:, g:g + 1])
                    V.tensor_copy(Pc[:, g:g + 1], scr["Pp"][:, CH - 1:CH])
                    V.tensor_copy(Qc[:, g:g + 1], scr["Qq"][:, CH - 1:CH])
                    V.tensor_mul(scr["numv"][:, :], scr["ekB"][:, :],
                                 V16[:, g * CH:(g + 1) * CH])
                    V.tensor_add(scr["numv"][:, :], scr["numv"][:, :], scr["Psh"][:, :])
                    V.tensor_add(scr["Qsh"][:, :], scr["ekB"][:, :], scr["Qsh"][:, :])
                    V.reciprocal(scr["Qsh"][:, :], scr["Qsh"][:, :])
                    V.tensor_mul(scr["numv"][:, :], scr["numv"][:, :], scr["Qsh"][:, :])
                    V.tensor_mul(G8[:, g * CH:(g + 1) * CH], scr["numv"][:, :],
                                 SR8[:, g * CH:(g + 1) * CH])
                P.vector(f)
            wkv_done = cV.n

            if debug and tc == 0:
                def f(sync, wkv_done=wkv_done):
                    sync.wait_ge(cV.sem, wkv_done)
                    sS.inc(sync.dma_start(k_dbg.ap(), K16[:, :]), 16)
                    sS.inc(sync.dma_start(y_dbg.ap(), G8[:, :]), 16)
                P.sync(f)

            ev_done = []
            for mt in range(NG):
                def f(Te, mt=mt, slot_o=slot_o, wn_o=wn_o, wkv_done=wkv_done):
                    Te.eng.wait_ge(sWr[slot_o].sem, wn_o)
                    if mt == 0:
                        cV.wait(Te, wkv_done)
                    if mt >= 2:
                        cA.wait(Te, ev_done[mt - 2])
                    pd = pool[4] if mt % 2 == 0 else pool[5]
                    for r in range(NG):
                        Te.matmul(pd[:, :],
                                  wring[slot_o][:, r * 1024 + mt * 128: r * 1024 + (mt + 1) * 128],
                                  G8[:, r * CH:(r + 1) * CH],
                                  start=(r == 0), stop=(r == NG - 1))
                P.tensor(f)
                mmd = cT.n

                def f(A, mt=mt, mmd=mmd):
                    cT.wait(A, mmd)
                    pd = pool[4] if mt % 2 == 0 else pool[5]
                    A.activation(out=XH[:, mt * CH:(mt + 1) * CH], in_=pd[:, :],
                                 func=AF.Identity, scale=0.5)
                P.scalar(f)
                ev_done.append(cA.n)
            piece_done(pidx_o)
            xh_done = cA.n

            xh_dump_sS = 0
            if debug and tc == 0:
                def f(sync, xh_done=xh_done):
                    sync.wait_ge(cA.sem, xh_done)
                    sS.inc(sync.dma_start(xh_dbg.ap(), XH[:, :]), 16)
                P.sync(f)
                xh_dump_sS = sS.n

            xhv = XH[:, :].rearrange("p (g t) -> p g t", g=NG)

            def f(V, xh_done=xh_done, xh_dump_sS=xh_dump_sS):
                cA.wait(V, xh_done)
                if xh_dump_sS:
                    V.eng.wait_ge(sS.sem, xh_dump_sS)
                for t in range(CH):
                    V.scalar_tensor_tensor(xhv[:, :, t], v1s[:, :], 0.5, xhv[:, :, t],
                                           ALU.mult, ALU.add)
                    V.scalar_tensor_tensor(v1s[:, :], xhv[:, :, t], 1.0, xhv[:, :, t],
                                           ALU.is_lt, ALU.mult)
            P.vector(f)

            def f(V):
                for g in range(NG):
                    V.tensor_single_scalar(s1f[:, :], XH[:, g * CH:(g + 1) * CH],
                                           1.0, ALU.is_ge)
                    V.tensor_add(Rb[:, g * CH:(g + 1) * CH],
                                 Rb[:, g * CH:(g + 1) * CH], s1f[:, :])
            P.vector(f)
            xmid_done = cV.n

            bc = emit_lnstats(Rb)

            def f(V, bc=bc):
                cT.wait(V, bc)
                for g in range(NG):
                    V.tensor_tensor(sqscr[:, :], Rb[:, g * CH:(g + 1) * CH], psM,
                                    ALU.subtract)
                    V.tensor_tensor(sqscr[:, :], sqscr[:, :], psRS, ALU.mult)
                    V.tensor_scalar(Lb[:, g * CH:(g + 1) * CH], sqscr[:, :],
                                    vecs[:, 16 + g:17 + g], vecs[:, 24 + g:25 + g],
                                    ALU.mult, ALU.add)
            P.vector(f)

            def f(V):
                for g in range(NG):
                    L = Lb[:, g * CH:(g + 1) * CH]
                    dd = sqscr[:, :]
                    V.tensor_sub(dd[:, 1:CH], L[:, 1:CH], L[:, 0:CH - 1])
                    V.tensor_sub(dd[:, 0:1], L[:, 0:1], tail2[:, g:g + 1])
                    for mi, col in ((0, 56), (1, 64)):
                        V.scalar_tensor_tensor(MIXB[mi][:, g * CH:(g + 1) * CH],
                                               dd, vecs[:, col + g:col + g + 1], L,
                                               ALU.mult, ALU.add)
                    V.tensor_copy(tail2[:, g:g + 1], L[:, CH - 1:CH])
            P.vector(f)
            mix2_done = cV.n

            ev_done = []
            for pc in range(4):
                slot, wn, pidx = load_piece(fwk_d.ap()[:, pc * 8192:(pc + 1) * 8192])
                for mo in range(8):
                    mt = pc * 8 + mo

                    def f(Te, slot=slot, wn=wn, mo=mo, mt=mt, mix2_done=mix2_done):
                        Te.eng.wait_ge(sWr[slot].sem, wn)
                        if mt == 0:
                            cV.wait(Te, mix2_done)
                            cA.wait(Te)
                        if mt >= 2:
                            cA.wait(Te, ev_done[mt - 2])
                        pd = pool[4] if mt % 2 == 0 else pool[5]
                        for r in range(NG):
                            Te.matmul(pd[:, :],
                                      wring[slot][:, mo * 1024 + r * 128: mo * 1024 + (r + 1) * 128],
                                      MIXB[0][:, r * CH:(r + 1) * CH],
                                      start=(r == 0), stop=(r == NG - 1))
                    P.tensor(f)
                    mmd = cT.n

                    def f(A, mt=mt, mmd=mmd):
                        cT.wait(A, mmd)
                        pd = pool[4] if mt % 2 == 0 else pool[5]
                        A.activation(out=sqscr[:, :], in_=pd[:, :], func=AF.Relu)
                        A.activation(out=KK[:, mt * CH:(mt + 1) * CH], in_=sqscr[:, :],
                                     func=AF.Square)
                    P.scalar(f)
                    ev_done.append(cA.n)
                piece_done(pidx)
            kk_done = cA.n

            slot_r, wn_r, pidx_r = load_piece(fwr_d.ap())
            ev_done = []
            for mt in range(NG):
                def f(Te, slot_r=slot_r, wn_r=wn_r, mt=mt, kk_done=kk_done):
                    Te.eng.wait_ge(sWr[slot_r].sem, wn_r)
                    if mt == 0:
                        cA.wait(Te, kk_done)
                    if mt >= 2:
                        cA.wait(Te, ev_done[mt - 2])
                    pd = pool[4] if mt % 2 == 0 else pool[5]
                    for r in range(NG):
                        Te.matmul(pd[:, :],
                                  wring[slot_r][:, r * 1024 + mt * 128: r * 1024 + (mt + 1) * 128],
                                  MIXB[1][:, r * CH:(r + 1) * CH],
                                  start=(r == 0), stop=(r == NG - 1))
                P.tensor(f)
                mmd = cT.n

                def f(A, mt=mt, mmd=mmd):
                    cT.wait(A, mmd)
                    pd = pool[4] if mt % 2 == 0 else pool[5]
                    A.activation(out=SR8[:, mt * CH:(mt + 1) * CH], in_=pd[:, :],
                                 func=AF.Sigmoid)
                P.scalar(f)
                ev_done.append(cA.n)
            piece_done(pidx_r)
            csig_done = cA.n

            ev_done = []
            for pc in range(4):
                slot, wn, pidx = load_piece(fwv_d.ap()[:, pc * 8192:(pc + 1) * 8192])
                for mo in range(2):
                    mt = pc * 2 + mo

                    def f(Te, slot=slot, wn=wn, mo=mo, mt=mt, csig_done=csig_done):
                        Te.eng.wait_ge(sWr[slot].sem, wn)
                        if mt == 0:
                            cA.wait(Te, csig_done)
                        if mt >= 2:
                            cV.wait(Te, ev_done[mt - 2])
                        pd = pool[4] if mt % 2 == 0 else pool[5]
                        for r in range(32):
                            Te.matmul(pd[:, :],
                                      wring[slot][:, mo * 4096 + r * 128: mo * 4096 + (r + 1) * 128],
                                      KK[:, r * CH:(r + 1) * CH],
                                      start=(r == 0), stop=(r == 31))
                    P.tensor(f)
                    mmd = cT.n

                    def f(V, mt=mt, mmd=mmd, xh_dump_sS=xh_dump_sS):
                        if mt == 0 and xh_dump_sS:
                            V.eng.wait_ge(sS.sem, xh_dump_sS)
                        cT.wait(V, mmd)
                        pd = pool[4] if mt % 2 == 0 else pool[5]
                        V.scalar_tensor_tensor(XH[:, mt * CH:(mt + 1) * CH], pd[:, :],
                                               0.5, SR8[:, mt * CH:(mt + 1) * CH],
                                               ALU.mult, ALU.mult)
                    P.vector(f)
                    ev_done.append(cV.n)
                piece_done(pidx)
            xh2_done = cV.n

            def f(V, xh2_done=xh2_done):
                cV.wait(V, xh2_done)
                for t in range(CH):
                    V.scalar_tensor_tensor(xhv[:, :, t], v2s[:, :], 0.5, xhv[:, :, t],
                                           ALU.mult, ALU.add)
                    V.scalar_tensor_tensor(v2s[:, :], xhv[:, :, t], 1.0, xhv[:, :, t],
                                           ALU.is_lt, ALU.mult)
                for g in range(NG):
                    V.tensor_single_scalar(s1f[:, :], XH[:, g * CH:(g + 1) * CH],
                                           1.0, ALU.is_ge)
                    V.tensor_add(Rb[:, g * CH:(g + 1) * CH],
                                 Rb[:, g * CH:(g + 1) * CH], s1f[:, :])
            P.vector(f)

            lif2_done = cV.n

            def f(sync, lif2_done=lif2_done, tc=tc, lsem=lsem):
                sync.wait_ge(cV.sem, lif2_done)
                for g in range(NG):
                    lsem.inc(sync.dma_start(
                        XH[:, g * CH:(g + 1) * CH],
                        x2T_d.ap()[g * 128:(g + 1) * 128, tc * CH:(tc + 1) * CH]), 16)
            P.sync(f)
            x2re_ld = lsem.n

            def f(V, x2re_ld=x2re_ld, lsem=lsem):
                V.eng.wait_ge(lsem.sem, x2re_ld)
                for g in range(NG):
                    V.tensor_add(Rb[:, g * CH:(g + 1) * CH],
                                 Rb[:, g * CH:(g + 1) * CH], XH[:, g * CH:(g + 1) * CH])
            P.vector(f)
            x3_done = cV.n

            if debug and tc == 0:
                def f(sync, x3_done=x3_done):
                    sync.wait_ge(cV.sem, x3_done)
                    sS.inc(sync.dma_start(x3_dbg.ap(), Rb[:, :]), 16)
                P.sync(f)
                dumpstate["x3"] = sS.n

            bc = emit_lnstats(Rb)

            def f(V, bc=bc):
                cT.wait(V, bc)
                for g in range(NG):
                    V.tensor_tensor(sqscr[:, :], Rb[:, g * CH:(g + 1) * CH], psM,
                                    ALU.subtract)
                    V.tensor_tensor(Lb[:, g * CH:(g + 1) * CH], sqscr[:, :], psRS,
                                    ALU.mult)
            P.vector(f)
            u_done = cV.n

            bc = emit_lnstats(Lb, sq_wait_extra=u_done)

            def f(V, bc=bc):
                cT.wait(V, bc)
                for g in range(NG):
                    V.tensor_tensor(sqscr[:, :], Lb[:, g * CH:(g + 1) * CH], psM,
                                    ALU.subtract)
                    V.tensor_tensor(sqscr[:, :], sqscr[:, :], psRS, ALU.mult)
                    V.tensor_scalar(sqscr[:, :], sqscr[:, :],
                                    vecs[:, 88 + g:89 + g], vecs[:, 96 + g:97 + g],
                                    ALU.mult, ALU.add)
                    V.tensor_add(K16[:, g * CH:(g + 1) * CH],
                                 Rb[:, g * CH:(g + 1) * CH], sqscr[:, :])
            P.vector(f)
            out_done = cV.n

            tpcp_trail = []
            for tt in range(CH // 128):
                ob = outnat[tt % 2]
                osem = sSo[tt % 2]
                for g in range(NG):
                    pd = pool[6] if g % 2 == 0 else pool[7]

                    def f(Te, tt=tt, g=g, pd=pd, out_done=out_done):
                        if tt == 0 and g == 0:
                            cV.wait(Te, out_done)
                        if len(tpcp_trail) >= 2:
                            Te.eng.wait_ge(cV.sem, tpcp_trail[-2])
                        Te.transpose(pd[:, 0:128],
                                     K16[:, g * CH + tt * 128: g * CH + (tt + 1) * 128],
                                     eyeF[:, :])
                    P.tensor(f)
                    tpd = cT.n

                    def f(V, g=g, pd=pd, tpd=tpd, ob=ob, osem=osem):
                        cT.wait(V, tpd)
                        if g == 0:
                            V.eng.wait_ge(osem.sem, osem.n)
                        V.tensor_copy(ob[:, g * 128:(g + 1) * 128], pd[:, 0:128])
                        tpcp_trail.append(cV.n)
                    P.vector(f)
                tcp_done = cV.n

                def f(sync, tt=tt, tc=tc, ob=ob, osem=osem, tcp_done=tcp_done):
                    sync.wait_ge(cV.sem, tcp_done)
                    osem.inc(sync.dma_start(
                        out_d.ap()[tc * CH + tt * 128: tc * CH + (tt + 1) * 128, :],
                        ob[:, :]), 16)
                P.sync(f)
            trans_done[tc] = cT.n

        def f(sync):
            for j in range(2):
                sync.wait_ge(sSo[j].sem, sSo[j].n)
            sync.wait_ge(sS.sem, sS.n)
        P.sync(f)
        pctx.close()




def build_C(T, debug=False):
    nc = bass.Bass("TRN2", target_bir_lowering=False)
    NCH = T // CH
    NW = 2

    x2T_d = nc.dram_tensor("x2T", [D, T], F32, kind="ExternalInput")
    wk_d = nc.dram_tensor("wk_img", [128, 8192], F32, kind="ExternalInput")
    wv_d = nc.dram_tensor("wv_img", [128, 8192], F32, kind="ExternalInput")
    wr_d = nc.dram_tensor("wr_img", [128, 8192], F32, kind="ExternalInput")
    wo_d = nc.dram_tensor("wo_img", [128, 8192], F32, kind="ExternalInput")
    fwr_d = nc.dram_tensor("fwr_img", [128, 8192], F32, kind="ExternalInput")
    fwk_d = nc.dram_tensor("fwk_img", [128, 32768], F32, kind="ExternalInput")
    fwv_d = nc.dram_tensor("fwv_img", [128, 32768], F32, kind="ExternalInput")
    vecs_d = nc.dram_tensor("vimgs", [13, 128, 8], F32, kind="ExternalInput")
    out_d = nc.dram_tensor("outn", [T, D], F32, kind="ExternalOutput")
    if debug:
        k_dbg = nc.dram_tensor("k_dbg", [128, NG * CH], F32, kind="ExternalOutput")
        y_dbg = nc.dram_tensor("y_dbg", [128, NG * CH], BF16, kind="ExternalOutput")
        xh_dbg = nc.dram_tensor("xh_dbg", [128, NG * CH], F32, kind="ExternalOutput")
        x3_dbg = nc.dram_tensor("x3_dbg", [128, NG * CH], F32, kind="ExternalOutput")

    d = {"x2T": x2T_d, "wk_img": wk_d, "wv_img": wv_d, "wr_img": wr_d, "wo_img": wo_d, "fwr_img": fwr_d, "fwk_img": fwk_d, "fwv_img": fwv_d, "vimgs": vecs_d, "outn": out_d}
    if debug:
        d.update({"k_dbg": k_dbg, "y_dbg": y_dbg, "xh_dbg": xh_dbg, "x3_dbg": x3_dbg})
    with ExitStack() as semctx:
        emit_C(nc, semctx, d, T, debug=debug)
    return nc


def _consts_A():
    eyeF = np.eye(128, dtype=np.float32)
    eyeB = np.eye(128).astype(BF)
    expand16 = np.kron(np.eye(16, dtype=np.float32), np.ones((1, 64), np.float32))
    trils = np.zeros((128, 4 * CH), np.float32)
    for j in range(4):
        p = np.arange(128)[:, None]
        q = np.arange(CH)[None, :]
        trils[:, j * CH:(j + 1) * CH] = (j * 128 + p <= q)
    return eyeF, eyeB, expand16, trils.astype(BF)


def _in_map_A(ins, x_full, b, T, consts):
    eyeF, eyeB, expand16, trils = consts
    wpdup = np.concatenate(
        [np.asarray(ins["L0_attn_W_pred"]).transpose(1, 0, 2).reshape(64, 1024)] * 2,
        axis=0).astype(np.float32)
    return {
        "x0": np.ascontiguousarray(x_full[b, :T], dtype=np.float32),
        "w_k": np.asarray(ins["L0_attn_W_K"], np.float32),
        "w_q": np.asarray(ins["L0_attn_W_Q"], np.float32),
        "w_v": np.asarray(ins["L0_attn_W_V"], np.float32),
        "wpdup": wpdup, "eyeF": eyeF, "eyeB": eyeB, "expand16": expand16,
        "trils": trils,
    }


def run_A(x_full, ins, T, n_cores=NCORES, debug=False):
    consts = _consts_A()
    nc = build_A(T, debug=debug)
    in_maps = [_in_map_A(ins, x_full, b, T, consts) for b in range(n_cores)]
    res = run_bass_kernel_spmd(nc, in_maps, core_ids=list(range(n_cores)))
    if debug:
        return res.results
    return np.stack([r["x1T"] for r in res.results])


def _prep_B(ins):
    eg = np.asarray(ins["L0_ffn_eg_w"], np.float32)
    eu = np.asarray(ins["L0_ffn_eu_w"], np.float32)
    ed = np.asarray(ins["L0_ffn_ed_w"], np.float32)
    E = eg.shape[0]
    moe = np.empty((E, 128, 12288), np.float32)
    for e in range(E):
        moe[e, :, 0:4096] = _img(eg[e].T)
        moe[e, :, 4096:8192] = _img(eu[e].T)
        moe[e, :, 8192:12288] = _img(ed[e].T)
    sg = _img(np.asarray(ins["L0_ffn_sg_w"], np.float32).T)
    su = _img(np.asarray(ins["L0_ffn_su_w"], np.float32).T)
    sd = _img(np.asarray(ins["L0_ffn_sd_w"], np.float32).T)
    gate = _img(np.asarray(ins["L0_ffn_gate_w"], np.float32).T)
    gbias = np.ascontiguousarray(
        np.asarray(ins["L0_ffn_gate_bias"], np.float32).reshape(64, 1))
    sel = np.kron(np.eye(64), np.ones((1, 128))).astype(BF)
    return {"moe_img": moe, "sg_img": sg, "su_img": su, "sd_img": sd,
            "gate_img": gate, "gate_bias": gbias, "sel_img": sel,
            "eyeF": np.eye(128, dtype=np.float32)}


def run_B(x1T_all, ins, T, n_cores=NCORES, debug=False):
    base = _prep_B(ins)
    nc = build_B(T, debug=debug)
    in_maps = []
    for b in range(n_cores):
        m = dict(base)
        m["x1T"] = np.ascontiguousarray(x1T_all[b], dtype=np.float32)
        in_maps.append(m)
    res = run_bass_kernel_spmd(nc, in_maps, core_ids=list(range(n_cores)))
    if debug:
        return res.results
    return np.stack([r["x2T"] for r in res.results])


def _prep_C2(ins):
    g = lambda k: np.asarray(ins[k], np.float32)
    names = ["L1_attn_ln1_s", "L1_attn_ln1_b", "L1_attn_ln2_s", "L1_attn_ln2_b",
             "L1_attn_tmk", "L1_attn_tmv", "L1_attn_tmr", "L1_attn_f_tmk",
             "L1_attn_f_tmr", "L1_attn_w_decay", "L1_attn_u_first",
             "L1_ffn_ln_g", "L1_ffn_ln_b"]
    vimgs = np.stack([_vec_img(g(k)) for k in names])
    return {
        "wk_img": _img(g("L1_attn_Wk")), "wv_img": _img(g("L1_attn_Wv")),
        "wr_img": _img(g("L1_attn_Wr")), "wo_img": _img(g("L1_attn_Wo")),
        "fwr_img": _img(g("L1_attn_fWr")),
        "fwk_img": _img_mmajor(g("L1_attn_fWk")),
        "fwv_img": _img_mmajor(g("L1_attn_fWv")),
        "vimgs": vimgs,
        "eyeFc_in": np.eye(128, dtype=np.float32),
    }


def run_C(x2T_all, ins, T, n_cores=NCORES, debug=False):
    base = _prep_C2(ins)
    nc = build_C(T, debug=debug)
    in_maps = []
    for b in range(n_cores):
        m = dict(base)
        m["x2T"] = np.ascontiguousarray(x2T_all[b], dtype=np.float32)
        in_maps.append(m)
    res = run_bass_kernel_spmd(nc, in_maps, core_ids=list(range(n_cores)))
    if debug:
        return res.results
    return np.stack([r["outn"] for r in res.results])


def build_ALL(T):
    nc = bass.Bass("TRN2", target_bir_lowering=False)
    d = {}
    d["x0"] = nc.dram_tensor("x0", [T, D], F32, kind="ExternalInput")
    d["w_k"] = nc.dram_tensor("w_k", [16, 16], F32, kind="ExternalInput")
    d["w_q"] = nc.dram_tensor("w_q", [16, 16], F32, kind="ExternalInput")
    d["w_v"] = nc.dram_tensor("w_v", [16, 16], F32, kind="ExternalInput")
    d["wpdup"] = nc.dram_tensor("wpdup", [128, 1024], F32, kind="ExternalInput")
    d["eyeF"] = nc.dram_tensor("eyeF", [128, 128], F32, kind="ExternalInput")
    d["eyeB"] = nc.dram_tensor("eyeB", [128, 128], BF16, kind="ExternalInput")
    d["expand16"] = nc.dram_tensor("expand16", [16, 1024], F32, kind="ExternalInput")
    d["trils"] = nc.dram_tensor("trils", [128, 4 * CH], BF16, kind="ExternalInput")
    d["x1T"] = nc.dram_tensor("x1T", [D, T], F32, kind="Internal")
    d["zT_scratch"] = nc.dram_tensor("zT_scratch", [D, T], BF16, kind="Internal")
    d["moe_img"] = nc.dram_tensor("moe_img", [64, 128, 12288], F32, kind="ExternalInput")
    d["sg_img"] = nc.dram_tensor("sg_img", [128, 8192], F32, kind="ExternalInput")
    d["su_img"] = nc.dram_tensor("su_img", [128, 8192], F32, kind="ExternalInput")
    d["sd_img"] = nc.dram_tensor("sd_img", [128, 8192], F32, kind="ExternalInput")
    d["gate_img"] = nc.dram_tensor("gate_img", [128, 512], F32, kind="ExternalInput")
    d["gate_bias"] = nc.dram_tensor("gate_bias", [64, 1], F32, kind="ExternalInput")
    d["sel_img"] = nc.dram_tensor("sel_img", [64, 8192], BF16, kind="ExternalInput")
    d["x2T"] = nc.dram_tensor("x2T", [D, T], F32, kind="Internal")
    d["wk_img"] = nc.dram_tensor("wk_img", [128, 8192], F32, kind="ExternalInput")
    d["wv_img"] = nc.dram_tensor("wv_img", [128, 8192], F32, kind="ExternalInput")
    d["wr_img"] = nc.dram_tensor("wr_img", [128, 8192], F32, kind="ExternalInput")
    d["wo_img"] = nc.dram_tensor("wo_img", [128, 8192], F32, kind="ExternalInput")
    d["fwr_img"] = nc.dram_tensor("fwr_img", [128, 8192], F32, kind="ExternalInput")
    d["fwk_img"] = nc.dram_tensor("fwk_img", [128, 32768], F32, kind="ExternalInput")
    d["fwv_img"] = nc.dram_tensor("fwv_img", [128, 32768], F32, kind="ExternalInput")
    d["vimgs"] = nc.dram_tensor("vimgs", [13, 128, 8], F32, kind="ExternalInput")
    d["outn"] = nc.dram_tensor("outn", [T, D], F32, kind="ExternalOutput")
    with ExitStack() as semctx:
        emit_A(nc, semctx, d, T)
        emit_B(nc, semctx, d, T)
        emit_C(nc, semctx, d, T)
    return nc


def kernel(**ins):
    x0 = np.asarray(ins["L0_attn_x"], dtype=np.float32)
    T = x0.shape[1]
    nc = build_ALL(T)
    consts = _consts_A()
    baseB = _prep_B(ins)
    baseC = _prep_C2(ins)
    del baseC["eyeFc_in"]
    in_maps = []
    for b in range(NCORES):
        m = dict(baseB)
        m.update(baseC)
        m.update(_in_map_A(ins, x0, b, T, consts))
        m["eyeFc_in"] = consts[0]
        in_maps.append(m)
    res = run_bass_kernel_spmd(nc, in_maps, core_ids=list(range(NCORES)))
    return np.stack([r["outn"] for r in res.results])


# version 08
# speedup vs baseline: 1.8517x; 1.0199x over previous
import numpy as np
import ml_dtypes
from contextlib import ExitStack

import concourse.bass as bass
import concourse.mybir as mybir
from concourse.bass_utils import run_bass_kernel_spmd

F32 = mybir.dt.float32
BF16 = mybir.dt.bfloat16
U8 = mybir.dt.uint8
AF = mybir.ActivationFunctionType
ALU = mybir.AluOpType
AX = mybir.AxisListType

NCORES = 8
D = 1024
NG = 8
CH = 512
BF = ml_dtypes.bfloat16

_SER_OPS = {
    "tensor_copy", "tensor_tensor", "tensor_add", "tensor_sub", "tensor_mul",
    "tensor_max", "tensor_scalar", "tensor_single_scalar", "tensor_scalar_add",
    "tensor_scalar_sub", "tensor_scalar_mul", "tensor_scalar_max", "tensor_scalar_min",
    "tensor_relu", "tensor_reduce", "reduce_max", "reduce_sum", "scalar_tensor_tensor",
    "tensor_tensor_scan", "memset", "memzero", "reciprocal", "select", "copy_predicated",
    "activation", "sign", "square", "sqrt", "mul", "copy", "add", "iota",
}
_PE_OPS = {"matmul", "transpose", "ldweights"}


class Ct:
    def __init__(self, ctx, nc, name):
        self.sem = ctx.enter_context(nc.semaphore(name))
        self.n = 0

    def inc(self, instr, k=1):
        instr.then_inc(self.sem, k)
        self.n += k
        return instr

    def wait(self, eng_or_ew, val=None):
        eng = eng_or_ew.eng if isinstance(eng_or_ew, EW) else eng_or_ew
        eng.wait_ge(self.sem, self.n if val is None else val)


class EW:
    def __init__(self, ct, serialize=True):
        self.ct = ct
        self.serialize = serialize
        self.eng = None

    def bind(self, eng):
        self.eng = eng
        return self

    def __getattr__(self, name):
        eng = object.__getattribute__(self, "eng")
        fn = getattr(eng, name)
        if name in _SER_OPS or name in _PE_OPS:
            ct = self.ct
            ser = self.serialize and name in _SER_OPS

            def wrap(*a, **k):
                if ser:
                    eng.wait_ge(ct.sem, ct.n)
                ins = fn(*a, **k)
                ct.inc(ins)
                return ins
            return wrap
        return fn


class Phase:
    def __init__(self, nc, ctx, prefix="", blk_ctx=None):
        self.nc = nc
        self.blk = (blk_ctx if blk_ctx is not None else ctx).enter_context(nc.Block())
        self.prefix = prefix
        self.sL = Ct(ctx, nc, prefix + "sL")
        self.sG = Ct(ctx, nc, prefix + "sG")
        self.sS = Ct(ctx, nc, prefix + "sS")
        self.cT = Ct(ctx, nc, prefix + "cT")
        self.cV = Ct(ctx, nc, prefix + "cV")
        self.cA = Ct(ctx, nc, prefix + "cA")
        self.cG = Ct(ctx, nc, prefix + "cG")
        self._ctx = ctx
        self._nc = nc
        self.T_ = EW(self.cT, serialize=False)
        self.V_ = EW(self.cV)
        self.A_ = EW(self.cA)
        self.G_ = EW(self.cG)
        self.Vf_ = EW(self.cV, serialize=False)
        self.Af_ = EW(self.cA, serialize=False)

    def ct(self, name):
        return Ct(self._ctx, self._nc, self.prefix + name)

    def tensor(self, f):
        self.blk.tensor(lambda eng: f(self.T_.bind(eng)))

    def vector(self, f):
        self.blk.vector(lambda eng: f(self.V_.bind(eng)))

    def vector_f(self, f):
        self.blk.vector(lambda eng: f(self.Vf_.bind(eng)))

    def scalar_f(self, f):
        self.blk.scalar(lambda eng: f(self.Af_.bind(eng)))

    def scalar(self, f):
        self.blk.scalar(lambda eng: f(self.A_.bind(eng)))

    def gpsimd(self, f):
        self.blk.gpsimd(lambda eng: f(self.G_.bind(eng)))

    def sync(self, f):
        self.blk.sync(f)


def _img(w):
    K_, M_ = w.shape
    return np.ascontiguousarray(
        w.reshape(K_ // 128, 128, M_).transpose(1, 0, 2).reshape(128, -1), dtype=np.float32)


def _img_mmajor(w):
    K_, M_ = w.shape
    t = w.reshape(K_ // 128, 128, M_ // 128, 128)
    return np.ascontiguousarray(t.transpose(1, 2, 0, 3).reshape(128, -1), dtype=np.float32)


def _apify(t):
    return t[:, :]


def _vec_img(v):
    return np.ascontiguousarray(np.asarray(v, np.float32).reshape(8, 128).T)



def emit_A(nc, semctx, d, T, debug=False):
    NT = T // 128
    NCH = T // CH
    QT = CH // 128
    x0_d = d["x0"]
    wk_d = d["w_k"]
    wq_d = d["w_q"]
    wv_d = d["w_v"]
    wp_d = d["wpdup"]
    eyeF_d = d["eyeF"]
    eyeB_d = d["eyeB"]
    exp16_d = d["expand16"]
    tril_d = d["trils"]
    x1T_d = d["x1T"]
    zT_d = d["zT_scratch"]
    xnT_dbg = d.get("xnT_dbg")
    mats_dbg = d.get("mats_dbg")
    kqv_dbg = d.get("kqv_dbg")
    zT_dbg = d.get("zT_dbg")
    with ExitStack() as ctx:
        sb = lambda name, shape, dt: ctx.enter_context(nc.sbuf_tensor(name + "_sb", shape, dt))

        xnT = sb("xnT", [128, NG * T], BF16)
        zstg = [sb(f"zstg{j}", [64, CH], BF16) for j in range(2)]
        zrow = sb("zrow", [128, D], BF16)
        eyeF = sb("eyeF", [128, 128], F32)
        eyeB = sb("eyeB", [128, 128], BF16)
        wp = sb("wp", [128, 1024], F32)
        exp16 = sb("exp16", [16, 1024], F32)
        trils = sb("trils", [128, 4 * CH], BF16)
        onescol_b = sb("onescol_b", [128, 1], BF16)
        ones1x64 = sb("ones1x64", [1, 64], F32)
        kT_i = sb("kT_i", [64, T], BF16)
        qT_i = sb("qT_i", [64, T], BF16)
        vT_i = sb("vT_i", [64, T], BF16)
        vnat = sb("vnat", [128, NT * 65], BF16)
        expS = [sb(f"expS{j}", [128, CH], BF16) for j in range(3)]
        rec_row = sb("rec_row", [1, CH], F32)
        ztmp = sb("ztmp", [64, CH], F32)
        mats = {nm: sb(f"mat_{nm}", [128, NG * 1024], BF16) for nm in ("K", "Q", "V")}
        pcols = {nm: sb(f"pcol_{nm}", [128, NG * 16], F32) for nm in ("K", "Q", "V")}
        wraw = {nm: sb(f"wraw_{nm}", [16, 16], F32) for nm in ("K", "Q", "V")}
        probsT = {nm: sb(f"probsT_{nm}", [16, 16], F32) for nm in ("K", "Q", "V")}
        scr16 = sb("scr16", [16, 16], F32)
        col16 = sb("col16", [16, 1], F32)
        x0nat = [sb(f"x0nat{j}", [128, D], F32) for j in range(2)]
        sqnat = sb("sqnat", [128, D], F32)
        statc = sb("statc", [128, 6], F32)
        xn_nat = sb("xn_nat", [128, D], BF16)
        x1row = [sb(f"x1row{j}", [128, D], F32) for j in range(2)]

        P = Phase(nc, semctx, "A_", blk_ctx=ctx)
        sL, sS, cT, cV, cA = P.sL, P.sS, P.cT, P.cV, P.cA
        sLx = [P.ct("sLx0"), P.ct("sLx1")]
        sSx = [P.ct("sSx0"), P.ct("sSx1")]
        sSz = [P.ct("sSz0"), P.ct("sSz1")]

        pctx = ExitStack()
        psf = lambda name, shape, dt=F32: pctx.enter_context(nc.psum_tensor(name + "_ps", shape, dt))
        psT0 = psf("psT0", [128, 128], BF16)
        psT1 = psf("psT1", [128, 128], BF16)
        psPC = psf("psPC", [128, 16], F32)
        ps16 = psf("ps16", [16, 16], F32)

        def f(sync):
            for dst, src in ((eyeF, eyeF_d), (eyeB, eyeB_d), (wp, wp_d),
                             (exp16, exp16_d), (trils, tril_d)):
                sL.inc(sync.dma_start(dst[:, :], src.ap()), 16)
            for nm, src in (("K", wk_d), ("Q", wq_d), ("V", wv_d)):
                sL.inc(sync.dma_start(wraw[nm][:, :], src.ap()), 16)
        P.sync(f)
        consts_loaded = sL.n

        def f(V):
            V.memset(onescol_b[:, :], 1.0)
            V.memset(ones1x64[:, :], 1.0)
            V.memset(vnat[:, :].rearrange("p (n c) -> p n c", c=65)[:, :, 64], 1.0)
        P.vector(f)

        tp1_trail = []
        tile_done = {}
        for i in range(NT):
            xb = x0nat[i % 2]

            lsem = sLx[i % 2]

            def f(sync, i=i, xb=xb, lsem=lsem):
                if i >= 2:
                    sync.wait_ge(cV.sem, tile_done[i - 2])
                lsem.inc(sync.dma_start(xb[:, :], x0_d.ap()[i * 128:(i + 1) * 128, :]), 16)
            P.sync(f)
            ld_now = lsem.n

            def f(A, xb=xb, ld_now=ld_now, lsem=lsem):
                A.eng.wait_ge(lsem.sem, ld_now)
                A.activation(out=sqnat[:, :], in_=xb[:, :], func=AF.Square)
            P.scalar(f)
            sq_done = cA.n

            def f(V, xb=xb, ld_now=ld_now, sq_done=sq_done, lsem=lsem):
                V.eng.wait_ge(lsem.sem, ld_now)
                V.tensor_reduce(statc[:, 0:1], xb[:, :], AX.X, ALU.add)
                cA.wait(V, sq_done)
                V.tensor_reduce(statc[:, 1:2], sqnat[:, :], AX.X, ALU.add)
                V.tensor_scalar_mul(statc[:, 2:3], statc[:, 0:1], 1.0 / D)
                V.tensor_mul(statc[:, 3:4], statc[:, 2:3], statc[:, 2:3])
                V.scalar_tensor_tensor(statc[:, 4:5], statc[:, 1:2], 1.0 / D,
                                       statc[:, 3:4], ALU.mult, ALU.subtract)
                V.tensor_scalar_add(statc[:, 4:5], statc[:, 4:5], 1e-5)
            P.vector(f)
            var_done = cV.n

            def f(A, var_done=var_done):
                cV.wait(A, var_done)
                A.activation(out=statc[:, 5:6], in_=statc[:, 4:5], func=AF.Sqrt)
            P.scalar(f)
            sqrt_done = cA.n

            def f(V, xb=xb, sqrt_done=sqrt_done):
                cA.wait(V, sqrt_done)
                V.reciprocal(statc[:, 4:5], statc[:, 5:6])
                V.tensor_scalar(xn_nat[:, :], xb[:, :], statc[:, 2:3],
                                statc[:, 4:5], ALU.subtract, ALU.mult)
            P.vector(f)
            xn_done = cV.n

            for g in range(NG):
                pst = psT0 if g % 2 == 0 else psT1

                def f(Te, g=g, pst=pst, xn_done=xn_done, i=i):
                    if i == 0 and g == 0:
                        Te.eng.wait_ge(sL.sem, consts_loaded)
                    cV.wait(Te, xn_done)
                    if len(tp1_trail) >= 2:
                        Te.eng.wait_ge(cV.sem, tp1_trail[-2])
                    Te.transpose(pst[:, :], xn_nat[:, g * 128:(g + 1) * 128], eyeB[:, :])
                P.tensor(f)
                tpn = cT.n

                def f(V, g=g, i=i, pst=pst, tpn=tpn):
                    cT.wait(V, tpn)
                    V.tensor_copy(xnT[:, g * T + i * 128: g * T + (i + 1) * 128], pst[:, :])
                    tp1_trail.append(cV.n)
                P.vector(f)
            tile_done[i] = cV.n

        for nm in ("K", "Q", "V"):
            def f(V, nm=nm):
                V.eng.wait_ge(sL.sem, consts_loaded)
                V.tensor_reduce(col16[:, :], wraw[nm][:, :], AX.X, ALU.max)
                V.tensor_scalar_mul(col16[:, :], col16[:, :], -1.0)
            P.vector(f)
            neg_done = cV.n

            def f(A, nm=nm, neg_done=neg_done):
                A.eng.wait_ge(sL.sem, consts_loaded)
                cV.wait(A, neg_done)
                A.activation(out=scr16[:, :], in_=wraw[nm][:, :], func=AF.Exp,
                             bias=col16[:, :], accum_out=col16[:, :])
            P.scalar(f)
            exp_done = cA.n

            def f(V, nm=nm, exp_done=exp_done):
                cA.wait(V, exp_done)
                V.reciprocal(col16[:, :], col16[:, :])
                V.tensor_single_scalar(scr16[:, :], scr16[:, :], col16[:, :], ALU.mult)
            P.vector(f)
            probs_done = cV.n

            def f(Te, probs_done=probs_done):
                cV.wait(Te, probs_done)
                Te.transpose(ps16[:, :], scr16[:, :], eyeF[0:16, 0:16])
            P.tensor(f)
            t16 = cT.n

            def f(V, nm=nm, t16=t16):
                cT.wait(V, t16)
                V.tensor_copy(probsT[nm][:, :], ps16[:, :])
            P.vector(f)
            pc_cp = cV.n

            for r in range(NG):
                def f(Te, nm=nm, r=r, pc_cp=pc_cp):
                    cV.wait(Te, pc_cp)
                    Te.matmul(psPC[:, :], exp16[:, r * 128:(r + 1) * 128],
                              probsT[nm][:, :], start=True, stop=True)
                P.tensor(f)
                pc_mm = cT.n

                def f(V, nm=nm, r=r, pc_mm=pc_mm):
                    cT.wait(V, pc_mm)
                    V.tensor_copy(pcols[nm][:, r * 16:(r + 1) * 16], psPC[:, :])
                P.vector(f)
                pc_cp = cV.n

        def f(V):
            for nm in ("K", "Q", "V"):
                for v in range(16):
                    pb = (v % 2) * 64
                    rr = v // 2
                    for i in range(16):
                        if nm == "Q":
                            src = wp[pb:pb + 64, i * 64:(i + 1) * 64]
                        else:
                            src = eyeF[pb:pb + 64, pb:pb + 64]
                        V.tensor_single_scalar(
                            mats[nm][pb:pb + 64, rr * 1024 + i * 64: rr * 1024 + (i + 1) * 64],
                            src, pcols[nm][pb:pb + 64, rr * 16 + i: rr * 16 + i + 1], ALU.mult)
        P.vector(f)
        mats_done = cV.n

        if debug:
            def f(sync):
                sync.wait_ge(cV.sem, mats_done)
                sS.inc(sync.dma_start(xnT_dbg.ap(), xnT[:, :]), 16)
                for mi, nm in enumerate(("K", "Q", "V")):
                    sS.inc(sync.dma_start(
                        mats_dbg.ap()[:, mi * NG * 1024:(mi + 1) * NG * 1024],
                        mats[nm][:, :]), 16)
            P.sync(f)

        pctx.close()
        pctx = ExitStack()
        psf = lambda name, shape, dt=F32: pctx.enter_context(nc.psum_tensor(name + "_ps", shape, dt))
        psKQV0 = psf("psKQV0", [64, CH])
        psKQV1 = psf("psKQV1", [64, CH])
        psTB0 = psf("psTB0", [128, 64], BF16)
        psTB1 = psf("psTB1", [128, 64], BF16)
        psS0 = psf("psS0", [128, CH])
        psS1 = psf("psS1", [128, CH])
        psS2 = psf("psS2", [128, CH])
        psZ = psf("psZ", [65, CH])

        pend_fin = [None]
        evac_trail = []
        vt_trail = []
        zfin_trail = []
        kqv_dump_sS = None
        for hi in range(16):
            for pi, (nm, dst) in enumerate((("K", kT_i), ("Q", qT_i), ("V", vT_i))):
                for tc in range(NCH):
                    j = (pi * NCH + tc) % 2
                    pskv = psKQV0 if j == 0 else psKQV1

                    def f(Te, nm=nm, tc=tc, pskv=pskv, pi=pi, hi=hi):
                        if hi == 0 and pi == 0 and tc == 0:
                            cV.wait(Te, mats_done)
                        if len(evac_trail) >= 2:
                            Te.eng.wait_ge(cV.sem, evac_trail[-2])
                        for r in range(NG):
                            Te.matmul(
                                pskv[:, :],
                                mats[nm][:, r * 1024 + hi * 64: r * 1024 + (hi + 1) * 64],
                                xnT[:, r * T + tc * CH: r * T + (tc + 1) * CH],
                                start=(r == 0), stop=(r == NG - 1))
                    P.tensor(f)
                    mm_done = cT.n

                    def f(V, dst=dst, tc=tc, pskv=pskv, mm_done=mm_done,
                          hi=hi, pi=pi, kqv_dump_sS=kqv_dump_sS):
                        if hi == 1 and pi == 0 and tc == 0 and kqv_dump_sS is not None:
                            V.eng.wait_ge(sS.sem, kqv_dump_sS)
                        cT.wait(V, mm_done)
                        V.tensor_copy(dst[:, tc * CH:(tc + 1) * CH], pskv[:, :])
                        evac_trail.append(cV.n)
                    P.vector(f)

            if debug and hi == 0:
                kqv_done0 = cV.n

                def f(sync, kqv_done0=kqv_done0):
                    sync.wait_ge(cV.sem, kqv_done0)
                    for src_t, off in ((kT_i, 0), (qT_i, T), (vT_i, 2 * T)):
                        sS.inc(sync.dma_start(kqv_dbg.ap()[:, off:off + T], src_t[:, :]), 16)
                P.sync(f)
                kqv_dump_sS = sS.n

            kqv_done = cV.n
            for pt in range(NT):
                pst = psTB0 if pt % 2 == 0 else psTB1

                def f(Te, pt=pt, pst=pst, kqv_done=kqv_done):
                    if pt == 0:
                        cV.wait(Te, kqv_done)
                    if len(vt_trail) >= 2:
                        Te.eng.wait_ge(cV.sem, vt_trail[-2])
                    Te.transpose(pst[:, :], vT_i[:, pt * 128:(pt + 1) * 128], eyeB[0:64, 0:64])
                P.tensor(f)
                vt_mm = cT.n

                def f(V, pt=pt, pst=pst, vt_mm=vt_mm):
                    cT.wait(V, vt_mm)
                    V.tensor_copy(vnat[:, pt * 65:pt * 65 + 64], pst[:, :])
                    vt_trail.append(cV.n)
                P.vector_f(f)

            vnat_done = cV.n
            ZLAG = 2
            for qc in range(NCH):
                np_tiles = (qc + 1) * QT
                cA_base = cA.n
                exp_specs = []
                smm_list = []
                zmm_list = []
                zspecs = []

                def emit_z(spec, np_tiles=np_tiles):
                    pt, eb, diag, e_done, m_done = spec

                    def f(Te, pt=pt, eb=eb, diag=diag, np_tiles=np_tiles,
                          e_done=e_done, m_done=m_done):
                        if diag:
                            cV.wait(Te, m_done)
                        else:
                            cA.wait(Te, e_done)
                        if pt == 0 and zfin_trail:
                            Te.eng.wait_ge(cV.sem, zfin_trail[-1])
                        Te.matmul(psZ[:, :], vnat[:, pt * 65:pt * 65 + 65], eb[:, :],
                                  start=(pt == 0), stop=(pt == np_tiles - 1))
                    P.tensor(f)
                    zmm_list.append(cT.n)

                for pt in range(np_tiles):
                    pss = (psS0, psS1, psS2)[pt % 3]
                    eb = expS[pt % 3]
                    diag = pt >= qc * QT
                    e_done = cA_base + pt + 1

                    def f(Te, pt=pt, pss=pss, qc=qc, hi=hi, vnat_done=vnat_done,
                          e_done=e_done, cA_base=cA_base):
                        if qc == 0 and pt == 0:
                            cV.wait(Te, vnat_done)
                        if pt >= 3:
                            Te.eng.wait_ge(cA.sem, cA_base + pt - 2)
                        Te.matmul(pss[:, :], kT_i[:, pt * 128:(pt + 1) * 128],
                                  qT_i[:, qc * CH:(qc + 1) * CH], start=True, stop=True)
                    P.tensor(f)
                    smm_list.append(cT.n)
                    exp_specs.append((smm_list[-1], pss, eb, pt))
                    if pt == 1 and pend_fin[0] is not None:
                        pend_fin[0]()
                        pend_fin[0] = None

                    m_done = None
                    if diag:
                        def f(V, eb=eb, pt=pt, qc=qc, e_done=e_done):
                            cA.wait(V, e_done)
                            dpos = pt - qc * QT
                            V.tensor_mul(eb[:, :], eb[:, :], trils[:, dpos * CH:(dpos + 1) * CH])
                        P.vector(f)
                        m_done = cV.n

                    zspecs.append((pt, eb, diag, e_done, m_done))
                    if pt >= ZLAG:
                        emit_z(zspecs[pt - ZLAG])
                for pt in range(max(0, np_tiles - ZLAG), np_tiles):
                    emit_z(zspecs[pt])

                def f(A, exp_specs=exp_specs, zmm_list=zmm_list):
                    for smm, pss, eb, zi in exp_specs:
                        A.eng.wait_ge(cT.sem, smm)
                        if zi >= 3:
                            A.eng.wait_ge(cT.sem, zmm_list[zi - 3])
                        A.activation(out=eb[:, :], in_=pss[:, :], func=AF.Exp, scale=0.125)
                P.scalar_f(f)
                assert cA.n == cA_base + np_tiles

                zacc_done = cT.n

                def f(V, zacc_done=zacc_done):
                    cT.wait(V, zacc_done)
                    V.reciprocal(rec_row[:, :], psZ[64:65, :])
                    V.tensor_copy(ztmp[:, :], psZ[0:64, :])
                P.vector(f)
                ev_done = cV.n

                def fin(ev_done=ev_done, qc=qc, hi=hi):
                    def f(Te, ev_done=ev_done):
                        cV.wait(Te, ev_done)
                        Te.matmul(psZ[0:64, :], ones1x64[:, :], rec_row[:, :],
                                  start=True, stop=True)
                    P.tensor(f)
                    bc_done = cT.n

                    zslot = (hi * NCH + qc) % 2
                    zs = zstg[zslot]
                    zsem = sSz[zslot]

                    def f(V, bc_done=bc_done, zs=zs, zsem=zsem):
                        cT.wait(V, bc_done)
                        V.eng.wait_ge(zsem.sem, zsem.n)
                        V.tensor_mul(zs[:, :], ztmp[:, :], psZ[0:64, :])
                        zfin_trail.append(cV.n)
                    P.vector(f)
                    zmul_done = cV.n

                    def f(sync, qc=qc, hi=hi, zs=zs, zsem=zsem, zmul_done=zmul_done):
                        sync.wait_ge(cV.sem, zmul_done)
                        zsem.inc(sync.dma_start(
                            zT_d.ap()[hi * 64:(hi + 1) * 64, qc * CH:(qc + 1) * CH],
                            zs[:, :]), 16)
                    P.sync(f)
                pend_fin[0] = fin
            pend_fin[0]()
            pend_fin[0] = None

        attn_done = cV.n
        if debug:
            def f(sync):
                sync.wait_ge(sSz[0].sem, sSz[0].n)
                sync.wait_ge(sSz[1].sem, sSz[1].n)
                sS.inc(sync.dma_start(zT_dbg.ap(), zT_d.ap()), 16)
            P.sync(f)

        pctx.close()
        pctx = ExitStack()
        psf = lambda name, shape, dt=F32: pctx.enter_context(nc.psum_tensor(name + "_ps", shape, dt))
        psX0 = psf("psX0", [128, 128])
        psX1 = psf("psX1", [128, 128])

        add_trail = []
        row_done = {}
        st_issued = {}
        for i in range(NT):
            xb = x0nat[i % 2]
            xr = x1row[i % 2]

            lsem = sLx[i % 2]
            ssem = sSx[i % 2]

            def f(sync, i=i, xb=xb, lsem=lsem, ssem=ssem):
                if i == 0:
                    sync.wait_ge(cV.sem, attn_done)
                    sync.wait_ge(sSz[0].sem, sSz[0].n)
                    sync.wait_ge(sSz[1].sem, sSz[1].n)
                if i >= 2:
                    sync.wait_ge(ssem.sem, ssem.n)
                    sync.wait_ge(cV.sem, row_done[i - 2])
                lsem.inc(sync.dma_start(xb[:, :], x0_d.ap()[i * 128:(i + 1) * 128, :]), 16)
                if i >= 1:
                    sync.wait_ge(cV.sem, row_done[i - 1])
                for g in range(NG):
                    lsem.inc(sync.dma_start(
                        zrow[:, g * 128:(g + 1) * 128],
                        zT_d.ap()[g * 128:(g + 1) * 128, i * 128:(i + 1) * 128]), 16)
            P.sync(f)
            ld_now = lsem.n

            for g in range(NG):
                pst = psX0 if g % 2 == 0 else psX1

                def f(Te, g=g, pst=pst, xb=xb, ld_now=ld_now, lsem=lsem):
                    Te.eng.wait_ge(lsem.sem, ld_now)
                    if len(add_trail) >= 2:
                        Te.eng.wait_ge(cV.sem, add_trail[-2])
                    Te.transpose(pst[:, :], xb[:, g * 128:(g + 1) * 128], eyeF[:, :])
                P.tensor(f)
                x_mm = cT.n

                def f(V, g=g, i=i, pst=pst, xr=xr, x_mm=x_mm, ld_now=ld_now):
                    V.eng.wait_ge(lsem.sem, ld_now)
                    cT.wait(V, x_mm)
                    V.tensor_add(xr[:, g * 128:(g + 1) * 128], pst[:, :],
                                 zrow[:, g * 128:(g + 1) * 128])
                    add_trail.append(cV.n)
                P.vector(f)
            row_done[i] = cV.n

            def f(sync, i=i, xr=xr, ssem=ssem):
                sync.wait_ge(cV.sem, row_done[i])
                for g in range(NG):
                    ssem.inc(sync.dma_start(
                        x1T_d.ap()[g * 128:(g + 1) * 128, i * 128:(i + 1) * 128],
                        xr[:, g * 128:(g + 1) * 128]), 16)
            P.sync(f)

        def f(sync):
            sync.wait_ge(sSx[0].sem, sSx[0].n)
            sync.wait_ge(sSx[1].sem, sSx[1].n)
            sync.wait_ge(sS.sem, sS.n)
        P.sync(f)
        pctx.close()





def build_A(T, debug=False):
    nc = bass.Bass("TRN2", target_bir_lowering=False)
    NT = T // 128
    NCH = T // CH
    QT = CH // 128

    x0_d = nc.dram_tensor("x0", [T, D], F32, kind="ExternalInput")
    wk_d = nc.dram_tensor("w_k", [16, 16], F32, kind="ExternalInput")
    wq_d = nc.dram_tensor("w_q", [16, 16], F32, kind="ExternalInput")
    wv_d = nc.dram_tensor("w_v", [16, 16], F32, kind="ExternalInput")
    wp_d = nc.dram_tensor("wpdup", [128, 1024], F32, kind="ExternalInput")
    eyeF_d = nc.dram_tensor("eyeF", [128, 128], F32, kind="ExternalInput")
    eyeB_d = nc.dram_tensor("eyeB", [128, 128], BF16, kind="ExternalInput")
    exp16_d = nc.dram_tensor("expand16", [16, 1024], F32, kind="ExternalInput")
    tril_d = nc.dram_tensor("trils", [128, 4 * CH], BF16, kind="ExternalInput")
    x1T_d = nc.dram_tensor("x1T", [D, T], F32, kind="ExternalOutput")
    zT_d = nc.dram_tensor("zT_scratch", [D, T], BF16, kind="Internal")
    if debug:
        xnT_dbg = nc.dram_tensor("xnT_dbg", [128, NG * T], BF16, kind="ExternalOutput")
        mats_dbg = nc.dram_tensor("mats_dbg", [128, 3 * NG * 1024], BF16, kind="ExternalOutput")
        kqv_dbg = nc.dram_tensor("kqv_dbg", [64, 3 * T], BF16, kind="ExternalOutput")
        zT_dbg = nc.dram_tensor("zT_dbg", [D, T], BF16, kind="ExternalOutput")

    d = {"x0": x0_d, "w_k": wk_d, "w_q": wq_d, "w_v": wv_d, "wpdup": wp_d, "eyeF": eyeF_d, "eyeB": eyeB_d, "expand16": exp16_d, "trils": tril_d, "x1T": x1T_d, "zT_scratch": zT_d}
    if debug:
        d.update({"xnT_dbg": xnT_dbg, "mats_dbg": mats_dbg, "kqv_dbg": kqv_dbg, "zT_dbg": zT_dbg})
    with ExitStack() as semctx:
        emit_A(nc, semctx, d, T, debug=debug)
    return nc


def emit_B(nc, semctx, d, T, debug=False):
    NHALF = 2
    TH = T // NHALF
    NTCH = TH // CH
    E = 64
    x1T_d = d["x1T"]
    moe_d = d["moe_img"]
    sg_d = d["sg_img"]
    su_d = d["su_img"]
    sd_d = d["sd_img"]
    gate_d = d["gate_img"]
    gbias_d = d["gate_bias"]
    sel_d = d["sel_img"]
    eyeF_d = d["eyeF"]
    x2T_d = d["x2T"]
    xn_dbg = d.get("xn_dbg")
    comb_dbg = d.get("comb_dbg")
    sh_dbg = d.get("sh_dbg")
    with ExitStack() as ctx:
        sb = lambda name, shape, dt: ctx.enter_context(nc.sbuf_tensor(name + "_sb", shape, dt))

        acc = sb("acc", [128, NG * TH], F32)
        xnT = sb("xnTb", [128, NG * TH], BF16)
        combT = sb("combT", [64, TH], BF16)
        hs = sb("hs", [128, NG * CH], BF16)
        hc = sb("hc", [128, 4 * CH], BF16)
        h1 = sb("h1", [128, CH], BF16)
        silu_sb = [sb(f"silu{j}", [128, CH], BF16) for j in range(2)]
        wslot = [sb(f"wslot{j}", [128, 12288], BF16) for j in range(2)]
        selslot = [sb(f"selslot{j}", [64, 128], BF16) for j in range(2)]
        gatew = sb("gatew", [128, 512], F32)
        gsum = sb("gsum", [64, 1], F32)
        mrows = sb("mrows", [1, 2048], F32)
        rsrows = sb("rsrows", [1, 2048], F32)

        gbias = sb("gbias", [64, 1], F32)
        eyeF = sb("eyeFb", [128, 128], F32)
        onescol_f = sb("onescol_f", [128, 1], F32)
        ones1x128 = sb("ones1x128", [1, 128], F32)
        scrB = sb("scrB", [128, CH], F32)
        sq = scrB
        gmb = scrB[0:64, :]
        lgb = scrB[64:128, :]
        rowst = sb("rowst", [1, 3 * CH], F32)

        sc_tm = sb("sc_tm", [128, 4 * 64], F32)
        sgate = scrB[0:64, :]
        gr = _apify(sb("gr", [128, 8], F32))
        gso = _apify(sb("gso", [128, 8], F32))
        mcol = _apify(sb("mcol", [128, 1], F32))
        gmask = _apify(sb("gmask", [128, 8], F32))
        msk64 = _apify(sb("msk64", [128, 64], F32))
        msv = _apify(sb("msv", [128, 64], F32))
        mso = _apify(sb("mso", [128, 64], F32))
        eqv = _apify(sb("eqv", [128, 64], F32))
        cu = _apify(sb("cu", [128, 64], F32))
        dcol = _apify(sb("dcol", [128, 1], F32))
        comb_tm = _apify(sb("comb_tm", [128, 64], F32))

        P = Phase(nc, semctx, "B_", blk_ctx=ctx)
        sL, sS, cT, cV, cA = P.sL, P.sS, P.cT, P.cV, P.cA
        sG = P.sG
        sWs = [P.ct("sW0"), P.ct("sW1")]

        pctx = ExitStack()
        pool = [pctx.enter_context(nc.psum_tensor(f"pp{j}_ps", [128, CH], F32))
                for j in range(8)]
        psSUM = pool[0][0:1, :]
        psSQ = pool[1][0:1, :]
        psM = pool[2][:, :]
        psRS = pool[3][:, :]

        def f(sync):
            for dst, src in ((gbias, gbias_d), (eyeF, eyeF_d)):
                sL.inc(sync.dma_start(dst[:, :], src.ap()), 16)
            sL.inc(sync.dma_start(gatew[:, :], gate_d.ap()), 16)
        P.sync(f)

        consts_sL = sL.n

        def f(V):
            V.memset(onescol_f[:, :], 1.0)
            V.memset(ones1x128[:, :], 1.0)
        P.vector(f)
        memset_done = cV.n

        def f(Te):
            cV.wait(Te, memset_done)
            Te.eng.wait_ge(sL.sem, consts_sL)
            for r in range(NG):
                Te.matmul(pool[6][0:64, 0:1], gatew[:, r * 64:(r + 1) * 64],
                          onescol_f[:, :], start=(r == 0), stop=(r == NG - 1))
        P.tensor(f)
        gsum_mm = cT.n

        def f(V, gsum_mm=gsum_mm):
            cT.wait(V, gsum_mm)
            V.tensor_copy(gsum[:, :], pool[6][0:64, 0:1])
        P.vector(f)
        gsum_done = cV.n

        wfree = {}
        last_store = 0

        for h in range(NHALF):
            sLh = P.ct(f"sLh{h}")

            def f(sync, h=h, sLh=sLh):
                if h > 0:
                    sync.wait_ge(sS.sem, sS.n)
                for g in range(NG):
                    sLh.inc(sync.dma_start(
                        acc[:, g * TH:(g + 1) * TH],
                        x1T_d.ap()[g * 128:(g + 1) * 128, h * TH:(h + 1) * TH]), 16)
            P.sync(f)
            x1_ld = sLh.n

            prev_var_done = 0
            prev_xn_done = 0
            for tc in range(NTCH):
                sq_tgt = []
                mm_tgt = []
                an = cA.n
                for g in range(NG):
                    an += 1
                    sq_tgt.append(an)

                    def f(Te, tc=tc, g=g, x1_ld=x1_ld, sLh=sLh, h=h,
                          pvd=prev_var_done):
                        Te.eng.wait_ge(sLh.sem, x1_ld)
                        if h == 0 and tc == 0 and g == 0:
                            cV.wait(Te, memset_done)
                            Te.eng.wait_ge(sL.sem, consts_sL)
                        if g == 0 and pvd:
                            Te.eng.wait_ge(cV.sem, pvd)
                        cA.wait(Te, sq_tgt[g])
                        Te.matmul(psSUM, onescol_f[:, :],
                                  acc[:, g * TH + tc * CH: g * TH + (tc + 1) * CH],
                                  start=(g == 0), stop=(g == NG - 1))
                        Te.matmul(psSQ, onescol_f[:, :], sq[:, :],
                                  start=(g == 0), stop=(g == NG - 1))
                    P.tensor(f)
                    mm_tgt.append(cT.n)

                def f(A, tc=tc, x1_ld=x1_ld, sLh=sLh, pxd=prev_xn_done,
                      mm_tgt=mm_tgt):
                    for g in range(NG):
                        if g == 0:
                            A.eng.wait_ge(sLh.sem, x1_ld)
                            if pxd:
                                A.eng.wait_ge(cV.sem, pxd)
                        else:
                            A.eng.wait_ge(cT.sem, mm_tgt[g - 1])
                        A.activation(out=sq[:, :],
                                     in_=acc[:, g * TH + tc * CH: g * TH + (tc + 1) * CH],
                                     func=AF.Square)
                P.scalar_f(f)
                assert cA.n == sq_tgt[-1], (cA.n, sq_tgt[-1])
                sums_done = cT.n

                def f(V, sums_done=sums_done):
                    cT.wait(V, sums_done)
                    V.tensor_scalar_mul(rowst[:, 0:CH], psSUM, 1.0 / D)
                    V.tensor_mul(rowst[:, 2 * CH:3 * CH], rowst[:, 0:CH], rowst[:, 0:CH])
                    V.scalar_tensor_tensor(rowst[:, CH:2 * CH], psSQ, 1.0 / D,
                                           rowst[:, 2 * CH:3 * CH], ALU.mult, ALU.subtract)
                    V.tensor_scalar_add(rowst[:, CH:2 * CH], rowst[:, CH:2 * CH], 1e-5)
                P.vector(f)
                prev_var_done = var_done = cV.n

                def f(A, var_done=var_done):
                    cV.wait(A, var_done)
                    A.activation(out=rowst[:, 2 * CH:3 * CH], in_=rowst[:, CH:2 * CH],
                                 func=AF.Sqrt)
                P.scalar(f)
                sqrt_done = cA.n

                def f(V, sqrt_done=sqrt_done):
                    cA.wait(V, sqrt_done)
                    V.reciprocal(rowst[:, CH:2 * CH], rowst[:, 2 * CH:3 * CH])
                P.vector(f)
                rs_done = cV.n

                def f(Te, rs_done=rs_done):
                    cV.wait(Te, rs_done)
                    Te.matmul(psM, ones1x128[:, :], rowst[:, 0:CH], start=True, stop=True)
                    Te.matmul(psRS, ones1x128[:, :], rowst[:, CH:2 * CH],
                              start=True, stop=True)
                P.tensor(f)
                bc_done = cT.n

                def f(V, tc=tc, bc_done=bc_done):
                    cT.wait(V, bc_done)
                    V.tensor_copy(mrows[:, tc * CH:(tc + 1) * CH], rowst[:, 0:CH])
                    V.tensor_copy(rsrows[:, tc * CH:(tc + 1) * CH], rowst[:, CH:2 * CH])
                    for g in range(NG):
                        V.tensor_tensor(sq[:, :],
                                        acc[:, g * TH + tc * CH: g * TH + (tc + 1) * CH],
                                        psM, ALU.subtract)
                        V.tensor_tensor(xnT[:, g * TH + tc * CH: g * TH + (tc + 1) * CH],
                                        sq[:, :], psRS, ALU.mult)
                P.vector(f)
                prev_xn_done = cV.n
            xn_done_h = cV.n

            if debug and h == 0:
                def f(sync, xn_done_h=xn_done_h):
                    sync.wait_ge(cV.sem, xn_done_h)
                    sS.inc(sync.dma_start(xn_dbg.ap(), xnT[:, :]), 16)
                P.sync(f)

            last_sig = 0
            last_tr_cp = 0
            last_gm = 0
            for tc in range(NTCH):
                def f(Te, tc=tc, xn_done_h=xn_done_h, last_gm=last_gm, h=h):
                    cV.wait(Te, xn_done_h)
                    if h == 0 and tc == 0:
                        cV.wait(Te, gsum_done)
                    if last_gm:
                        Te.eng.wait_ge(cV.sem, last_gm)
                    for r in range(NG):
                        Te.matmul(pool[4][0:64, :], gatew[:, r * 64:(r + 1) * 64],
                                  acc[:, r * TH + tc * CH: r * TH + (tc + 1) * CH],
                                  start=(r == 0), stop=(r == NG - 1))
                    Te.matmul(pool[2][0:64, :], ones1x128[:, 0:64],
                              mrows[:, tc * CH:(tc + 1) * CH], start=True, stop=True)
                    Te.matmul(pool[3][0:64, :], ones1x128[:, 0:64],
                              rsrows[:, tc * CH:(tc + 1) * CH], start=True, stop=True)
                P.tensor(f)
                g_mm = cT.n

                def f(V, g_mm=g_mm):
                    cT.wait(V, g_mm)
                    V.tensor_single_scalar(gmb, pool[2][0:64, :], gsum[:, :], ALU.mult)
                    V.tensor_tensor(lgb, pool[4][0:64, :], gmb, ALU.subtract)
                    V.tensor_tensor(lgb, lgb, pool[3][0:64, :], ALU.mult)
                P.vector(f)
                last_gm = lg_done = cV.n

                def f(A, lg_done=lg_done):
                    cV.wait(A, lg_done)
                    A.activation(out=sgate, in_=lgb, func=AF.Sigmoid)
                P.scalar(f)
                last_sig = sig_done = cA.n

                def f(V, sig_done=sig_done):
                    cA.wait(V, sig_done)
                    V.tensor_scalar_add(sgate, sgate, gbias[:, :])
                P.vector(f)
                sb_done = cV.n

                for tt in range(4):
                    def f(Te, tt=tt, sb_done=sb_done, last_tr_cp=last_tr_cp):
                        cV.wait(Te, sb_done)
                        if last_tr_cp:
                            Te.eng.wait_ge(cV.sem, last_tr_cp)
                        Te.transpose(pool[5][:, 0:64], sgate[:, tt * 128:(tt + 1) * 128][0:64, :] if False else scrB[0:64, tt * 128:(tt + 1) * 128],
                                     eyeF[0:64, 0:64])
                    P.tensor(f)
                    tr_done = cT.n

                    def f(V, tt=tt, tr_done=tr_done):
                        cT.wait(V, tr_done)
                        V.tensor_copy(sc_tm[:, tt * 64:(tt + 1) * 64], pool[5][:, 0:64])
                    P.vector(f)
                    last_tr_cp = cV.n

                for tt in range(4):
                    def f(V, tt=tt):
                        sc = sc_tm[:, tt * 64:(tt + 1) * 64]
                        scv = sc.rearrange("p (g j) -> p g j", j=8)
                        V.tensor_reduce(gr.rearrange("p (g o) -> p g o", o=1),
                                        scv, AX.X, ALU.max)
                        V.tensor_copy(gso, gr)
                        for j in range(4):
                            V.reduce_max(mcol, gr, AX.X)
                            if j < 3:
                                V.tensor_single_scalar(gmask, gr, mcol,
                                                       ALU.is_ge)
                                V.scalar_tensor_tensor(gr, gmask, -1e30,
                                                       gr, ALU.mult, ALU.add)
                        V.tensor_single_scalar(gmask, gso, mcol, ALU.is_ge)
                        m64 = msk64.rearrange("p (g j) -> p g j", j=8)
                        for jj in range(8):
                            V.tensor_copy(m64[:, :, jj], gmask)
                        V.tensor_mul(msv, sc, msk64)
                        V.tensor_copy(mso, msv)
                        for j in range(6):
                            V.reduce_max(mcol, msv, AX.X)
                            if j < 5:
                                V.tensor_single_scalar(eqv, msv, mcol,
                                                       ALU.is_ge)
                                V.scalar_tensor_tensor(msv, eqv, -1e30,
                                                       msv, ALU.mult, ALU.add)
                        V.tensor_scalar_max(mcol, mcol, 0.0)
                        V.tensor_single_scalar(eqv, mso, mcol, ALU.is_ge)
                        V.tensor_mul(cu, eqv, mso)
                        V.reduce_sum(dcol, cu, AX.X)
                        V.tensor_scalar_add(dcol, dcol, 1e-8)
                        V.reciprocal(dcol, dcol)
                        V.tensor_single_scalar(comb_tm, cu, dcol, ALU.mult)
                    P.vector(f)
                    ct_done = cV.n

                    def f(Te, ct_done=ct_done, last_tr_cp=last_tr_cp):
                        cV.wait(Te, ct_done)
                        Te.eng.wait_ge(cV.sem, last_tr_cp)
                        Te.transpose(pool[5][0:64, 0:128], comb_tm, eyeF[:, :])
                    P.tensor(f)
                    cb_mm = cT.n

                    def f(V, tc=tc, tt=tt, cb_mm=cb_mm):
                        cT.wait(V, cb_mm)
                        V.tensor_copy(combT[:, tc * CH + tt * 128: tc * CH + (tt + 1) * 128],
                                      pool[5][0:64, 0:128])
                    P.vector(f)
                    last_tr_cp = cV.n
            comb_done_h = cV.n

            if debug and h == 0:
                def f(sync, comb_done_h=comb_done_h):
                    sync.wait_ge(cV.sem, comb_done_h)
                    sS.inc(sync.dma_start(comb_dbg.ap()[:, 0:TH], combT[:, :]), 16)
                P.sync(f)

            tfull = cT.n

            def f(G, tfull=tfull):
                G.eng.wait_ge(cT.sem, tfull)
                sG.inc(G.eng.dma_start(wslot[0][:, 0:8192], sg_d.ap()), 16)
                sG.inc(G.eng.dma_start(wslot[0][:, 8192:12288], su_d.ap()[:, 0:4096]), 16)
                sG.inc(G.eng.dma_start(wslot[1][:, 0:4096], su_d.ap()[:, 4096:8192]), 16)
                sG.inc(G.eng.dma_start(wslot[1][:, 4096:12288], sd_d.ap()), 16)
            P.gpsimd(f)
            shw_ld = sG.n

            def _su(r, mt):
                off = 8192 + r * 1024 + mt * 128
                if off < 12288:
                    return wslot[0][:, off:off + 128]
                return wslot[1][:, off - 12288:off - 12288 + 128]

            act_next = cA.n
            silu_tgt = []
            hmul_tgt = []
            for tc in range(NTCH):
                specs = []
                for mt in range(NG):
                    k = len(silu_tgt)
                    pg = pool[4] if k % 2 == 0 else pool[0]
                    pu = (pool[5], pool[1], pool[3])[k % 3]

                    def f(Te, tc=tc, mt=mt, k=k, pg=pg, comb_done_h=comb_done_h,
                          shw_ld=shw_ld):
                        if tc == 0 and mt == 0:
                            cV.wait(Te, comb_done_h)
                            Te.eng.wait_ge(sG.sem, shw_ld)
                        if k >= 2:
                            Te.eng.wait_ge(cA.sem, silu_tgt[k - 2])
                        for r in range(NG):
                            Te.matmul(pg[:, :],
                                      wslot[0][:, r * 1024 + mt * 128: r * 1024 + (mt + 1) * 128],
                                      xnT[:, r * TH + tc * CH: r * TH + (tc + 1) * CH],
                                      start=(r == 0), stop=(r == NG - 1))
                    P.tensor(f)
                    ga_done = cT.n
                    act_next += 1
                    silu_tgt.append(act_next)
                    specs.append((ga_done, pg, k))

                    def f(Te, tc=tc, mt=mt, k=k, pu=pu):
                        if k >= 3:
                            Te.eng.wait_ge(cV.sem, hmul_tgt[k - 3])
                        for r in range(NG):
                            Te.matmul(pu[:, :], _su(r, mt),
                                      xnT[:, r * TH + tc * CH: r * TH + (tc + 1) * CH],
                                      start=(r == 0), stop=(r == NG - 1))
                    P.tensor(f)
                    ua_done = cT.n

                    def f(V, mt=mt, k=k, ua_done=ua_done, pu=pu):
                        cT.wait(V, ua_done)
                        cA.wait(V, silu_tgt[k])
                        V.tensor_tensor(hs[:, mt * CH:(mt + 1) * CH],
                                        silu_sb[k % 2][:, :], pu[:, :], ALU.mult)
                    P.vector(f)
                    hmul_tgt.append(cV.n)
                hs_done = cV.n

                def f(A, specs=specs):
                    for ga, pg, k in specs:
                        A.eng.wait_ge(cT.sem, ga)
                        if k >= 2:
                            A.eng.wait_ge(cV.sem, hmul_tgt[k - 2])
                        A.activation(out=silu_sb[k % 2][:, :], in_=pg[:, :],
                                     func=AF.Silu)
                P.scalar_f(f)
                assert cA.n == silu_tgt[-1], (cA.n, silu_tgt[-1])

                hs_part = hmul_tgt[-2]
                ev_done = []
                for mt in range(NG):
                    def f(Te, tc=tc, mt=mt, hs_done=hs_done, hs_part=hs_part):
                        if mt == 0:
                            cV.wait(Te, hs_part)
                        if mt >= 2:
                            cV.wait(Te, ev_done[mt - 2])
                        pd = pool[6] if mt % 2 == 0 else pool[7]
                        for r in range(NG):
                            if mt == 0 and r == NG - 1:
                                cV.wait(Te, hs_done)
                            Te.matmul(pd[:, :],
                                      wslot[1][:, 4096 + r * 1024 + mt * 128: 4096 + r * 1024 + (mt + 1) * 128],
                                      hs[:, r * CH:(r + 1) * CH],
                                      start=(r == 0), stop=(r == NG - 1))
                    P.tensor(f)
                    da_done = cT.n

                    def f(V, tc=tc, mt=mt, da_done=da_done):
                        cT.wait(V, da_done)
                        pd = pool[6] if mt % 2 == 0 else pool[7]
                        a = acc[:, mt * TH + tc * CH: mt * TH + (tc + 1) * CH]
                        V.tensor_tensor(a, a, pd[:, :], ALU.add)
                    P.vector(f)
                    ev_done.append(cV.n)
            shared_done = cV.n

            sh_dump_sS = 0
            if debug and h == 0:
                def f(sync, shared_done=shared_done):
                    sync.wait_ge(cV.sem, shared_done)
                    for g in range(NG):
                        sS.inc(sync.dma_start(
                            sh_dbg.ap()[g * 128:(g + 1) * 128, :],
                            acc[:, g * TH:(g + 1) * TH]), 16)
                P.sync(f)
                sh_dump_sS = sS.n

            wfree[h * E + 0] = cT.n
            wfree[h * E + 1] = cT.n
            last_hc = 0
            for e in range(E):
                gi = h * E + e
                slot = gi % 2
                ws = wslot[slot]
                wsem = sWs[slot]

                sel_s = selslot[slot]

                def f(G, gi=gi, e=e, ws=ws, wsem=wsem, sel_s=sel_s):
                    if gi in wfree:
                        G.eng.wait_ge(cT.sem, wfree[gi])
                    wsem.inc(G.eng.dma_start(ws[:, :], moe_d.ap()[e]), 16)
                    wsem.inc(G.eng.dma_start(sel_s[:, :], sel_d.ap()[:, e * 128:(e + 1) * 128]), 16)
                P.gpsimd(f)
                w_ld = wsem.n

                especs = []
                for tc in range(NTCH):
                    def f(Te, e=e, tc=tc, w_ld=w_ld, wsem=wsem, shared_done=shared_done,
                          last_hc=last_hc, sel_s=sel_s):
                        if tc == 0:
                            Te.eng.wait_ge(wsem.sem, w_ld)
                            if e == 0:
                                cV.wait(Te, shared_done)
                        if last_hc:
                            Te.eng.wait_ge(cV.sem, last_hc)
                        Te.matmul(pool[2][:, :], sel_s[:, :],
                                  combT[:, tc * CH:(tc + 1) * CH], start=True, stop=True)
                    P.tensor(f)

                    for it in range(4):
                        k = len(silu_tgt)
                        pg = pool[4] if k % 2 == 0 else pool[0]
                        pu = (pool[5], pool[1], pool[3])[k % 3]

                        def f(Te, tc=tc, it=it, ws=ws, k=k, pg=pg):
                            Te.eng.wait_ge(cA.sem, silu_tgt[k - 2])
                            for r in range(NG):
                                Te.matmul(pg[:, :],
                                          ws[:, r * 512 + it * 128: r * 512 + (it + 1) * 128],
                                          xnT[:, r * TH + tc * CH: r * TH + (tc + 1) * CH],
                                          start=(r == 0), stop=(r == NG - 1))
                        P.tensor(f)
                        gmm = cT.n
                        act_next += 1
                        silu_tgt.append(act_next)
                        especs.append((gmm, pg, k))

                        def f(Te, tc=tc, it=it, ws=ws, k=k, pu=pu):
                            Te.eng.wait_ge(cV.sem, hmul_tgt[k - 3])
                            for r in range(NG):
                                Te.matmul(
                                    pu[:, :],
                                    ws[:, 4096 + r * 512 + it * 128: 4096 + r * 512 + (it + 1) * 128],
                                    xnT[:, r * TH + tc * CH: r * TH + (tc + 1) * CH],
                                    start=(r == 0), stop=(r == NG - 1))
                        P.tensor(f)
                        umm = cT.n

                        def f(V, it=it, k=k, umm=umm, pu=pu):
                            cT.wait(V, umm)
                            cA.wait(V, silu_tgt[k])
                            V.tensor_tensor(h1[:, :], silu_sb[k % 2][:, :], pu[:, :], ALU.mult)
                            V.tensor_tensor(hc[:, it * CH:(it + 1) * CH], h1[:, :],
                                            pool[2][:, :], ALU.mult)
                        P.vector(f)
                        hmul_tgt.append(cV.n)
                    last_hc = hc_done = cV.n

                    hc_part = hmul_tgt[-2]
                    ev_done = []
                    for mt in range(NG):
                        def f(Te, tc=tc, mt=mt, ws=ws, hc_done=hc_done,
                              hc_part=hc_part):
                            if mt == 0:
                                cV.wait(Te, hc_part)
                            if mt >= 2:
                                cV.wait(Te, ev_done[mt - 2])
                            pd = pool[6] if mt % 2 == 0 else pool[7]
                            for r in range(4):
                                if mt == 0 and r == 3:
                                    cV.wait(Te, hc_done)
                                Te.matmul(
                                    pd[:, :],
                                    ws[:, 8192 + r * 1024 + mt * 128: 8192 + r * 1024 + (mt + 1) * 128],
                                    hc[:, r * CH:(r + 1) * CH],
                                    start=(r == 0), stop=(r == 3))
                        P.tensor(f)
                        dmm = cT.n

                        def f(V, tc=tc, mt=mt, dmm=dmm, e=e, sh_dump_sS=sh_dump_sS):
                            if e == 0 and tc == 0 and mt == 0 and sh_dump_sS:
                                V.eng.wait_ge(sS.sem, sh_dump_sS)
                            cT.wait(V, dmm)
                            pd = pool[6] if mt % 2 == 0 else pool[7]
                            a = acc[:, mt * TH + tc * CH: mt * TH + (tc + 1) * CH]
                            V.tensor_tensor(a, a, pd[:, :], ALU.add)
                        P.vector(f)
                        ev_done.append(cV.n)

                def f(A, especs=especs):
                    for gmm, pg, k in especs:
                        A.eng.wait_ge(cT.sem, gmm)
                        A.eng.wait_ge(cV.sem, hmul_tgt[k - 2])
                        A.activation(out=silu_sb[k % 2][:, :], in_=pg[:, :],
                                     func=AF.Silu)
                P.scalar_f(f)
                assert cA.n == silu_tgt[-1], (cA.n, silu_tgt[-1])
                wfree[gi + 2] = cT.n

            experts_done = cV.n

            def f(sync, h=h, experts_done=experts_done):
                sync.wait_ge(cV.sem, experts_done)
                for g in range(NG):
                    sS.inc(sync.dma_start(
                        x2T_d.ap()[g * 128:(g + 1) * 128, h * TH:(h + 1) * TH],
                        acc[:, g * TH:(g + 1) * TH]), 16)
            P.sync(f)

        def f(sync):
            sync.wait_ge(sS.sem, sS.n)
        P.sync(f)
        pctx.close()




def build_B(T, debug=False):
    nc = bass.Bass("TRN2", target_bir_lowering=False)
    NHALF = 2
    TH = T // NHALF
    NTCH = TH // CH
    E = 64

    x1T_d = nc.dram_tensor("x1T", [D, T], F32, kind="ExternalInput")
    moe_d = nc.dram_tensor("moe_img", [E, 128, 12288], F32, kind="ExternalInput")
    sg_d = nc.dram_tensor("sg_img", [128, 8192], F32, kind="ExternalInput")
    su_d = nc.dram_tensor("su_img", [128, 8192], F32, kind="ExternalInput")
    sd_d = nc.dram_tensor("sd_img", [128, 8192], F32, kind="ExternalInput")
    gate_d = nc.dram_tensor("gate_img", [128, 512], F32, kind="ExternalInput")
    gbias_d = nc.dram_tensor("gate_bias", [64, 1], F32, kind="ExternalInput")
    sel_d = nc.dram_tensor("sel_img", [64, 8192], BF16, kind="ExternalInput")
    eyeF_d = nc.dram_tensor("eyeF", [128, 128], F32, kind="ExternalInput")
    x2T_d = nc.dram_tensor("x2T", [D, T], F32, kind="ExternalOutput")
    if debug:
        xn_dbg = nc.dram_tensor("xn_dbg", [128, NG * TH], BF16, kind="ExternalOutput")
        comb_dbg = nc.dram_tensor("comb_dbg", [64, T], BF16, kind="ExternalOutput")
        sh_dbg = nc.dram_tensor("sh_dbg", [D, TH], F32, kind="ExternalOutput")

    d = {"x1T": x1T_d, "moe_img": moe_d, "sg_img": sg_d, "su_img": su_d, "sd_img": sd_d, "gate_img": gate_d, "gate_bias": gbias_d, "sel_img": sel_d, "eyeF": eyeF_d, "x2T": x2T_d}
    if debug:
        d.update({"xn_dbg": xn_dbg, "comb_dbg": comb_dbg, "sh_dbg": sh_dbg})
    with ExitStack() as semctx:
        emit_B(nc, semctx, d, T, debug=debug)
    return nc


def emit_C(nc, semctx, d, T, debug=False):
    NCH = T // CH
    NW = 2
    x2T_d = d["x2T"]
    wk_d = d["wk_img"]
    wv_d = d["wv_img"]
    wr_d = d["wr_img"]
    wo_d = d["wo_img"]
    fwr_d = d["fwr_img"]
    fwk_d = d["fwk_img"]
    fwv_d = d["fwv_img"]
    vecs_d = d["vimgs"]
    out_d = d["outn"]
    k_dbg = d.get("k_dbg")
    y_dbg = d.get("y_dbg")
    xh_dbg = d.get("xh_dbg")
    x3_dbg = d.get("x3_dbg")
    with ExitStack() as ctx:
        sb = lambda name, shape, dt: ctx.enter_context(nc.sbuf_tensor(name + "_sb", shape, dt))

        GW = NG * CH
        Rb = sb("Rb", [128, GW], F32)
        K16 = sb("K16", [128, GW], F32)
        x2b = K16
        XH = sb("XH", [128, GW], F32)
        MIXB = [sb(f"mixb{j}", [128, GW], BF16) for j in range(3)]
        SR8 = sb("SR8", [128, GW], BF16)
        G8 = sb("G8", [128, GW], BF16)
        KK = sb("KK", [128, 32 * CH], BF16)
        V16 = KK[:, 0:8192].bitcast(F32)
        Lb = KK[:, 8192:16384].bitcast(F32)
        SU = sb("SU", [128, GW], U8)
        s1f = sb("s1f", [128, CH], F32)
        decays = sb("decays", [128, CH], F32)
        ones512 = sb("ones512", [128, CH], F32)
        zeros512 = sb("zeros512", [128, CH], F32)
        zeros8 = sb("zeros8", [128, 8], F32)
        sqscr = sb("sqscr", [128, CH], F32)
        scr = {nm: sb(nm, [128, CH], F32)
               for nm in ("ek", "ekB", "ekv", "Pp", "Qq", "cum", "Psh", "Qsh", "numv")}
        rowst = sb("rowstC", [1, 2 * CH], F32)
        vecs = sb("vecs", [128, 104], F32)
        edec = sb("edec", [128, 8], F32)
        Pc = sb("Pc", [128, 8], F32)
        Qc = sb("Qc", [128, 8], F32)
        v1s = sb("v1s", [128, 8], F32)
        v2s = sb("v2s", [128, 8], F32)
        tail1 = sb("tail1", [128, 8], F32)
        tail2 = sb("tail2", [128, 8], F32)
        onescol_f = sb("onescol_fc", [128, 1], F32)
        ones1x128 = sb("ones1x128c", [1, 128], F32)
        eyeF = sb("eyeFc", [128, 128], F32)
        wring = [sb(f"wring{j}", [128, 8192], BF16) for j in range(NW)]
        outnat = [sb(f"outnat{j}", [128, D], F32) for j in range(2)]

        P = Phase(nc, semctx, "C_", blk_ctx=ctx)
        sL, sS, cT, cV, cA = P.sL, P.sS, P.cT, P.cV, P.cA
        sG = P.sG
        sWr = [P.ct(f"sWr{j}") for j in range(NW)]
        sLc = [P.ct("sLc0"), P.ct("sLc1")]
        sSo = [P.ct("sSo0"), P.ct("sSo1")]

        pctx = ExitStack()
        pool = [pctx.enter_context(nc.psum_tensor(f"pc{j}_ps", [128, CH], F32))
                for j in range(8)]
        psSUM = pool[0][0:1, :]
        psSQ = pool[1][0:1, :]
        psM = pool[2][:, :]
        psRS = pool[3][:, :]

        ring_state = {"i": 0}
        ring_free_at = {}

        def load_piece(src_ap):
            i = ring_state["i"]
            slot = i % NW
            ring_state["i"] = i + 1

            def f(G, i=i, slot=slot, src_ap=src_ap):
                if i >= NW:
                    G.eng.wait_ge(cT.sem, ring_free_at[i - NW])
                sWr[slot].inc(G.eng.dma_start(wring[slot][:, :], src_ap), 16)
            P.gpsimd(f)
            return slot, sWr[slot].n, i

        def piece_done(i):
            ring_free_at[i] = cT.n

        def f(sync):
            for j in range(13):
                sL.inc(sync.dma_start(vecs[:, j * 8:(j + 1) * 8], vecs_d.ap()[j]), 16)
            sL.inc(sync.dma_start(eyeF[:, :], eyeF_dC.ap()), 16)
        eyeF_dC = nc.dram_tensor("eyeFc_in", [128, 128], F32, kind="ExternalInput")
        P.sync(f)
        init_ld = sL.n

        def f(V):
            V.memset(ones512[:, :], 1.0)
            V.memset(zeros512[:, :], 0.0)
            V.memset(zeros8[:, :], 0.0)
            V.memset(Pc[:, :], 0.0)
            V.memset(Qc[:, :], 0.0)
            V.memset(v1s[:, :], 0.0)
            V.memset(v2s[:, :], 0.0)
            V.memset(tail1[:, :], 0.0)
            V.memset(tail2[:, :], 0.0)
            V.memset(onescol_f[:, :], 1.0)
            V.memset(ones1x128[:, :], 1.0)
            V.eng.wait_ge(sL.sem, init_ld)
            V.tensor_scalar_add(vecs[:, 32:72], vecs[:, 32:72], -1.0)
        P.vector(f)
        memset_done = cV.n

        def f(A):
            A.eng.wait_ge(sL.sem, init_ld)
            cV.wait(A, memset_done)
            A.activation(out=edec[:, :], in_=vecs[:, 72:80], func=AF.Exp)
            A.activation(out=edec[:, :], in_=edec[:, :], func=AF.Exp, scale=-1.0)
        P.scalar(f)
        dc_done = cA.n

        decays_done = dc_done

        def emit_lnstats(srcb, sq_wait_extra=None):
            cv_src = cV.n
            sq_tgt = []
            mm_tgt = []
            an = cA.n
            for g in range(NG):
                an += 1
                sq_tgt.append(an)

                def f(Te, g=g):
                    cA.wait(Te, sq_tgt[g])
                    if g == 0:
                        cV.wait(Te)
                    Te.matmul(psSUM, onescol_f[:, :], srcb[:, g * CH:(g + 1) * CH],
                              start=(g == 0), stop=(g == NG - 1))
                    Te.matmul(psSQ, onescol_f[:, :], sqscr[:, :],
                              start=(g == 0), stop=(g == NG - 1))
                P.tensor(f)
                mm_tgt.append(cT.n)

            def f(A, cv_src=cv_src, mm_tgt=mm_tgt):
                for g in range(NG):
                    if g == 0:
                        cV.wait(A, cv_src)
                    else:
                        A.eng.wait_ge(cT.sem, mm_tgt[g - 1])
                    A.activation(out=sqscr[:, :], in_=srcb[:, g * CH:(g + 1) * CH],
                                 func=AF.Square)
            P.scalar_f(f)
            assert cA.n == sq_tgt[-1], (cA.n, sq_tgt[-1])
            sums_done = cT.n

            def f(V, sums_done=sums_done):
                cT.wait(V, sums_done)
                V.tensor_scalar_mul(rowst[:, 0:CH], psSUM, 1.0 / D)
                V.tensor_mul(rowst[:, CH:2 * CH], rowst[:, 0:CH], rowst[:, 0:CH])
                V.scalar_tensor_tensor(rowst[:, CH:2 * CH], psSQ, 1.0 / D,
                                       rowst[:, CH:2 * CH], ALU.mult, ALU.subtract)
                V.tensor_scalar_add(rowst[:, CH:2 * CH], rowst[:, CH:2 * CH], 1e-5)
            P.vector(f)
            vd = cV.n

            def f(A, vd=vd):
                cV.wait(A, vd)
                A.activation(out=rowst[:, CH:2 * CH], in_=rowst[:, CH:2 * CH], func=AF.Sqrt)
            P.scalar(f)
            sqd2 = cA.n

            def f(V, sqd2=sqd2):
                cA.wait(V, sqd2)
                V.reciprocal(rowst[:, CH:2 * CH], rowst[:, CH:2 * CH])
            P.vector(f)
            rd = cV.n

            def f(Te, rd=rd):
                cV.wait(Te, rd)
                Te.matmul(psM, ones1x128[:, :], rowst[:, 0:CH], start=True, stop=True)
                Te.matmul(psRS, ones1x128[:, :], rowst[:, CH:2 * CH], start=True, stop=True)
            P.tensor(f)
            return cT.n

        def mm8(dst_psum, lhs_slices, rhs_buf, extra_wait=None):
            def f(Te):
                if extra_wait is not None:
                    extra_wait(Te)
                for r in range(NG):
                    Te.matmul(dst_psum, lhs_slices[r], rhs_buf[:, r * CH:(r + 1) * CH],
                              start=(r == 0), stop=(r == NG - 1))
            P.tensor(f)
            return cT.n

        dumpstate = {"x3": 0}
        for tc in range(NCH):
            lsem = sLc[tc % 2]

            def f(sync, tc=tc, lsem=lsem):
                if tc >= 1:
                    sync.wait_ge(cV.sem, x2_used[tc - 1])
                    sync.wait_ge(cT.sem, trans_done[tc - 1])
                for g in range(NG):
                    lsem.inc(sync.dma_start(
                        x2b[:, g * CH:(g + 1) * CH],
                        x2T_d.ap()[g * 128:(g + 1) * 128, tc * CH:(tc + 1) * CH]), 16)
            if tc == 0:
                x2_used = {}
                trans_done = {}
            P.sync(f)
            x2_ld = lsem.n

            def f(Te, x2_ld=x2_ld, lsem=lsem, tc=tc):
                Te.eng.wait_ge(lsem.sem, x2_ld)
                if tc == 0:
                    cA.wait(Te, decays_done)
                    Te.eng.wait_ge(sL.sem, init_ld)
            P.tensor(f)

            def f(A, x2_ld=x2_ld, lsem=lsem):
                A.eng.wait_ge(lsem.sem, x2_ld)
            P.scalar(f)

            def f(V, x2_ld=x2_ld, lsem=lsem):
                V.eng.wait_ge(lsem.sem, x2_ld)
            P.vector(f)

            bc = emit_lnstats(x2b)

            def f(V, bc=bc):
                if dumpstate["x3"]:
                    V.eng.wait_ge(sS.sem, dumpstate["x3"])
                cT.wait(V, bc)
                for g in range(NG):
                    V.tensor_tensor(sqscr[:, :], x2b[:, g * CH:(g + 1) * CH], psM,
                                    ALU.subtract)
                    V.tensor_tensor(Rb[:, g * CH:(g + 1) * CH], sqscr[:, :], psRS,
                                    ALU.mult)
            P.vector(f)
            x2_used[tc] = cV.n
            w_done = cV.n

            bc = emit_lnstats(Rb)

            def f(V, bc=bc):
                cT.wait(V, bc)
                for g in range(NG):
                    V.tensor_tensor(sqscr[:, :], Rb[:, g * CH:(g + 1) * CH], psM,
                                    ALU.subtract)
                    V.tensor_tensor(sqscr[:, :], sqscr[:, :], psRS, ALU.mult)
                    V.tensor_scalar(Lb[:, g * CH:(g + 1) * CH], sqscr[:, :],
                                    vecs[:, 0 + g:1 + g], vecs[:, 8 + g:9 + g],
                                    ALU.mult, ALU.add)
            P.vector(f)
            ln1_done = cV.n

            def f(V, ln1_done=ln1_done):
                for g in range(NG):
                    L = Lb[:, g * CH:(g + 1) * CH]
                    dd = sqscr[:, :]
                    V.tensor_sub(dd[:, 1:CH], L[:, 1:CH], L[:, 0:CH - 1])
                    V.tensor_sub(dd[:, 0:1], L[:, 0:1], tail1[:, g:g + 1])
                    for mi, col in ((0, 32), (1, 40), (2, 48)):
                        V.scalar_tensor_tensor(MIXB[mi][:, g * CH:(g + 1) * CH],
                                               dd, vecs[:, col + g:col + g + 1], L,
                                               ALU.mult, ALU.add)
                    V.tensor_copy(tail1[:, g:g + 1], L[:, CH - 1:CH])
            P.vector(f)
            mix1_done = cV.n

            kvr_dst = (K16, V16, None)
            for wi, (wd, dstb) in enumerate(((wk_d, K16), (wv_d, V16), (wr_d, None))):
                for half in range(1):
                    pass
                slot, wn, pidx = load_piece(wd.ap())
                ev_done = []
                for mt in range(NG):
                    def f(Te, slot=slot, wn=wn, mt=mt, mix1_done=mix1_done, wi=wi):
                        Te.eng.wait_ge(sWr[slot].sem, wn)
                        if mt == 0:
                            cV.wait(Te, mix1_done)
                            cV.wait(Te)
                            cA.wait(Te)
                        if mt >= 2:
                            ct_ev, val = ev_done[mt - 2]
                            ct_ev.wait(Te, val)
                        pd = pool[4] if mt % 2 == 0 else pool[5]
                        for r in range(NG):
                            Te.matmul(pd[:, :],
                                      wring[slot][:, r * 1024 + mt * 128: r * 1024 + (mt + 1) * 128],
                                      MIXB[wi][:, r * CH:(r + 1) * CH],
                                      start=(r == 0), stop=(r == NG - 1))
                    P.tensor(f)
                    mmd = cT.n

                    if dstb is not None:
                        def f(V, mt=mt, mmd=mmd, dstb=dstb):
                            cT.wait(V, mmd)
                            pd = pool[4] if mt % 2 == 0 else pool[5]
                            V.tensor_copy(dstb[:, mt * CH:(mt + 1) * CH], pd[:, :])
                        P.vector(f)
                        ev_done.append((cV, cV.n))
                    else:
                        def f(A, mt=mt, mmd=mmd):
                            cT.wait(A, mmd)
                            pd = pool[4] if mt % 2 == 0 else pool[5]
                            A.activation(out=SR8[:, mt * CH:(mt + 1) * CH], in_=pd[:, :],
                                         func=AF.Sigmoid)
                        P.scalar(f)
                        ev_done.append((cA, cA.n))
                piece_done(pidx)
            kvr_done_v = cV.n
            kvr_done_a = cA.n
            slot_o, wn_o, pidx_o = load_piece(wo_d.ap())

            for g in range(NG):
                def f(A, g=g, kvr_done_v=kvr_done_v):
                    if g == 0:
                        cV.wait(A, kvr_done_v)
                    else:
                        cV.wait(A)
                    A.activation(out=scr["ek"][:, :], in_=K16[:, g * CH:(g + 1) * CH],
                                 func=AF.Exp)
                    A.activation(out=scr["ekB"][:, :], in_=K16[:, g * CH:(g + 1) * CH],
                                 func=AF.Exp, bias=vecs[:, 80 + g:81 + g])
                P.scalar(f)
                ek_done = cA.n

                def f(V, g=g, ek_done=ek_done):
                    cA.wait(V, ek_done)
                    V.tensor_single_scalar(decays[:, :], ones512[:, :],
                                           edec[:, g:g + 1], ALU.mult)
                    V.tensor_mul(scr["ekv"][:, :], scr["ek"][:, :],
                                 V16[:, g * CH:(g + 1) * CH])
                    V.eng.wait_ge(cV.sem, cV.n)
                    V.tensor_tensor_scan(scr["Pp"][:, :], decays[:, :],
                                         scr["ekv"][:, :], 0.0, ALU.mult, ALU.add)
                    V.tensor_tensor_scan(scr["Qq"][:, :], decays[:, :],
                                         scr["ek"][:, :], 0.0, ALU.mult, ALU.add)
                    V.tensor_tensor_scan(scr["cum"][:, :], decays[:, :],
                                         zeros512[:, :], 1.0, ALU.mult, ALU.add)
                    V.scalar_tensor_tensor(scr["Pp"][:, :], scr["cum"][:, :],
                                           Pc[:, g:g + 1], scr["Pp"][:, :],
                                           ALU.mult, ALU.add)
                    V.scalar_tensor_tensor(scr["Qq"][:, :], scr["cum"][:, :],
                                           Qc[:, g:g + 1], scr["Qq"][:, :],
                                           ALU.mult, ALU.add)
                    V.tensor_copy(scr["Psh"][:, 1:CH], scr["Pp"][:, 0:CH - 1])
                    V.tensor_copy(scr["Psh"][:, 0:1], Pc[:, g:g + 1])
                    V.tensor_copy(scr["Qsh"][:, 1:CH], scr["Qq"][:, 0:CH - 1])
                    V.tensor_copy(scr["Qsh"][:, 0:1], Qc[:, g:g + 1])
                    V.tensor_copy(Pc[:, g:g + 1], scr["Pp"][:, CH - 1:CH])
                    V.tensor_copy(Qc[:, g:g + 1], scr["Qq"][:, CH - 1:CH])
                    V.tensor_mul(scr["numv"][:, :], scr["ekB"][:, :],
                                 V16[:, g * CH:(g + 1) * CH])
                    V.tensor_add(scr["numv"][:, :], scr["numv"][:, :], scr["Psh"][:, :])
                    V.tensor_add(scr["Qsh"][:, :], scr["ekB"][:, :], scr["Qsh"][:, :])
                    V.reciprocal(scr["Qsh"][:, :], scr["Qsh"][:, :])
                    V.tensor_mul(scr["numv"][:, :], scr["numv"][:, :], scr["Qsh"][:, :])
                    V.tensor_mul(G8[:, g * CH:(g + 1) * CH], scr["numv"][:, :],
                                 SR8[:, g * CH:(g + 1) * CH])
                P.vector(f)
            wkv_done = cV.n

            if debug and tc == 0:
                def f(sync, wkv_done=wkv_done):
                    sync.wait_ge(cV.sem, wkv_done)
                    sS.inc(sync.dma_start(k_dbg.ap(), K16[:, :]), 16)
                    sS.inc(sync.dma_start(y_dbg.ap(), G8[:, :]), 16)
                P.sync(f)

            ev_done = []
            for mt in range(NG):
                def f(Te, mt=mt, slot_o=slot_o, wn_o=wn_o, wkv_done=wkv_done):
                    Te.eng.wait_ge(sWr[slot_o].sem, wn_o)
                    if mt == 0:
                        cV.wait(Te, wkv_done)
                    if mt >= 2:
                        cA.wait(Te, ev_done[mt - 2])
                    pd = pool[4] if mt % 2 == 0 else pool[5]
                    for r in range(NG):
                        Te.matmul(pd[:, :],
                                  wring[slot_o][:, r * 1024 + mt * 128: r * 1024 + (mt + 1) * 128],
                                  G8[:, r * CH:(r + 1) * CH],
                                  start=(r == 0), stop=(r == NG - 1))
                P.tensor(f)
                mmd = cT.n

                def f(A, mt=mt, mmd=mmd):
                    cT.wait(A, mmd)
                    pd = pool[4] if mt % 2 == 0 else pool[5]
                    A.activation(out=XH[:, mt * CH:(mt + 1) * CH], in_=pd[:, :],
                                 func=AF.Identity, scale=0.5)
                P.scalar(f)
                ev_done.append(cA.n)
            piece_done(pidx_o)
            xh_done = cA.n

            xh_dump_sS = 0
            if debug and tc == 0:
                def f(sync, xh_done=xh_done):
                    sync.wait_ge(cA.sem, xh_done)
                    sS.inc(sync.dma_start(xh_dbg.ap(), XH[:, :]), 16)
                P.sync(f)
                xh_dump_sS = sS.n

            xhv = XH[:, :].rearrange("p (g t) -> p g t", g=NG)

            def f(V, xh_done=xh_done, xh_dump_sS=xh_dump_sS):
                cA.wait(V, xh_done)
                if xh_dump_sS:
                    V.eng.wait_ge(sS.sem, xh_dump_sS)
                for t in range(CH):
                    V.scalar_tensor_tensor(xhv[:, :, t], v1s[:, :], 0.5, xhv[:, :, t],
                                           ALU.mult, ALU.add)
                    V.scalar_tensor_tensor(v1s[:, :], xhv[:, :, t], 1.0, xhv[:, :, t],
                                           ALU.is_lt, ALU.mult)
            P.vector(f)

            def f(V):
                for g in range(NG):
                    V.tensor_single_scalar(s1f[:, :], XH[:, g * CH:(g + 1) * CH],
                                           1.0, ALU.is_ge)
                    V.tensor_add(Rb[:, g * CH:(g + 1) * CH],
                                 Rb[:, g * CH:(g + 1) * CH], s1f[:, :])
            P.vector(f)
            xmid_done = cV.n

            bc = emit_lnstats(Rb)

            def f(V, bc=bc):
                cT.wait(V, bc)
                for g in range(NG):
                    V.tensor_tensor(sqscr[:, :], Rb[:, g * CH:(g + 1) * CH], psM,
                                    ALU.subtract)
                    V.tensor_tensor(sqscr[:, :], sqscr[:, :], psRS, ALU.mult)
                    V.tensor_scalar(Lb[:, g * CH:(g + 1) * CH], sqscr[:, :],
                                    vecs[:, 16 + g:17 + g], vecs[:, 24 + g:25 + g],
                                    ALU.mult, ALU.add)
            P.vector(f)

            def f(V):
                for g in range(NG):
                    L = Lb[:, g * CH:(g + 1) * CH]
                    dd = sqscr[:, :]
                    V.tensor_sub(dd[:, 1:CH], L[:, 1:CH], L[:, 0:CH - 1])
                    V.tensor_sub(dd[:, 0:1], L[:, 0:1], tail2[:, g:g + 1])
                    for mi, col in ((0, 56), (1, 64)):
                        V.scalar_tensor_tensor(MIXB[mi][:, g * CH:(g + 1) * CH],
                                               dd, vecs[:, col + g:col + g + 1], L,
                                               ALU.mult, ALU.add)
                    V.tensor_copy(tail2[:, g:g + 1], L[:, CH - 1:CH])
            P.vector(f)
            mix2_done = cV.n

            ev_done = []
            for pc in range(4):
                slot, wn, pidx = load_piece(fwk_d.ap()[:, pc * 8192:(pc + 1) * 8192])
                for mo in range(8):
                    mt = pc * 8 + mo

                    def f(Te, slot=slot, wn=wn, mo=mo, mt=mt, mix2_done=mix2_done):
                        Te.eng.wait_ge(sWr[slot].sem, wn)
                        if mt == 0:
                            cV.wait(Te, mix2_done)
                            cA.wait(Te)
                        if mt >= 2:
                            cA.wait(Te, ev_done[mt - 2])
                        pd = pool[4] if mt % 2 == 0 else pool[5]
                        for r in range(NG):
                            Te.matmul(pd[:, :],
                                      wring[slot][:, mo * 1024 + r * 128: mo * 1024 + (r + 1) * 128],
                                      MIXB[0][:, r * CH:(r + 1) * CH],
                                      start=(r == 0), stop=(r == NG - 1))
                    P.tensor(f)
                    mmd = cT.n

                    def f(A, mt=mt, mmd=mmd):
                        cT.wait(A, mmd)
                        pd = pool[4] if mt % 2 == 0 else pool[5]
                        A.activation(out=sqscr[:, :], in_=pd[:, :], func=AF.Relu)
                        A.activation(out=KK[:, mt * CH:(mt + 1) * CH], in_=sqscr[:, :],
                                     func=AF.Square)
                    P.scalar(f)
                    ev_done.append(cA.n)
                piece_done(pidx)
            kk_done = cA.n

            slot_r, wn_r, pidx_r = load_piece(fwr_d.ap())
            ev_done = []
            for mt in range(NG):
                def f(Te, slot_r=slot_r, wn_r=wn_r, mt=mt, kk_done=kk_done):
                    Te.eng.wait_ge(sWr[slot_r].sem, wn_r)
                    if mt == 0:
                        cA.wait(Te, kk_done)
                    if mt >= 2:
                        cA.wait(Te, ev_done[mt - 2])
                    pd = pool[4] if mt % 2 == 0 else pool[5]
                    for r in range(NG):
                        Te.matmul(pd[:, :],
                                  wring[slot_r][:, r * 1024 + mt * 128: r * 1024 + (mt + 1) * 128],
                                  MIXB[1][:, r * CH:(r + 1) * CH],
                                  start=(r == 0), stop=(r == NG - 1))
                P.tensor(f)
                mmd = cT.n

                def f(A, mt=mt, mmd=mmd):
                    cT.wait(A, mmd)
                    pd = pool[4] if mt % 2 == 0 else pool[5]
                    A.activation(out=SR8[:, mt * CH:(mt + 1) * CH], in_=pd[:, :],
                                 func=AF.Sigmoid)
                P.scalar(f)
                ev_done.append(cA.n)
            piece_done(pidx_r)
            csig_done = cA.n

            ev_done = []
            for pc in range(4):
                slot, wn, pidx = load_piece(fwv_d.ap()[:, pc * 8192:(pc + 1) * 8192])
                for mo in range(2):
                    mt = pc * 2 + mo

                    def f(Te, slot=slot, wn=wn, mo=mo, mt=mt, csig_done=csig_done):
                        Te.eng.wait_ge(sWr[slot].sem, wn)
                        if mt == 0:
                            cA.wait(Te, csig_done)
                        if mt >= 2:
                            cV.wait(Te, ev_done[mt - 2])
                        pd = pool[4] if mt % 2 == 0 else pool[5]
                        for r in range(32):
                            Te.matmul(pd[:, :],
                                      wring[slot][:, mo * 4096 + r * 128: mo * 4096 + (r + 1) * 128],
                                      KK[:, r * CH:(r + 1) * CH],
                                      start=(r == 0), stop=(r == 31))
                    P.tensor(f)
                    mmd = cT.n

                    def f(V, mt=mt, mmd=mmd, xh_dump_sS=xh_dump_sS):
                        if mt == 0 and xh_dump_sS:
                            V.eng.wait_ge(sS.sem, xh_dump_sS)
                        cT.wait(V, mmd)
                        pd = pool[4] if mt % 2 == 0 else pool[5]
                        V.scalar_tensor_tensor(XH[:, mt * CH:(mt + 1) * CH], pd[:, :],
                                               0.5, SR8[:, mt * CH:(mt + 1) * CH],
                                               ALU.mult, ALU.mult)
                    P.vector(f)
                    ev_done.append(cV.n)
                piece_done(pidx)
            xh2_done = cV.n

            def f(V, xh2_done=xh2_done):
                cV.wait(V, xh2_done)
                for t in range(CH):
                    V.scalar_tensor_tensor(xhv[:, :, t], v2s[:, :], 0.5, xhv[:, :, t],
                                           ALU.mult, ALU.add)
                    V.scalar_tensor_tensor(v2s[:, :], xhv[:, :, t], 1.0, xhv[:, :, t],
                                           ALU.is_lt, ALU.mult)
                for g in range(NG):
                    V.tensor_single_scalar(s1f[:, :], XH[:, g * CH:(g + 1) * CH],
                                           1.0, ALU.is_ge)
                    V.tensor_add(Rb[:, g * CH:(g + 1) * CH],
                                 Rb[:, g * CH:(g + 1) * CH], s1f[:, :])
            P.vector(f)

            lif2_done = cV.n

            def f(sync, lif2_done=lif2_done, tc=tc, lsem=lsem):
                sync.wait_ge(cV.sem, lif2_done)
                for g in range(NG):
                    lsem.inc(sync.dma_start(
                        XH[:, g * CH:(g + 1) * CH],
                        x2T_d.ap()[g * 128:(g + 1) * 128, tc * CH:(tc + 1) * CH]), 16)
            P.sync(f)
            x2re_ld = lsem.n

            def f(V, x2re_ld=x2re_ld, lsem=lsem):
                V.eng.wait_ge(lsem.sem, x2re_ld)
                for g in range(NG):
                    V.tensor_add(Rb[:, g * CH:(g + 1) * CH],
                                 Rb[:, g * CH:(g + 1) * CH], XH[:, g * CH:(g + 1) * CH])
            P.vector(f)
            x3_done = cV.n

            if debug and tc == 0:
                def f(sync, x3_done=x3_done):
                    sync.wait_ge(cV.sem, x3_done)
                    sS.inc(sync.dma_start(x3_dbg.ap(), Rb[:, :]), 16)
                P.sync(f)
                dumpstate["x3"] = sS.n

            bc = emit_lnstats(Rb)

            def f(V, bc=bc):
                cT.wait(V, bc)
                for g in range(NG):
                    V.tensor_tensor(sqscr[:, :], Rb[:, g * CH:(g + 1) * CH], psM,
                                    ALU.subtract)
                    V.tensor_tensor(Lb[:, g * CH:(g + 1) * CH], sqscr[:, :], psRS,
                                    ALU.mult)
            P.vector(f)
            u_done = cV.n

            bc = emit_lnstats(Lb, sq_wait_extra=u_done)

            def f(V, bc=bc):
                cT.wait(V, bc)
                for g in range(NG):
                    V.tensor_tensor(sqscr[:, :], Lb[:, g * CH:(g + 1) * CH], psM,
                                    ALU.subtract)
                    V.tensor_tensor(sqscr[:, :], sqscr[:, :], psRS, ALU.mult)
                    V.tensor_scalar(sqscr[:, :], sqscr[:, :],
                                    vecs[:, 88 + g:89 + g], vecs[:, 96 + g:97 + g],
                                    ALU.mult, ALU.add)
                    V.tensor_add(K16[:, g * CH:(g + 1) * CH],
                                 Rb[:, g * CH:(g + 1) * CH], sqscr[:, :])
            P.vector(f)
            out_done = cV.n

            tpcp_trail = []
            for tt in range(CH // 128):
                ob = outnat[tt % 2]
                osem = sSo[tt % 2]
                for g in range(NG):
                    pd = pool[6] if g % 2 == 0 else pool[7]

                    def f(Te, tt=tt, g=g, pd=pd, out_done=out_done):
                        if tt == 0 and g == 0:
                            cV.wait(Te, out_done)
                        if len(tpcp_trail) >= 2:
                            Te.eng.wait_ge(cV.sem, tpcp_trail[-2])
                        Te.transpose(pd[:, 0:128],
                                     K16[:, g * CH + tt * 128: g * CH + (tt + 1) * 128],
                                     eyeF[:, :])
                    P.tensor(f)
                    tpd = cT.n

                    def f(V, g=g, pd=pd, tpd=tpd, ob=ob, osem=osem):
                        cT.wait(V, tpd)
                        if g == 0:
                            V.eng.wait_ge(osem.sem, osem.n)
                        V.tensor_copy(ob[:, g * 128:(g + 1) * 128], pd[:, 0:128])
                        tpcp_trail.append(cV.n)
                    P.vector(f)
                tcp_done = cV.n

                def f(sync, tt=tt, tc=tc, ob=ob, osem=osem, tcp_done=tcp_done):
                    sync.wait_ge(cV.sem, tcp_done)
                    osem.inc(sync.dma_start(
                        out_d.ap()[tc * CH + tt * 128: tc * CH + (tt + 1) * 128, :],
                        ob[:, :]), 16)
                P.sync(f)
            trans_done[tc] = cT.n

        def f(sync):
            for j in range(2):
                sync.wait_ge(sSo[j].sem, sSo[j].n)
            sync.wait_ge(sS.sem, sS.n)
        P.sync(f)
        pctx.close()




def build_C(T, debug=False):
    nc = bass.Bass("TRN2", target_bir_lowering=False)
    NCH = T // CH
    NW = 2

    x2T_d = nc.dram_tensor("x2T", [D, T], F32, kind="ExternalInput")
    wk_d = nc.dram_tensor("wk_img", [128, 8192], F32, kind="ExternalInput")
    wv_d = nc.dram_tensor("wv_img", [128, 8192], F32, kind="ExternalInput")
    wr_d = nc.dram_tensor("wr_img", [128, 8192], F32, kind="ExternalInput")
    wo_d = nc.dram_tensor("wo_img", [128, 8192], F32, kind="ExternalInput")
    fwr_d = nc.dram_tensor("fwr_img", [128, 8192], F32, kind="ExternalInput")
    fwk_d = nc.dram_tensor("fwk_img", [128, 32768], F32, kind="ExternalInput")
    fwv_d = nc.dram_tensor("fwv_img", [128, 32768], F32, kind="ExternalInput")
    vecs_d = nc.dram_tensor("vimgs", [13, 128, 8], F32, kind="ExternalInput")
    out_d = nc.dram_tensor("outn", [T, D], F32, kind="ExternalOutput")
    if debug:
        k_dbg = nc.dram_tensor("k_dbg", [128, NG * CH], F32, kind="ExternalOutput")
        y_dbg = nc.dram_tensor("y_dbg", [128, NG * CH], BF16, kind="ExternalOutput")
        xh_dbg = nc.dram_tensor("xh_dbg", [128, NG * CH], F32, kind="ExternalOutput")
        x3_dbg = nc.dram_tensor("x3_dbg", [128, NG * CH], F32, kind="ExternalOutput")

    d = {"x2T": x2T_d, "wk_img": wk_d, "wv_img": wv_d, "wr_img": wr_d, "wo_img": wo_d, "fwr_img": fwr_d, "fwk_img": fwk_d, "fwv_img": fwv_d, "vimgs": vecs_d, "outn": out_d}
    if debug:
        d.update({"k_dbg": k_dbg, "y_dbg": y_dbg, "xh_dbg": xh_dbg, "x3_dbg": x3_dbg})
    with ExitStack() as semctx:
        emit_C(nc, semctx, d, T, debug=debug)
    return nc


def _consts_A():
    eyeF = np.eye(128, dtype=np.float32)
    eyeB = np.eye(128).astype(BF)
    expand16 = np.kron(np.eye(16, dtype=np.float32), np.ones((1, 64), np.float32))
    trils = np.zeros((128, 4 * CH), np.float32)
    for j in range(4):
        p = np.arange(128)[:, None]
        q = np.arange(CH)[None, :]
        trils[:, j * CH:(j + 1) * CH] = (j * 128 + p <= q)
    return eyeF, eyeB, expand16, trils.astype(BF)


def _in_map_A(ins, x_full, b, T, consts):
    eyeF, eyeB, expand16, trils = consts
    wpdup = np.concatenate(
        [np.asarray(ins["L0_attn_W_pred"]).transpose(1, 0, 2).reshape(64, 1024)] * 2,
        axis=0).astype(np.float32)
    return {
        "x0": np.ascontiguousarray(x_full[b, :T], dtype=np.float32),
        "w_k": np.asarray(ins["L0_attn_W_K"], np.float32),
        "w_q": np.asarray(ins["L0_attn_W_Q"], np.float32),
        "w_v": np.asarray(ins["L0_attn_W_V"], np.float32),
        "wpdup": wpdup, "eyeF": eyeF, "eyeB": eyeB, "expand16": expand16,
        "trils": trils,
    }


def run_A(x_full, ins, T, n_cores=NCORES, debug=False):
    consts = _consts_A()
    nc = build_A(T, debug=debug)
    in_maps = [_in_map_A(ins, x_full, b, T, consts) for b in range(n_cores)]
    res = run_bass_kernel_spmd(nc, in_maps, core_ids=list(range(n_cores)))
    if debug:
        return res.results
    return np.stack([r["x1T"] for r in res.results])


def _prep_B(ins):
    eg = np.asarray(ins["L0_ffn_eg_w"], np.float32)
    eu = np.asarray(ins["L0_ffn_eu_w"], np.float32)
    ed = np.asarray(ins["L0_ffn_ed_w"], np.float32)
    E = eg.shape[0]
    moe = np.empty((E, 128, 12288), np.float32)
    for e in range(E):
        moe[e, :, 0:4096] = _img(eg[e].T)
        moe[e, :, 4096:8192] = _img(eu[e].T)
        moe[e, :, 8192:12288] = _img(ed[e].T)
    sg = _img(np.asarray(ins["L0_ffn_sg_w"], np.float32).T)
    su = _img(np.asarray(ins["L0_ffn_su_w"], np.float32).T)
    sd = _img(np.asarray(ins["L0_ffn_sd_w"], np.float32).T)
    gate = _img(np.asarray(ins["L0_ffn_gate_w"], np.float32).T)
    gbias = np.ascontiguousarray(
        np.asarray(ins["L0_ffn_gate_bias"], np.float32).reshape(64, 1))
    sel = np.kron(np.eye(64), np.ones((1, 128))).astype(BF)
    return {"moe_img": moe, "sg_img": sg, "su_img": su, "sd_img": sd,
            "gate_img": gate, "gate_bias": gbias, "sel_img": sel,
            "eyeF": np.eye(128, dtype=np.float32)}


def run_B(x1T_all, ins, T, n_cores=NCORES, debug=False):
    base = _prep_B(ins)
    nc = build_B(T, debug=debug)
    in_maps = []
    for b in range(n_cores):
        m = dict(base)
        m["x1T"] = np.ascontiguousarray(x1T_all[b], dtype=np.float32)
        in_maps.append(m)
    res = run_bass_kernel_spmd(nc, in_maps, core_ids=list(range(n_cores)))
    if debug:
        return res.results
    return np.stack([r["x2T"] for r in res.results])


def _prep_C2(ins):
    g = lambda k: np.asarray(ins[k], np.float32)
    names = ["L1_attn_ln1_s", "L1_attn_ln1_b", "L1_attn_ln2_s", "L1_attn_ln2_b",
             "L1_attn_tmk", "L1_attn_tmv", "L1_attn_tmr", "L1_attn_f_tmk",
             "L1_attn_f_tmr", "L1_attn_w_decay", "L1_attn_u_first",
             "L1_ffn_ln_g", "L1_ffn_ln_b"]
    vimgs = np.stack([_vec_img(g(k)) for k in names])
    return {
        "wk_img": _img(g("L1_attn_Wk")), "wv_img": _img(g("L1_attn_Wv")),
        "wr_img": _img(g("L1_attn_Wr")), "wo_img": _img(g("L1_attn_Wo")),
        "fwr_img": _img(g("L1_attn_fWr")),
        "fwk_img": _img_mmajor(g("L1_attn_fWk")),
        "fwv_img": _img_mmajor(g("L1_attn_fWv")),
        "vimgs": vimgs,
        "eyeFc_in": np.eye(128, dtype=np.float32),
    }


def run_C(x2T_all, ins, T, n_cores=NCORES, debug=False):
    base = _prep_C2(ins)
    nc = build_C(T, debug=debug)
    in_maps = []
    for b in range(n_cores):
        m = dict(base)
        m["x2T"] = np.ascontiguousarray(x2T_all[b], dtype=np.float32)
        in_maps.append(m)
    res = run_bass_kernel_spmd(nc, in_maps, core_ids=list(range(n_cores)))
    if debug:
        return res.results
    return np.stack([r["outn"] for r in res.results])


def build_ALL(T):
    nc = bass.Bass("TRN2", target_bir_lowering=False)
    d = {}
    d["x0"] = nc.dram_tensor("x0", [T, D], F32, kind="ExternalInput")
    d["w_k"] = nc.dram_tensor("w_k", [16, 16], F32, kind="ExternalInput")
    d["w_q"] = nc.dram_tensor("w_q", [16, 16], F32, kind="ExternalInput")
    d["w_v"] = nc.dram_tensor("w_v", [16, 16], F32, kind="ExternalInput")
    d["wpdup"] = nc.dram_tensor("wpdup", [128, 1024], F32, kind="ExternalInput")
    d["eyeF"] = nc.dram_tensor("eyeF", [128, 128], F32, kind="ExternalInput")
    d["eyeB"] = nc.dram_tensor("eyeB", [128, 128], BF16, kind="ExternalInput")
    d["expand16"] = nc.dram_tensor("expand16", [16, 1024], F32, kind="ExternalInput")
    d["trils"] = nc.dram_tensor("trils", [128, 4 * CH], BF16, kind="ExternalInput")
    d["x1T"] = nc.dram_tensor("x1T", [D, T], F32, kind="Internal")
    d["zT_scratch"] = nc.dram_tensor("zT_scratch", [D, T], BF16, kind="Internal")
    d["moe_img"] = nc.dram_tensor("moe_img", [64, 128, 12288], F32, kind="ExternalInput")
    d["sg_img"] = nc.dram_tensor("sg_img", [128, 8192], F32, kind="ExternalInput")
    d["su_img"] = nc.dram_tensor("su_img", [128, 8192], F32, kind="ExternalInput")
    d["sd_img"] = nc.dram_tensor("sd_img", [128, 8192], F32, kind="ExternalInput")
    d["gate_img"] = nc.dram_tensor("gate_img", [128, 512], F32, kind="ExternalInput")
    d["gate_bias"] = nc.dram_tensor("gate_bias", [64, 1], F32, kind="ExternalInput")
    d["sel_img"] = nc.dram_tensor("sel_img", [64, 8192], BF16, kind="ExternalInput")
    d["x2T"] = nc.dram_tensor("x2T", [D, T], F32, kind="Internal")
    d["wk_img"] = nc.dram_tensor("wk_img", [128, 8192], F32, kind="ExternalInput")
    d["wv_img"] = nc.dram_tensor("wv_img", [128, 8192], F32, kind="ExternalInput")
    d["wr_img"] = nc.dram_tensor("wr_img", [128, 8192], F32, kind="ExternalInput")
    d["wo_img"] = nc.dram_tensor("wo_img", [128, 8192], F32, kind="ExternalInput")
    d["fwr_img"] = nc.dram_tensor("fwr_img", [128, 8192], F32, kind="ExternalInput")
    d["fwk_img"] = nc.dram_tensor("fwk_img", [128, 32768], F32, kind="ExternalInput")
    d["fwv_img"] = nc.dram_tensor("fwv_img", [128, 32768], F32, kind="ExternalInput")
    d["vimgs"] = nc.dram_tensor("vimgs", [13, 128, 8], F32, kind="ExternalInput")
    d["outn"] = nc.dram_tensor("outn", [T, D], F32, kind="ExternalOutput")
    with ExitStack() as semctx:
        emit_A(nc, semctx, d, T)
        emit_B(nc, semctx, d, T)
        emit_C(nc, semctx, d, T)
    return nc


def kernel(**ins):
    x0 = np.asarray(ins["L0_attn_x"], dtype=np.float32)
    T = x0.shape[1]
    nc = build_ALL(T)
    consts = _consts_A()
    baseB = _prep_B(ins)
    baseC = _prep_C2(ins)
    del baseC["eyeFc_in"]
    in_maps = []
    for b in range(NCORES):
        m = dict(baseB)
        m.update(baseC)
        m.update(_in_map_A(ins, x0, b, T, consts))
        m["eyeFc_in"] = consts[0]
        in_maps.append(m)
    res = run_bass_kernel_spmd(nc, in_maps, core_ids=list(range(NCORES)))
    return np.stack([r["outn"] for r in res.results])
